# version 10
# baseline (speedup 1.0000x reference)
import numpy as np
import ml_dtypes
from contextlib import ExitStack
import concourse.bass as bass
import concourse.mybir as mybir
from concourse.bass_utils import run_bass_kernel_spmd

F32 = mybir.dt.float32
BF16 = mybir.dt.bfloat16
AF = mybir.ActivationFunctionType
ALU = mybir.AluOpType
AX = mybir.AxisListType
BF = ml_dtypes.bfloat16

NCORES = 8
T = 1024
TH = T + 3
NCH = 8
CHL = 128
DM = 4096
KC = 32
INTER = 8192
NFI = 64
NXBC = 81
NH = 128
HD = 64
SS = 128
NG = 8
GROUP = 1024
EPS = 1e-5
LNEPS = 1e-5
RANKS = (8, 16, 32, 64)
TS = (347, 340, 340)
TS_OFF = (0, 347, 687)

PHASES = 99
DEBUG = False


def build_program():
    nc = bass.Bass("TRN2", target_bir_lowering=False)
    scr = dict(kind="ExternalOutput") if DEBUG else {}

    d_xh = nc.dram_tensor("xh", [KC, 128, TH], F32, kind="ExternalInput").ap()
    d_wxbc = nc.dram_tensor("wxbc", [NXBC, 128, KC, 128], F32, kind="ExternalInput").ap()
    d_wgate = nc.dram_tensor("wgate", [NFI, 128, KC, 128], F32, kind="ExternalInput").ap()
    d_wout = nc.dram_tensor("wout", [8, NFI, 4, 128, 128], F32, kind="ExternalInput").ap()
    d_wffn = nc.dram_tensor("wffn", [8, KC, 4, 128, 128], F32, kind="ExternalInput").ap()
    d_wag = nc.dram_tensor("wag", [KC, 128, 68], F32, kind="ExternalInput").ap()
    d_wbl = nc.dram_tensor("wbl", [KC, 64, 128], F32, kind="ExternalInput").ap()
    d_identbf = nc.dram_tensor("dr_identbf", [128, 128], BF16, kind="ExternalInput").ap()
    d_identf = nc.dram_tensor("dr_identf", [128, 128], F32, kind="ExternalInput").ap()
    d_maskm = nc.dram_tensor("dr_maskm", [128, 128], F32, kind="ExternalInput").ap()
    d_ltm = nc.dram_tensor("dr_ltm", [4, 64], BF16, kind="ExternalInput").ap()
    d_onesb = nc.dram_tensor("dr_onesb", [128, 1], BF16, kind="ExternalInput").ap()
    d_acol = nc.dram_tensor("dr_acol", [128, 1], F32, kind="ExternalInput").ap()
    d_dtb = nc.dram_tensor("dtb", [128, 1], F32, kind="ExternalInput").ap()
    d_drep = nc.dram_tensor("dr_drep", [128, NFI], F32, kind="ExternalInput").ap()
    d_nw = nc.dram_tensor("nw", [128, NFI], F32, kind="ExternalInput").ap()
    d_bcol = nc.dram_tensor("bcol", [128, KC], F32, kind="ExternalInput").ap()
    d_gb = nc.dram_tensor("gb", [4, 1], F32, kind="ExternalInput").ap()
    d_cw = nc.dram_tensor("dr_cw", [128, 320], F32, kind="ExternalInput").ap()
    d_cb = nc.dram_tensor("dr_cb", [128, 80], F32, kind="ExternalInput").ap()
    d_ringm = nc.dram_tensor("dr_ringm", [128, 4], F32, kind="ExternalInput").ap()
    d_ringm1 = nc.dram_tensor("dr_ringm1", [128, 4], F32, kind="ExternalInput").ap()
    d_ringmc = nc.dram_tensor("dr_ringmc", [4, 1], F32, kind="ExternalInput").ap()
    d_ringmc1 = nc.dram_tensor("dr_ringmc1", [4, 1], F32, kind="ExternalInput").ap()
    d_epsc = nc.dram_tensor("dr_epsc", [128, 1], F32, kind="ExternalInput").ap()
    d_onesrow = nc.dram_tensor("dr_onesrow", [1, 128], F32, kind="ExternalInput").ap()

    d_out = nc.dram_tensor("out", [KC, 128, T], F32, kind="ExternalOutput").ap()

    d_xtbf = nc.dram_tensor("xtbf_s", [KC, 128, T], BF16, **scr).ap()
    d_xbcs = nc.dram_tensor("xbc_s", [80, 128, T], BF16, **scr).ap()
    d_dts = nc.dram_tensor("dt_s", [128, T], F32, **scr).ap()
    d_csf = nc.dram_tensor("csflat_s", [NCH, 16384], F32, **scr).ap()
    d_ecs = nc.dram_tensor("expcs_s", [NCH, 16384], F32, **scr).ap()
    d_cdr = nc.dram_tensor("cdrow_s", [NCH, 128], F32, **scr).ap()
    d_yds = nc.dram_tensor("yd_s", [NCH, 128, 8192], BF16, **scr).ap()
    d_sts = nc.dram_tensor("st_s", [NCH, 128, 8192], BF16, **scr).ap()
    d_gs = nc.dram_tensor("g_s", [NFI, 128, T], BF16, **scr).ap()
    d_ngs = nc.dram_tensor("ng_s", [NFI, 128, T], BF16, **scr).ap()
    d_rows = nc.dram_tensor("rows_s", [8, 1024], F32, **scr).ap()
    d_gnr = nc.dram_tensor("gnrow_s", [4, NG, 256], F32, **scr).ap()
    d_ccsi = nc.dram_tensor("ccs_in", [128, 8193], F32).ap()
    d_ccso = nc.dram_tensor("ccs_out", [4 * 128, 8193], F32).ap()
    d_dbg_sloc = nc.dram_tensor("dbg_sloc", [128, 8192], F32, **scr).ap()
    d_dbg_state = nc.dram_tensor("dbg_state", [128, 8192], F32, **scr).ap()
    d_ys = nc.dram_tensor("y_s", [NCH, 128, 8192], BF16, **scr).ap()
    d_r1 = nc.dram_tensor("r1_s", [KC, 128, T], BF16, **scr).ap()
    d_x2s = nc.dram_tensor("x2_s", [KC, 128, T], BF16, **scr).ap()
    d_df1 = nc.dram_tensor("dbgf_scbt", [128, 32], F32, **scr).ap()
    d_df2 = nc.dram_tensor("dbgf_fbf", [128, 32], BF16, **scr).ap()
    d_df3 = nc.dram_tensor("dbgf_msb", [64, 1024], BF16, **scr).ap()
    d_df4 = nc.dram_tensor("dbgf_umb", [64, 1024], BF16, **scr).ap()
    d_df5 = nc.dram_tensor("dbgf_sc4", [4, 1024], F32, **scr).ap()

    st = ExitStack()
    with st:
        ent = st.enter_context
        identbf = ent(nc.sbuf_tensor("identbf", [128, 128], BF16))
        identf = ent(nc.sbuf_tensor("identf", [128, 128], F32))
        maskm = ent(nc.sbuf_tensor("maskm", [128, 128], F32))
        ltm = ent(nc.sbuf_tensor("ltm", [4, 64], BF16))
        onesb = ent(nc.sbuf_tensor("onesb", [128, 1], BF16))
        acol = ent(nc.sbuf_tensor("acol", [128, 1], F32))
        dtbc = ent(nc.sbuf_tensor("dtbc", [128, 1], F32))
        drep = ent(nc.sbuf_tensor("drep", [128, NFI], F32))
        nwc = ent(nc.sbuf_tensor("nwc", [128, NFI], F32))
        bcolc = ent(nc.sbuf_tensor("bcolc", [128, KC], F32))
        gbc = ent(nc.sbuf_tensor("gbc", [4, 1], F32))
        cwc = ent(nc.sbuf_tensor("cwc", [128, 320], F32))
        cbc = ent(nc.sbuf_tensor("cbc", [128, 80], F32))
        ringm = ent(nc.sbuf_tensor("ringm", [128, 4], F32))
        ringm1 = ent(nc.sbuf_tensor("ringm1", [128, 4], F32))
        ringmc = ent(nc.sbuf_tensor("ringmc", [4, 1], F32))
        ringmc1 = ent(nc.sbuf_tensor("ringmc1", [4, 1], F32))
        epsc = ent(nc.sbuf_tensor("epsc", [128, 1], F32))
        onesrow = ent(nc.sbuf_tensor("onesrow", [1, 128], F32))
        zcol = ent(nc.sbuf_tensor("zcol", [128, 128], F32))
        dtq = ent(nc.sbuf_tensor("dtq", [128, T], F32))
        datq = ent(nc.sbuf_tensor("datq", [128, T], F32))
        logd = ent(nc.sbuf_tensor("logd", [128, 1], F32))
        csb = ent(nc.sbuf_tensor("csb", [128, 256], F32))
        ecsb = ent(nc.sbuf_tensor("ecsb", [128, 256], F32))
        dcy = ent(nc.sbuf_tensor("dcy", [128, 256], F32))
        csT = ent(nc.sbuf_tensor("csT", [128, 256], F32))
        dcyT = ent(nc.sbuf_tensor("dcyT", [128, 256], F32))
        dtT = ent(nc.sbuf_tensor("dtT", [128, 256], F32))
        cdrp = ent(nc.sbuf_tensor("cdrp", [128, 256], F32))
        btb = ent(nc.sbuf_tensor("btb", [128, 1024], BF16))
        erow = ent(nc.sbuf_tensor("erow", [4, 128], F32))
        expdr = ent(nc.sbuf_tensor("expdr", [4, 128], F32))
        scbt = ent(nc.sbuf_tensor("scbt", [128, 32], F32))
        mxb = ent(nc.sbuf_tensor("mxb", [128, 8], F32))
        ebx = ent(nc.sbuf_tensor("ebx", [128, 32], F32))
        cub = ent(nc.sbuf_tensor("cub", [128, 32], F32))
        fbf = ent(nc.sbuf_tensor("fbf", [128, 32], BF16))
        fsb = ent(nc.sbuf_tensor("fsb", [4, 1024], BF16))
        sc4 = ent(nc.sbuf_tensor("sc4", [4, 1024], F32))
        msb = ent(nc.sbuf_tensor("msb", [64, 1024], BF16))
        umb = ent(nc.sbuf_tensor("umb", [64, 1024], BF16))
        agstg = ent(nc.sbuf_tensor("agstg", [128, 68], F32))
        agb = ent(nc.sbuf_tensor("agb", [128, KC * 68], BF16))
        blb = [ent(nc.sbuf_tensor(f"blb{i}", [64, 512], BF16)) for i in range(2)]
        blstg2 = [ent(nc.sbuf_tensor(f"blstg{i}", [64, 512], F32)) for i in range(2)]

        MSZ = 76288
        M = ent(nc.sbuf_tensor("mega", [128, MSZ], BF16))

        def mb(off_kb, sz_kb, dtype=BF16):
            e0 = int(off_kb * 512)
            n = int(sz_kb * 512)
            assert e0 + n <= MSZ, f"arena overflow {off_kb}+{sz_kb}"
            ap = M[:, e0:e0 + n]
            if dtype == F32:
                ap = ap.bitcast(F32)
            return ap

        xtbf = mb(0, 64.25)
        wsl = [mb(64.25 + 16 * i, 16, F32) for i in range(2)]
        wbfs = [mb(96.25 + 8 * i, 8) for i in range(2)]
        projf = [mb(112.25 + 4.25 * i, 4.25, F32) for i in range(2)]
        convt = [mb(120.75 + 4 * i, 4, F32) for i in range(2)]
        xbcb = [mb(128.75 + 2 * i, 2) for i in range(2)]
        gtile = [mb(112.25 + 2 * i, 2) for i in range(2)]
        ctl = mb(104, 2)
        rbc = [mb(0 + 8 * i, 8, F32) for i in range(2)]
        p1b = [mb(16 + 4 * i, 4) for i in range(2)]
        lmt = [mb(24 + 4 * i, 4) for i in range(2)]
        scp = [mb(32 + 4 * i, 4) for i in range(2)]
        xgw = [mb(46 + 5 * i, 5) for i in range(2)]
        xdt = [mb(56 + 4 * i, 4) for i in range(2)]
        xdtp = [mb(64 + 4 * i, 4) for i in range(2)]
        ydb = [mb(72 + 2 * i, 2) for i in range(2)]
        stb2 = [mb(76 + 1 * i, 1) for i in range(2)]
        scr2 = mb(78, 8, F32)
        sloc = mb(116.5, 32, F32)
        ybuf = mb(0, 32)
        pvb = mb(32, 16)
        stre = mb(48, 16)
        ydre = mb(64, 16)
        ebc = [mb(80 + 8 * i, 8, F32) for i in range(2)]
        cpb = [mb(96 + 4 * i, 4) for i in range(2)]
        ygg = mb(106, 4)
        gtall = mb(64, 32)
        gt2 = [mb(110 + 0.5 * i, 0.5) for i in range(2)]
        sqt = [mb(111 + 0.5 * i, 0.5) for i in range(2)]
        ngt = [mb(112 + 0.5 * i, 0.5) for i in range(2)]
        rsr = mb(113, 1, F32)
        x2b = mb(0, 64)
        resid = mb(64, 64)
        wsm = [mb(128 + 2 * i, 2, F32) for i in range(2)]
        wsmb = [mb(132 + 1 * i, 1) for i in range(2)]
        ngrh = [mb(134 + 2 * i, 2) for i in range(2)]
        xtre = [mb(138 + 2 * i, 2) for i in range(2)]
        outt = [mb(64 + 4 * i, 4, F32) for i in range(2)]

        outt = [mb(0 + 4 * i, 4, F32) for i in range(2)]
        rowg = mb(114, 1, F32)[0:1, 0:256]
        rowg2 = mb(115, 1, F32)[0:1, 0:256]
        wsl5f = mb(0, 16, F32)
        wbf5 = [mb(16 + 8 * i, 8) for i in range(2)]
        rhs8 = [mb(32 + 16 * i, 16) for i in range(2)]
        wslff = mb(134, 8, F32)
        wbff = [mb(142, 4), mb(128, 4)]
        rowb = mb(48, 4, F32)[0:1, 0:1024]
        rowb2 = mb(52, 4, F32)[0:1, 0:1024]
        rowb3 = mb(56, 4, F32)[0:1, 0:1024]
        mrep = mb(138, 4, F32)
        rrep = mb(142, 4, F32)
        tmpf1 = mb(128, 4, F32)

        pbank_t = [ent(nc.psum_tensor(f"pb{i}", [128, 512], F32)) for i in range(8)]
        pbank = [pbank_t[i][:] for i in range(8)]
        ptb = [pbank_t[6 + i][:].bitcast(BF16) for i in range(2)]

        sem_names = ["W", "G", "G2", "V", "A", "Tn", "CC", "WS0", "WS1", "XS0", "XS1",
                     "XG0", "XG1", "RB0", "RB1", "YS0", "YS1", "SS0", "SS1", "SG0", "SG1"]
        SEM = {n: ent(nc.semaphore(n)) for n in sem_names}
        CNT = {n: 0 for n in sem_names}
        sched = []

        def op(eng, builder, wait=(), sem=None, k=1):
            for (s, v) in wait:
                assert v <= CNT[s], f"wait {s}>={v} but only {CNT[s]} scheduled"
            incs = []
            if sem is not None:
                CNT[sem] += k
                incs = [(sem, k)]
            sched.append((eng, builder, [(s, v) for (s, v) in wait if v > 0], incs))
            return {s: CNT[s] for s in sem_names}

        def bar(*sems):
            return [(s, CNT[s]) for s in sems]

        def allbar():
            return bar(*sem_names)

        def dma(queue, out, in_, wait=(), sem=None, slow_ok=False):
            if sem is None:
                sem = "W" if queue == "sync" else ("G" if queue == "gpsimd" else "G2")
            return op(queue, (lambda e, o=out, i=in_, s=slow_ok: e.dma_start(
                out=o, in_=i, allow_slow_non_contiguous=s)), wait, sem, 16)

        def phase_barrier():
            b = allbar()
            for e in ("sync", "gpsimd", "vector", "scalar", "tensor"):
                sched.append((e, None, [(s, v) for (s, v) in b if v > 0], []))

        for dst, src in ((identbf[:], d_identbf), (identf[:], d_identf), (maskm[:], d_maskm),
                         (ltm[:], d_ltm), (onesb[:], d_onesb), (acol[:], d_acol),
                         (dtbc[:], d_dtb), (drep[:], d_drep), (nwc[:], d_nw),
                         (bcolc[:], d_bcol), (gbc[:], d_gb), (cwc[:], d_cw), (cbc[:], d_cb),
                         (ringm[:], d_ringm), (ringm1[:], d_ringm1),
                         (ringmc[:], d_ringmc), (ringmc1[:], d_ringmc1),
                         (epsc[:], d_epsc), (onesrow[:], d_onesrow)):
            dma("gpsimd", dst, src)
        op("vector", lambda e: e.memset(zcol[:], 0.0), sem="V")
        op("vector", lambda e: e.memset(logd[:], 0.0), sem="V")

        xcast_done = []
        for kc in range(KC):
            par = f"SG{kc % 2}"
            w = [("V", xcast_done[kc - 2])] if kc >= 2 else (bar("G") if kc == 0 else ())
            c = dma("gpsimd", projf[kc % 2][:, 0:TH], d_xh[kc], wait=w, sem=par)
            c = op("vector",
                   (lambda e, kc=kc: e.tensor_copy(out=xtbf[:, kc * TH:(kc + 1) * TH],
                                                   in_=projf[kc % 2][:, 0:TH])),
                   wait=[(par, c[par])], sem="V")
            xcast_done.append(c["V"])
            dma("sync", d_xtbf[kc], xtbf[:, kc * TH + 3:(kc + 1) * TH], wait=bar("V"))

        if PHASES >= 1:
            v_cast = {}
            t_mm = {}
            a_evac = {}
            v_conv = {}
            g_store = {}

            def p1_fetch(fc):
                b = fc % 2
                ws_ = f"WS{b}"
                w = [("V", v_cast[fc - 2])] if fc >= 2 else ()
                c0 = dma("sync", wsl[b][:], d_wxbc[fc].rearrange("p kc f -> p (kc f)"),
                         wait=w, sem=ws_)
                wv = [(ws_, c0[ws_])]
                if fc >= 2:
                    wv.append(("Tn", t_mm[fc - 2]))
                c = op("vector", (lambda e, b=b: e.tensor_copy(out=wbfs[b][:], in_=wsl[b][:])),
                       wait=wv, sem="V")
                v_cast[fc] = c["V"]

            p1_fetch(0)
            p1_fetch(1)
            for fc in range(NXBC):
                b = fc % 2
                wt = [("V", v_cast[fc])]
                if fc >= 2:
                    wt.append(("A", a_evac[fc - 2]))
                for kc in range(KC):
                    for tsi in range(3):
                        def mmb(e, b=b, kc=kc, tsi=tsi):
                            return e.matmul(
                                pbank[3 * b + tsi][:, 0:TS[tsi]],
                                wbfs[b][:, kc * 128:(kc + 1) * 128],
                                xtbf[:, kc * TH + TS_OFF[tsi]:kc * TH + TS_OFF[tsi] + TS[tsi]],
                                start=(kc == 0), stop=(kc == KC - 1))
                        c = op("tensor", mmb, wait=(wt if (kc == 0 and tsi == 0) else ()),
                               sem=("Tn" if (kc == KC - 1 and tsi == 2) else None))
                t_mm[fc] = c["Tn"]
                if fc + 2 < NXBC:
                    p1_fetch(fc + 2)
                we = [("Tn", t_mm[fc])]
                if fc >= 2:
                    we.append(("V", v_conv.get(fc - 2, 0)))
                else:
                    we.append(("V", v_cast[1]))
                for tsi in range(3):
                    c = op("scalar",
                           (lambda e, b=b, tsi=tsi: e.copy(
                               out=projf[b][:, TS_OFF[tsi]:TS_OFF[tsi] + TS[tsi]],
                               in_=pbank[3 * b + tsi][:, 0:TS[tsi]])),
                           wait=(we if tsi == 0 else ()), sem="A")
                a_evac[fc] = c["A"]
                if fc < 80:
                    wv2 = [("A", a_evac[fc])]
                    for k in range(4):
                        if k == 0:
                            c = op("vector",
                                   (lambda e, b=b, fc=fc: e.tensor_scalar(
                                       out=convt[b][:], in0=projf[b][:, 0:T],
                                       scalar1=cwc[:, fc * 4:fc * 4 + 1], scalar2=None,
                                       op0=ALU.mult)),
                                   wait=wv2, sem="V")
                        else:
                            c = op("vector",
                                   (lambda e, b=b, fc=fc, k=k: e.scalar_tensor_tensor(
                                       out=convt[b][:], in0=projf[b][:, k:k + T],
                                       scalar=cwc[:, fc * 4 + k:fc * 4 + k + 1],
                                       in1=convt[b][:], op0=ALU.mult, op1=ALU.add)),
                                   sem="V")
                    v_conv[fc] = c["V"]
                    ws = [("V", v_conv[fc])]
                    if fc >= 2:
                        ws.append((f"XS{b}", g_store[fc - 2]))
                    c = op("scalar",
                           (lambda e, b=b, fc=fc: e.activation(
                               out=xbcb[b][:], in_=convt[b][:], func=AF.Silu,
                               bias=cbc[:, fc:fc + 1], scale=1.0)),
                           wait=ws, sem="A")
                    c = dma("gpsimd", d_xbcs[fc], xbcb[b][:], wait=bar("A"), sem=f"XS{b}")
                    g_store[fc] = c[f"XS{b}"]
                else:
                    op("scalar", (lambda e, b=b: e.activation(
                        out=dtq[:], in_=projf[b][:, 3:TH], func=AF.Identity,
                        bias=dtbc[:, 0:1])), wait=bar("A", "V"), sem="A")
                    op("scalar", (lambda e: e.activation(out=convt[0][:], in_=dtq[:],
                                                         func=AF.Abs)), sem="A")
                    op("scalar", (lambda e: e.activation(out=convt[1][:], in_=convt[0][:],
                                                         func=AF.Exp, scale=-1.0)), sem="A")
                    op("scalar", (lambda e: e.activation(out=convt[0][:], in_=convt[1][:],
                                                         func=AF.Ln, bias=1.0)), sem="A")
                    op("vector", (lambda e: e.tensor_relu(out=convt[1][:], in_=dtq[:])),
                       wait=bar("A"), sem="V")
                    op("vector", (lambda e: e.tensor_add(out=dtq[:], in0=convt[1][:],
                                                         in1=convt[0][:])), sem="V")
                    op("vector", (lambda e: e.tensor_scalar(out=dtq[:], in0=dtq[:],
                                                            scalar1=100.0, scalar2=0.001,
                                                            op0=ALU.min, op1=ALU.max)), sem="V")
                    op("vector", (lambda e: e.tensor_scalar(out=datq[:], in0=dtq[:],
                                                            scalar1=acol[:, 0:1], scalar2=None,
                                                            op0=ALU.mult)), sem="V")
                    if DEBUG:
                        dma("gpsimd", d_dts, dtq[:], wait=bar("V"))

        if PHASES >= 2:
            phase_barrier()
            op("vector", lambda e: e.memset(sloc[:], 0.0), sem="V")
            grp_done_V = {}
            g_csf = {}
            grp_done_T = {}
            v_states_evac = {0: 0, 1: 0}
            v_yd_evac = {}
            g_yd_store = {}
            g_st_store = {0: 0, 1: 0}
            v_p2 = {}
            a_exp = {}
            v_xdt_done = {}
            v_bt_evac = {0: 0, 1: 0}
            v_sloc_upd = 0
            for wv in range(4):
                wpre = list(bar("V", "A", "Tn", "G", "G2", "XG0", "XG1", "RB0", "RB1",
                                "YS0", "YS1", "SS0", "SS1"))
                v_pre = 0
                first_g = True
                for j in range(2):
                    c = 2 * wv + j
                    op("vector", (lambda e, c=c, j=j: e.tensor_tensor_scan(
                        out=csb[:, j * 128:(j + 1) * 128],
                        data0=datq[:, c * 128:(c + 1) * 128], data1=zcol[:],
                        initial=0.0, op0=ALU.add, op1=ALU.add)),
                       wait=(wpre if j == 0 else ()), sem="V")
                    op("vector", (lambda e, j=j: e.tensor_add(
                        out=logd[:], in0=logd[:], in1=csb[:, j * 128 + 127:j * 128 + 128])),
                       wait=[("V", CNT["V"])], sem="V")
                    op("scalar", (lambda e, j=j: e.activation(
                        out=ecsb[:, j * 128:(j + 1) * 128], in_=csb[:, j * 128:(j + 1) * 128],
                        func=AF.Exp)), wait=bar("V"), sem="A")
                    op("scalar", (lambda e, j=j: e.activation(
                        out=dcy[:, j * 128:(j + 1) * 128], in_=csb[:, j * 128:(j + 1) * 128],
                        func=AF.Exp, scale=-1.0,
                        bias=csb[:, j * 128 + 127:j * 128 + 128])), sem="A")
                    wg0 = list(bar("A")) + ([("V", CNT["V"])] if first_g else [])
                    first_g = False
                    dma("gpsimd", d_ecs[c].rearrange("(h l) -> h l", h=128),
                        ecsb[:, j * 128:(j + 1) * 128], wait=wg0)
                    dma("gpsimd", d_cdr[c].rearrange("(h o) -> h o", o=1),
                        ecsb[:, j * 128 + 127:j * 128 + 128])
                    ccsf = dma("gpsimd", d_csf[c].rearrange("(h l) -> h l", h=128),
                               csb[:, j * 128:(j + 1) * 128])
                    g_csf[c] = ccsf["G"]
                    cg = dma("gpsimd", cdrp[:, j * 128:(j + 1) * 128],
                             d_cdr[c].rearrange("(o h) -> o h", o=1).partition_broadcast(128),
                             wait=bar("G"))
                    g_cdrp_wave = cg["G"]
                    wtp = list(bar("A", "V")) + [("V", v_states_evac[0]),
                                                 ("V", v_states_evac[1])]
                    op("tensor", (lambda e, j=j: e.transpose(
                        pbank[0][:, 0:128], csb[:, j * 128:(j + 1) * 128], identf[:])),
                       wait=wtp, sem="Tn")
                    op("tensor", (lambda e, j=j: e.transpose(
                        pbank[0][:, 128:256], dcy[:, j * 128:(j + 1) * 128], identf[:])),
                       sem="Tn")
                    op("tensor", (lambda e, c=c: e.transpose(
                        pbank[0][:, 256:384], dtq[:, c * 128:(c + 1) * 128], identf[:])),
                       sem="Tn")
                    op("vector", (lambda e, j=j: e.tensor_copy(
                        out=csT[:, j * 128:(j + 1) * 128], in_=pbank[0][:, 0:128])),
                       wait=bar("Tn"), sem="V")
                    op("vector", (lambda e, j=j: e.tensor_copy(
                        out=dcyT[:, j * 128:(j + 1) * 128], in_=pbank[0][:, 128:256])),
                       sem="V")
                    c2 = op("vector", (lambda e, j=j: e.tensor_copy(
                        out=dtT[:, j * 128:(j + 1) * 128], in_=pbank[0][:, 256:384])),
                       sem="V")
                    v_pre = c2["V"]

                for g in range(NG):
                    gidx = wv * NG + g
                    gb_ = g % 2
                    xg_ = f"XG{gb_}"
                    wl = []
                    if gidx >= 2:
                        wl = [("V", grp_done_V[gidx - 2]), ("Tn", grp_done_T[gidx - 2])]
                    for i in range(8):
                        dma("scalar", xgw[gb_][:, i * 256:(i + 1) * 256],
                            d_xbcs[8 * g + i][:, wv * 256:(wv + 1) * 256],
                            wait=(wl if i == 0 else ()), sem=xg_)
                    dma("scalar", xgw[gb_][:, 8 * 256:9 * 256],
                        d_xbcs[64 + g][:, wv * 256:(wv + 1) * 256], sem=xg_)
                    c2 = dma("scalar", xgw[gb_][:, 9 * 256:10 * 256],
                             d_xbcs[72 + g][:, wv * 256:(wv + 1) * 256], sem=xg_)
                    g_xgw = c2[xg_]

                    for j in range(2):
                        c = 2 * wv + j
                        sidx = 2 * gidx + j
                        jj = sidx % 2
                        rb_ = f"RB{jj}"
                        wr = [("V", v_p2.get(jj, 0)), ("G", g_csf[c])]
                        c2 = dma("sync", rbc[jj][:],
                                 d_csf[c][g * 2048:(g + 1) * 2048].partition_broadcast(128),
                                 wait=wr, sem=rb_)
                        grbc = c2[rb_]
                        wp1 = [(rb_, grbc), ("V", v_pre), ("A", a_exp.get(jj, 0))]
                        h0 = 16 * g
                        op("vector", (lambda e, jj=jj, j=j, h0=h0: e.tensor_tensor(
                            out=scr2[:].rearrange("p (k l) -> p k l", k=16),
                            in0=rbc[jj][:].rearrange("p (k l) -> p k l", k=16),
                            in1=csT[:, j * 128 + h0:j * 128 + h0 + 16]
                                .unsqueeze(2).broadcast_to([128, 16, 128]),
                            op=ALU.subtract)), wait=wp1, sem="V")
                        c2 = op("vector", (lambda e, jj=jj: e.tensor_tensor(
                            out=p1b[jj][:].rearrange("p (k l) -> p k l", k=16),
                            in0=scr2[:].rearrange("p (k l) -> p k l", k=16),
                            in1=maskm[:].unsqueeze(1).broadcast_to([128, 16, 128]),
                            op=ALU.add)), sem="V")
                        vp1 = c2["V"]
                        c2 = op("scalar", (lambda e, jj=jj: e.activation(
                            out=lmt[jj][:], in_=p1b[jj][:], func=AF.Exp)),
                           wait=[("V", vp1), ("V", v_p2.get(jj, 0))], sem="A")
                        a_exp[jj] = c2["A"]
                        ws = [(xg_, g_xgw), ("V", v_p2.get(jj, 0))]
                        c2 = op("tensor", (lambda e, gb_=gb_, j=j, jj=jj: e.matmul(
                            pbank[2 + jj][:, 0:128],
                            xgw[gb_][:, 8 * 256 + j * 128:8 * 256 + (j + 1) * 128],
                            xgw[gb_][:, 9 * 256 + j * 128:9 * 256 + (j + 1) * 128],
                            start=True, stop=True)),
                           wait=ws, sem="Tn")
                        t_sc = c2["Tn"]
                        wp2 = [("Tn", t_sc), ("A", a_exp[jj])]
                        if gidx >= 1:
                            wp2.append(("Tn", grp_done_T[gidx - 1]))
                        for q in range(4):
                            c2 = op("vector", (lambda e, jj=jj, q=q: e.tensor_tensor(
                                out=scp[jj][:, q * 512:(q + 1) * 512].rearrange(
                                    "p (r f) -> p r f", r=4),
                                in0=pbank[2 + jj][:, 0:128].unsqueeze(1).broadcast_to(
                                    [128, 4, 128]),
                                in1=lmt[jj][:, q * 512:(q + 1) * 512].rearrange(
                                    "p (r f) -> p r f", r=4),
                                op=ALU.mult)),
                               wait=(wp2 if q == 0 else ()), sem="V")
                        v_p2[jj] = c2["V"]
                        t_tx = {}
                        for i in range(8):
                            tp = i // 4
                            wtx = ([("V", v_xdt_done.get(tp, 0))] if i % 4 == 0 else ())
                            c2 = op("tensor", (lambda e, gb_=gb_, i=i, j=j, tp=tp: e.transpose(
                                ptb[tp][:, (i % 4) * 128:(i % 4 + 1) * 128],
                                xgw[gb_][:, i * 256 + j * 128:i * 256 + (j + 1) * 128],
                                identbf[:])), wait=wtx, sem="Tn")
                            if i % 4 == 3:
                                t_tx[tp] = c2["Tn"]
                        for half in range(2):
                            hh0 = 16 * g + 8 * half
                            op("vector", (lambda e, jj=jj, j=j, half=half, hh0=hh0:
                               e.tensor_tensor(
                                   out=xdt[jj][:, half * 512:(half + 1) * 512]
                                       .rearrange("p (k d) -> p k d", k=8),
                                   in0=ptb[half][:, 0:512].rearrange("p (k d) -> p k d", k=8),
                                   in1=dtT[:, j * 128 + hh0:j * 128 + hh0 + 8]
                                       .unsqueeze(2).broadcast_to([128, 8, 64]),
                                   op=ALU.mult)),
                               wait=[("Tn", t_tx[half])], sem="V")
                            c2 = op("vector", (lambda e, jj=jj, j=j, half=half, hh0=hh0:
                                    e.tensor_tensor(
                                        out=xdtp[jj][:, half * 512:(half + 1) * 512]
                                            .rearrange("p (k d) -> p k d", k=8),
                                        in0=xdt[jj][:, half * 512:(half + 1) * 512]
                                            .rearrange("p (k d) -> p k d", k=8),
                                        in1=dcyT[:, j * 128 + hh0:j * 128 + hh0 + 8]
                                            .unsqueeze(2).broadcast_to([128, 8, 64]),
                                        op=ALU.mult)), sem="V")
                            v_xdt_done[half] = c2["V"]
                        t_ydiag = {}
                        for i in range(8):
                            bk = 4 + (i // 4)
                            off = (i % 4) * 128
                            wyd = []
                            if i % 4 == 0:
                                wyd = [("V", v_xdt_done[i // 4]),
                                       ("V", v_yd_evac.get(bk, 0))]
                            op("tensor", (lambda e, jj=jj, i=i, bk=bk, off=off: e.matmul(
                                pbank[bk][0:64, off:off + 128],
                                xdt[jj][:, (2 * i) * 64:(2 * i + 1) * 64],
                                scp[jj][:, (2 * i) * 128:(2 * i + 1) * 128],
                                start=True, stop=True, tile_position=(0, 0))),
                               wait=wyd, sem="Tn")
                            c2 = op("tensor", (lambda e, jj=jj, i=i, bk=bk, off=off: e.matmul(
                                pbank[bk][64:128, off:off + 128],
                                xdt[jj][:, (2 * i + 1) * 64:(2 * i + 2) * 64],
                                scp[jj][:, (2 * i + 1) * 128:(2 * i + 2) * 128],
                                start=True, stop=True, tile_position=(0, 64))),
                               sem="Tn")
                            t_ydiag[bk] = c2["Tn"]
                        for half in range(2):
                            bk = 4 + half
                            fc0 = 8 * g + 4 * half
                            wye = [("Tn", t_ydiag[bk]), (f"YS{jj}", g_yd_store.get(jj, 0))]
                            xb0 = (4 * half) * 256 + j * 128
                            op("vector", (lambda e, gb_=gb_, xb0=xb0, fc0=fc0:
                               e.tensor_tensor(
                                   out=scr2[:, 0:512].rearrange("p (q l) -> p q l", q=4),
                                   in0=xgw[gb_][:, xb0:xb0 + 1024]
                                       .rearrange("p (i l) -> p i l", i=4)[:, :, 0:128],
                                   in1=drep[:, fc0:fc0 + 4]
                                       .unsqueeze(2).broadcast_to([128, 4, 128]),
                                   op=ALU.mult)), wait=wye, sem="V")
                            c2 = op("vector", (lambda e, jj=jj, bk=bk, half=half:
                                    e.tensor_add(
                                        out=ydb[jj][:, half * 512:(half + 1) * 512],
                                        in0=scr2[:, 0:512],
                                        in1=pbank[bk][:, 0:512])), sem="V")
                            v_yd_evac[bk] = c2["V"]
                        c2 = dma("gpsimd", d_yds[c][:, (8 * g) * 128:(8 * g) * 128 + 1024],
                                 ydb[jj][:], wait=[("V", v_yd_evac[5])], sem=f"YS{jj}")
                        g_yd_store[jj] = c2[f"YS{jj}"]
                        wbt = [("V", v_bt_evac[j])]
                        c2 = op("tensor", (lambda e, gb_=gb_, j=j: e.transpose(
                            ptb[j][:, 512:640],
                            xgw[gb_][:, 8 * 256 + j * 128:8 * 256 + (j + 1) * 128],
                            identbf[:])), wait=wbt, sem="Tn")
                        c2 = op("vector", (lambda e, j=j: e.tensor_copy(
                            out=btb[:, j * 128:(j + 1) * 128], in_=ptb[j][:, 512:640])),
                           wait=[("Tn", c2["Tn"])], sem="V")
                        v_bt_evac[j] = c2["V"]
                        for half in range(2):
                            wsm_ = [("V", v_bt_evac[j]), ("V", v_states_evac[half])]
                            for q in range(8):
                                k = 8 * half + q
                                c2 = op("tensor", (lambda e, jj=jj, j=j, k=k, q=q, half=half:
                                        e.matmul(
                                            pbank[half][:, q * 64:(q + 1) * 64],
                                            btb[:, j * 128:(j + 1) * 128],
                                            xdtp[jj][:, k * 64:(k + 1) * 64],
                                            start=True, stop=True)),
                                       wait=(wsm_ if q == 0 else ()), sem="Tn")
                            t_st = c2["Tn"]
                            wse = [("Tn", t_st), (f"SS{half}", g_st_store[half]),
                                   ("V", v_sloc_upd), ("G", g_cdrp_wave)]
                            c2 = op("vector", (lambda e, half=half: e.tensor_copy(
                                out=stb2[half][:], in_=pbank[half][:])),
                               wait=wse, sem="V")
                            vse = c2["V"]
                            v_states_evac[half] = vse
                            c2 = dma("gpsimd",
                                     d_sts[c][:, (16 * g + 8 * half) * 64:(16 * g + 8 * half) * 64 + 512],
                                     stb2[half][:], wait=[("V", vse)], sem=f"SS{half}")
                            g_st_store[half] = c2[f"SS{half}"]
                            h0u = 16 * g + 8 * half
                            op("vector", (lambda e, j=j, h0u=h0u: e.tensor_tensor(
                                out=sloc[:, h0u * 64:h0u * 64 + 512]
                                    .rearrange("p (k d) -> p k d", k=8),
                                in0=sloc[:, h0u * 64:h0u * 64 + 512]
                                    .rearrange("p (k d) -> p k d", k=8),
                                in1=cdrp[:, j * 128 + h0u:j * 128 + h0u + 8]
                                    .unsqueeze(2).broadcast_to([128, 8, 64]),
                                op=ALU.mult)), sem="V")
                            c2 = op("vector", (lambda e, half=half, h0u=h0u: e.tensor_add(
                                out=sloc[:, h0u * 64:h0u * 64 + 512],
                                in0=sloc[:, h0u * 64:h0u * 64 + 512],
                                in1=stb2[half][:])), sem="V")
                            v_sloc_upd = c2["V"]
                    grp_done_V[gidx] = CNT["V"]
                    grp_done_T[gidx] = CNT["Tn"]

        if PHASES >= 3:
            phase_barrier()
            dma("gpsimd", d_ccsi[:, 0:8192], sloc[:])
            dma("gpsimd", d_ccsi[:, 8192:8193], logd[:], slow_ok=True)
            if DEBUG:
                dma("gpsimd", d_dbg_sloc, sloc[:])
            op("gpsimd", (lambda e: e.collective_compute(
                "AllGather", ALU.bypass,
                replica_groups=[[0, 1, 2, 3], [4, 5, 6, 7]],
                ins=[d_ccsi],
                outs=[d_ccso])), wait=bar("G"), sem="CC")

            for kc in range(KC):
                dma("gpsimd", xtbf[:, kc * T:(kc + 1) * T], d_xtbf[kc], sem="G2")
            xg_ld = bar("G2")
            v_gcast = {}
            t_gmm = {}
            a_gsilu = {}
            g_gstore = {}
            for fc in range(NFI):
                b = fc % 2
                ws_ = f"WS{b}"
                w = [("V", v_gcast[fc - 2])] if fc >= 2 else ()
                c0 = dma("sync", wsl[b][:], d_wgate[fc].rearrange("p kc f -> p (kc f)"),
                         wait=w, sem=ws_)
                wv_ = [(ws_, c0[ws_])] + ([("Tn", t_gmm[fc - 2])] if fc >= 2 else [])
                c2 = op("vector", (lambda e, b=b: e.tensor_copy(out=wbfs[b][:], in_=wsl[b][:])),
                        wait=wv_, sem="V")
                v_gcast[fc] = c2["V"]
                wt = [("V", c2["V"])] + xg_ld + ([("A", a_gsilu[fc - 2])] if fc >= 2 else [])
                for kc in range(KC):
                    for tsi in range(2):
                        c2 = op("tensor", (lambda e, b=b, kc=kc, tsi=tsi: e.matmul(
                            pbank[2 * b + tsi][:],
                            wbfs[b][:, kc * 128:(kc + 1) * 128],
                            xtbf[:, kc * T + tsi * 512:kc * T + (tsi + 1) * 512],
                            start=(kc == 0), stop=(kc == KC - 1))),
                           wait=(wt if (kc == 0 and tsi == 0) else ()),
                           sem=("Tn" if (kc == KC - 1 and tsi == 1) else None))
                t_gmm[fc] = c2["Tn"]
                ws = [("Tn", c2["Tn"])] + ([(f"XS{b}", g_gstore[fc - 2])] if fc >= 2 else [])
                for tsi in range(2):
                    c2 = op("scalar", (lambda e, b=b, tsi=tsi: e.activation(
                        out=gtile[b][:, tsi * 512:(tsi + 1) * 512],
                        in_=pbank[2 * b + tsi][:], func=AF.Silu)),
                       wait=(ws if tsi == 0 else ()), sem="A")
                a_gsilu[fc] = c2["A"]
                c2 = dma("gpsimd", d_gs[fc], gtile[b][:], wait=[("A", c2["A"])], sem=f"XS{b}")
                g_gstore[fc] = c2[f"XS{b}"]

            phase_barrier()
            op("vector", lambda e: e.memset(sloc[:], 0.0),
               wait=list(bar("G")) + [("CC", CNT["CC"])], sem="V")
            for jr in range(4):
                dma("gpsimd", erow[jr:jr + 1, :],
                    d_ccso[jr * 128:(jr + 1) * 128, 8192:8193].rearrange("h o -> o h"),
                    wait=(bar("CC") if jr == 0 else ()), slow_ok=True)
            op("scalar", (lambda e: e.activation(out=expdr[:], in_=erow[:], func=AF.Exp)),
               wait=bar("G"), sem="A")
            op("vector", (lambda e: e.tensor_scalar(
                out=expdr[:], in0=expdr[:], scalar1=ringmc[0:4, 0:1],
                scalar2=ringmc1[0:4, 0:1], op0=ALU.mult, op1=ALU.add)),
               wait=bar("A"), sem="V")
            for jr in range(4):
                dma("gpsimd", d_rows[jr][0:128].rearrange("(o f) -> o f", o=1),
                    expdr[jr:jr + 1, :], wait=bar("V"), sem="SG0")
            g_rows_done = CNT["SG0"]
            Sstg = mb(64.25, 32, F32)
            for jr in range(3):
                dma("gpsimd", Sstg[:], d_ccso[jr * 128:(jr + 1) * 128, 0:8192], wait=bar("V"))
                c2 = op("vector", (lambda e, jr=jr: e.tensor_scalar(
                    out=Sstg[:], in0=Sstg[:], scalar1=ringm[:, jr:jr + 1], scalar2=None,
                    op0=ALU.mult)), wait=bar("G"), sem="V")
                dma("gpsimd", cdrp[:, 0:128],
                    d_rows[jr][0:128].rearrange("(o f) -> o f", o=1).partition_broadcast(128),
                    wait=[("V", c2["V"]), ("SG0", g_rows_done)])
                wcomb = bar("G", "V")
                op("vector", (lambda e: e.tensor_tensor(
                    out=sloc[:].rearrange("p (h d) -> p h d", h=128),
                    in0=sloc[:].rearrange("p (h d) -> p h d", h=128),
                    in1=cdrp[:, 0:128].unsqueeze(2).broadcast_to([128, 128, 64]),
                    op=ALU.mult)), wait=wcomb, sem="V")
                c2 = op("vector", (lambda e: e.tensor_add(
                    out=sloc[:], in0=sloc[:], in1=Sstg[:])), sem="V")
            if DEBUG:
                dma("gpsimd", d_dbg_state, sloc[:], wait=bar("V"))

        if PHASES >= 4:
            phase_barrier()
            v_supd2 = 0
            v_ycomb = {4: 0, 5: 0}
            v_cp = {0: 0, 1: 0}
            t_yoff = {0: 0, 1: 0}
            v_yg = {}
            v_gate_done = {}
            t_mq = {}
            a_ln_grp = {}
            sg1_prev = {}
            g_ng = {0: 0, 1: 0}
            v_apply_grp = {}
            sw_bc = 0
            for wv in range(4):
                wpre2 = list(bar("V", "A", "Tn", "G", "SS0", "SS1", "YS0", "YS1",
                                 "XG0", "XG1", "RB0", "RB1", "XS0", "XS1", "SG1"))
                for j in range(2):
                    c = 2 * wv + j
                    w0 = (wpre2 if j == 0 else [("V", v_supd2)])
                    cst4 = []
                    for q in range(4):
                        cq = dma("gpsimd", stre[:, q * 2048:(q + 1) * 2048],
                                 d_sts[c][:, q * 2048:(q + 1) * 2048],
                                 wait=(w0 if q == 0 else ()), sem="SS0")
                        cst4.append(cq["SS0"])
                    wyd0 = ([("V", v_ycomb[4]), ("V", v_ycomb[5])] if (wv + j) else [])
                    wyd0 += bar("SS0", "SS1")
                    cyd8 = []
                    for q in range(8):
                        cq = dma("gpsimd", ydre[:, q * 1024:(q + 1) * 1024],
                                 d_yds[c][:, q * 1024:(q + 1) * 1024],
                                 wait=(wyd0 if q == 0 else ()), sem="YS0")
                        cyd8.append(cq["YS0"])
                    ccd = dma("gpsimd", cdrp[:, 0:128],
                              d_cdr[c].rearrange("(o h) -> o h", o=1).partition_broadcast(128),
                              wait=[("V", v_supd2)], sem="G")
                    wcpv = [("Tn", max(t_yoff[0], t_yoff[1]))]
                    if j == 0:
                        wcpv += bar("SG1", "XS0", "XS1", "SS0", "SS1")
                    cpv = op("vector", (lambda e: e.tensor_copy(out=pvb[:], in_=sloc[:])),
                             wait=wcpv, sem="V")
                    for g in range(NG):
                        par = g % 2
                        cct = dma("scalar", ctl[:, g * 128:(g + 1) * 128],
                                  d_xbcs[72 + g][:, c * 128:(c + 1) * 128],
                                  wait=([("V", v_cp[par])] if (wv + j + g) else ()), sem="XG0")
                        ceb = dma("sync", ebc[par][:],
                                  d_ecs[c][g * 2048:(g + 1) * 2048].partition_broadcast(128),
                                  wait=[("V", v_cp[par]),
                                        ("V", v_gate_done.get(wv - 1, 0))], sem=f"RB{par}")
                        ccp = op("vector", (lambda e, par=par, g=g: e.tensor_tensor(
                            out=cpb[par][:].rearrange("p (r f) -> p r f", r=16),
                            in0=ebc[par][:].rearrange("p (r f) -> p r f", r=16),
                            in1=ctl[:, g * 128:(g + 1) * 128].unsqueeze(1).broadcast_to(
                                [128, 16, 128]),
                            op=ALU.mult)),
                           wait=[("XG0", cct["XG0"]), (f"RB{par}", ceb[f"RB{par}"]),
                                 ("Tn", t_yoff[par]), ("V", cpv["V"])], sem="V")
                        v_cp[par] = ccp["V"]
                        t_ydk = {4: 0, 5: 0}
                        for i in range(8):
                            bk = 4 + (i // 4)
                            off = (i % 4) * 128
                            for hh in range(2):
                                h = 16 * g + 2 * i + hh
                                wyo = []
                                if i == 0 and hh == 0:
                                    wyo = [("V", ccp["V"]), ("V", v_ycomb[4]),
                                           ("V", v_ycomb[5])]
                                cmo = op("tensor", (lambda e, par=par, i=i, hh=hh, h=h,
                                                    bk=bk, off=off: e.matmul(
                                    pbank[bk][hh * 64:(hh + 1) * 64, off:off + 128],
                                    pvb[:, h * 64:(h + 1) * 64],
                                    cpb[par][:, (2 * i + hh) * 128:(2 * i + hh + 1) * 128],
                                    start=True, stop=True,
                                    tile_position=(0, hh * 64))),
                                   wait=wyo, sem="Tn")
                            t_ydk[bk] = cmo["Tn"]
                        t_yoff[par] = cmo["Tn"]
                        for half in range(2):
                            bk = 4 + half
                            fc0 = 8 * g + 4 * half
                            cyc = op("vector", (lambda e, bk=bk, fc0=fc0, j=j: e.tensor_tensor(
                                out=ybuf[:, j * 8192 + fc0 * 128:j * 8192 + fc0 * 128 + 512],
                                in0=pbank[bk][:, 0:512],
                                in1=ydre[:, fc0 * 128:fc0 * 128 + 512],
                                op=ALU.add)),
                               wait=[("Tn", t_ydk[bk]), ("YS0", cyd8[g])], sem="V")
                            v_ycomb[bk] = cyc["V"]
                    op("vector", (lambda e: e.tensor_tensor(
                        out=sloc[:].rearrange("p (h d) -> p h d", h=128),
                        in0=sloc[:].rearrange("p (h d) -> p h d", h=128),
                        in1=cdrp[:, 0:128].unsqueeze(2).broadcast_to([128, 128, 64]),
                        op=ALU.mult)),
                       wait=[("G", ccd["G"]), ("V", cpv["V"])], sem="V")
                    for q in range(4):
                        csu = op("vector", (lambda e, q=q: e.tensor_add(
                            out=sloc[:, q * 2048:(q + 1) * 2048],
                            in0=sloc[:, q * 2048:(q + 1) * 2048],
                            in1=stre[:, q * 2048:(q + 1) * 2048])),
                           wait=[("SS0", cst4[q])], sem="V")
                    v_supd2 = csu["V"]
                if DEBUG:
                    for j in range(2):
                        dma("gpsimd", d_ys[2 * wv + j],
                            ybuf[:, j * 8192:(j + 1) * 8192], wait=bar("V"))
                c_gt = dma("gpsimd", gtall[:].rearrange("p (k t) -> p k t", k=64),
                           d_gs[0:64][:, :, wv * 256:(wv + 1) * 256]
                           .rearrange("k p t -> p k t"),
                           wait=[("V", v_ycomb[4]), ("V", v_ycomb[5]),
                                 ("V", v_cp[0]), ("V", v_cp[1])], sem="XS0")
                for g in range(NG):
                    for i in range(8):
                        fc = 8 * g + i
                        par = i % 2
                        gflat = (wv * NG + g) * 8 + i
                        wyg = []
                        if g == 0 and i == 0:
                            wyg = [("XS0", c_gt["XS0"]), ("V", CNT["V"])]
                        cyg = op("vector", (lambda e, fc=fc, i=i: e.tensor_tensor(
                            out=ygg[:, i * 256:(i + 1) * 256].rearrange(
                                "p (c l) -> p c l", c=2),
                            in0=gtall[:, fc * 256:(fc + 1) * 256]
                                .rearrange("p (c l) -> p c l", c=2),
                            in1=ybuf[:].rearrange("p (c f) -> p c f", c=2)[
                                :, :, fc * 128:(fc + 1) * 128],
                            op=ALU.mult)),
                           wait=wyg, sem="V")
                        v_yg[gflat] = cyg["V"]
                        csq = op("scalar", (lambda e, i=i, par=par: e.activation(
                            out=sqt[par][:], in_=ygg[:, i * 256:(i + 1) * 256],
                            func=AF.Square)),
                           wait=[("V", cyg["V"]), ("Tn", t_mq.get(par, 0))],
                           sem="A")
                        cmq = op("tensor", (lambda e, i=i, par=par: e.matmul(
                            pbank[0][0:1, 0:256], onesb[:, 0:1], sqt[par][:],
                            start=(i == 0), stop=(i == 7))),
                           wait=[("A", csq["A"])], sem="Tn")
                        t_mq[par] = cmq["Tn"]
                    cln = op("scalar", (lambda e: e.activation(
                        out=rowg, in_=pbank[0][0:1, 0:256],
                        func=AF.Ln, bias=epsc[0:1, 0:1], scale=1.0 / GROUP)),
                       wait=[("Tn", cmq["Tn"])], sem="A")
                    cex = op("scalar", (lambda e: e.activation(
                        out=rowg2, in_=rowg,
                        func=AF.Exp, scale=-0.5)), sem="A")
                    wrb = [("A", cex["A"]),
                           ("V", v_apply_grp.get(wv * NG + g - 1, 0))]
                    crb = op("tensor", (lambda e: e.matmul(
                        pbank[1][:, 0:256], onesrow[0:1, 0:128], rowg2,
                        start=True, stop=True)), wait=wrb, sem="Tn")
                    for i in range(8):
                        fc = 8 * g + i
                        cap = op("vector", (lambda e, fc=fc, i=i: e.scalar_tensor_tensor(
                            out=ybuf[:].rearrange("p (cc f) -> p cc f", cc=2)
                                [:, :, fc * 128:(fc + 1) * 128],
                            in0=ygg[:, i * 256:(i + 1) * 256]
                                .rearrange("p (cc l) -> p cc l", cc=2),
                            scalar=nwc[:, fc:fc + 1],
                            in1=pbank[1][:, 0:256].rearrange("p (cc l) -> p cc l", cc=2),
                            op0=ALU.mult, op1=ALU.mult)),
                           wait=([("Tn", crb["Tn"])] if i == 0 else ()), sem="V")
                    for cc in range(2):
                        cns = dma("gpsimd",
                                  d_ngs[8 * g:8 * g + 8][:, :, wv * 256 + cc * 128:
                                                         wv * 256 + cc * 128 + 128]
                                  .rearrange("k p l -> p k l"),
                                  ybuf[:, cc * 8192 + (8 * g) * 128:
                                       cc * 8192 + (8 * g) * 128 + 1024]
                                  .rearrange("p (k l) -> p k l", k=8),
                                  wait=([("V", cap["V"])] if cc == 0 else ()),
                                  sem=f"SS{cc}")
                        g_ng[cc] = cns[f"SS{cc}"]
                    v_apply_grp[wv * NG + g] = CNT["V"]
                v_gate_done[wv] = CNT["V"]

        if PHASES >= 5:
            phase_barrier()
            v_wcast5 = {}
            t_mm5 = {}
            v_evac5 = 0
            v_ev5b = {}
            for db in range(8):
                for fg in range(8):
                    idx = db * 8 + fg
                    wpar = fg % 2
                    w = [("V", v_wcast5[idx - 1])] if idx >= 1 else ()
                    cw5 = dma("sync", wsl5f[:].rearrange("p (k d f) -> p k d f", k=8, d=4),
                              d_wout[db, fg * 8:(fg + 1) * 8].rearrange("k d p f -> p k d f"),
                              wait=w, sem="WS0")
                    cc5 = op("vector", (lambda e, wpar=wpar: e.tensor_copy(
                        out=wbf5[wpar][:], in_=wsl5f[:])),
                       wait=[("WS0", cw5["WS0"])] +
                            ([("Tn", t_mm5[idx - 2])] if idx >= 2 else []), sem="V")
                    v_wcast5[idx] = cc5["V"]
                    crh = dma("gpsimd", rhs8[wpar][:].rearrange("p (k t) -> p k t", k=8),
                              d_ngs[fg * 8:(fg + 1) * 8].rearrange("k p t -> p k t"),
                              wait=([("Tn", t_mm5[idx - 2])] if idx >= 2 else ()),
                              sem=f"XG{wpar}")
                    for k8 in range(8):
                        for dcin in range(4):
                            for th in range(2):
                                wmm = []
                                if k8 == 0 and dcin == 0 and th == 0:
                                    wmm = [("V", cc5["V"]), (f"XG{wpar}", crh[f"XG{wpar}"])]
                                if k8 == 0 and fg == 0:
                                    wmm.append(("V", v_ev5b.get((dcin, th), 0)))
                                cm5 = op("tensor", (lambda e, wpar=wpar, dcin=dcin, th=th,
                                                    k8=k8, fg=fg: e.matmul(
                                        pbank[dcin * 2 + th][:],
                                        wbf5[wpar][:, (k8 * 4 + dcin) * 128:(k8 * 4 + dcin + 1) * 128],
                                        rhs8[wpar][:, k8 * 1024 + th * 512:k8 * 1024 + (th + 1) * 512],
                                        start=(fg == 0 and k8 == 0),
                                        stop=(fg == 7 and k8 == 7))),
                                   wait=wmm,
                                   sem=("Tn" if (dcin == 3 and th == 1) else None))
                    t_mm5[idx] = cm5["Tn"]
                for dcin in range(4):
                    dc = 4 * db + dcin
                    xpar = dcin % 2
                    cxt = dma("gpsimd", xtre[xpar][:], d_xtbf[dc],
                              wait=([("V", v_evac5)] if (db + dcin) else list(bar("V"))),
                              sem=f"RB{xpar}")
                    for th in range(2):
                        cev = op("vector", (lambda e, dcin=dcin, th=th, dc=dc, xpar=xpar:
                                e.tensor_tensor(
                                    out=resid[:, dc * 1024 + th * 512:dc * 1024 + (th + 1) * 512],
                                    in0=pbank[dcin * 2 + th][:],
                                    in1=xtre[xpar][:, th * 512:(th + 1) * 512],
                                    op=ALU.add)),
                           wait=[("Tn", t_mm5[db * 8 + 7]),
                                 (f"RB{xpar}", cxt[f"RB{xpar}"])], sem="V")
                        v_ev5b[(dcin, th)] = cev["V"]
                    v_evac5 = cev["V"]
            if DEBUG:
                dma("gpsimd", d_r1.rearrange("k p t -> p k t"),
                    resid[:].rearrange("p (k t) -> p k t", k=KC), wait=bar("V"))

            phase_barrier()
            a_sq5 = {}
            t_row5 = 0
            for dc in range(KC):
                par = dc % 2
                csq = op("scalar", (lambda e, dc=dc, par=par: e.activation(
                    out=ngrh[par][:], in_=resid[:, dc * 1024:(dc + 1) * 1024],
                    func=AF.Square)),
                   wait=([("Tn", a_sq5[dc - 2])] if dc >= 2 else ()), sem="A")
                for th in range(2):
                    op("tensor", (lambda e, dc=dc, th=th: e.matmul(
                        pbank[0 + th][0:1, 0:512], onesb[:, 0:1],
                        resid[:, dc * 1024 + th * 512:dc * 1024 + (th + 1) * 512],
                        start=(dc == 0), stop=(dc == KC - 1))),
                       wait=([("A", csq["A"])] if (dc == 0 and th == 0) else ()), sem="Tn")
                    cm6 = op("tensor", (lambda e, dc=dc, th=th, par=par: e.matmul(
                        pbank[2 + th][0:1, 0:512], onesb[:, 0:1],
                        ngrh[par][:, th * 512:(th + 1) * 512],
                        start=(dc == 0), stop=(dc == KC - 1))),
                       wait=[("A", csq["A"])], sem="Tn")
                a_sq5[dc] = cm6["Tn"]
            t_row5 = cm6["Tn"]
            for th in range(2):
                op("vector", (lambda e, th=th: e.tensor_scalar(
                    out=rowb[0:1, th * 512:(th + 1) * 512], in0=pbank[th][0:1, 0:512],
                    scalar1=1.0 / DM, scalar2=None, op0=ALU.mult)),
                   wait=[("Tn", t_row5)], sem="V")
                op("vector", (lambda e, th=th: e.tensor_scalar(
                    out=rowb2[0:1, th * 512:(th + 1) * 512], in0=pbank[2 + th][0:1, 0:512],
                    scalar1=1.0 / DM, scalar2=None, op0=ALU.mult)), sem="V")
            op("vector", (lambda e: e.tensor_mul(out=rowb3[0:1, :], in0=rowb[0:1, :],
                                                 in1=rowb[0:1, :])), sem="V")
            cvv = op("vector", (lambda e: e.tensor_sub(out=rowb3[0:1, :], in0=rowb2[0:1, :],
                                                       in1=rowb3[0:1, :])), sem="V")
            cl1 = op("scalar", (lambda e: e.activation(out=rowb2[0:1, :], in_=rowb3[0:1, :],
                                                       func=AF.Ln, bias=epsc[0:1, 0:1])),
                     wait=[("V", cvv["V"])], sem="A")
            cl2 = op("scalar", (lambda e: e.activation(out=rowb3[0:1, :], in_=rowb2[0:1, :],
                                                       func=AF.Exp, scale=-0.5)), sem="A")
            cr1 = dma("gpsimd", d_rows[4][:].rearrange("(o f) -> o f", o=1), rowb[0:1, :],
                      wait=[("A", cl2["A"])], sem="SG0")
            cr2 = dma("gpsimd", d_rows[5][:].rearrange("(o f) -> o f", o=1), rowb3[0:1, :],
                      sem="SG0")
            cb1 = dma("gpsimd", mrep[:], d_rows[4][:].rearrange("(o f) -> o f", o=1)
                      .partition_broadcast(128), wait=[("SG0", cr2["SG0"])], sem="SG0")
            cb2 = dma("gpsimd", rrep[:], d_rows[5][:].rearrange("(o f) -> o f", o=1)
                      .partition_broadcast(128), sem="SG0")
            ln1_ready = cb2["SG0"]
            for dc in range(KC):
                par = dc % 2
                ct1 = op("vector", (lambda e, dc=dc: e.tensor_sub(
                    out=tmpf1[:], in0=resid[:, dc * 1024:(dc + 1) * 1024],
                    in1=mrep[:])),
                   wait=([("SG0", ln1_ready)] if dc == 0 else ()), sem="V")
                op("vector", (lambda e, dc=dc: e.tensor_mul(
                    out=x2b[:, dc * 1024:(dc + 1) * 1024], in0=tmpf1[:], in1=rrep[:])),
                   sem="V")
            if DEBUG:
                dma("gpsimd", d_x2s.rearrange("k p t -> p k t"),
                    x2b[:, 0:KC * 1024].rearrange("p (k t) -> p k t", k=KC), wait=bar("V"))

        if PHASES >= 6:
            phase_barrier()
            v_agc = {}
            for kc in range(KC):
                par = kc % 2
                w = [("V", v_agc[kc - 1])] if kc >= 1 else ()
                cag = dma("sync", agstg[:, 0:68], d_wag[kc], wait=w, sem=f"WS{par}")
                cac = op("vector", (lambda e, kc=kc: e.tensor_copy(
                    out=agb[:, kc * 68:(kc + 1) * 68], in_=agstg[:, 0:68])),
                   wait=[(f"WS{par}", cag[f"WS{par}"])], sem="V")
                v_agc[kc] = cac["V"]
            for kc in range(KC):
                for th in range(2):
                    cmA = op("tensor", (lambda e, kc=kc, th=th: e.matmul(
                        pbank[4 + th][0:68, 0:512],
                        agb[:, kc * 68:(kc + 1) * 68],
                        x2b[:, kc * 1024 + th * 512:kc * 1024 + (th + 1) * 512],
                        start=(kc == 0), stop=(kc == KC - 1))),
                       wait=(bar("V") if (kc == 0 and th == 0) else ()), sem="Tn")
            t_ag = cmA["Tn"]
            for th in range(2):
                op("vector", (lambda e, th=th: e.tensor_copy(
                    out=sc4[0:4, th * 512:(th + 1) * 512],
                    in_=pbank[4 + th][64:68, 0:512])),
                   wait=([("Tn", t_ag)] if th == 0 else ()), sem="V")
            cgb = op("vector", (lambda e: e.tensor_scalar(
                out=sc4[0:4, :], in0=sc4[0:4, :], scalar1=gbc[0:4, 0:1], scalar2=None,
                op0=ALU.add)), wait=[("V", CNT["V"])], sem="V")
            t_tr6 = {}
            for q in range(8):
                ctr = op("tensor", (lambda e, q=q: e.transpose(
                    pbank[6][:, q * 4:(q + 1) * 4],
                    sc4[0:4, q * 128:(q + 1) * 128], identf[0:4, 0:4])),
                   wait=([("V", cgb["V"])] if q == 0 else ()), sem="Tn")
                t_tr6[q] = ctr["Tn"]
            cst6 = op("vector", (lambda e: e.tensor_copy(out=scbt[:, 0:32],
                                                         in_=pbank[6][:, 0:32])),
                      wait=[("Tn", t_tr6[7])], sem="V")
            cpr = cst6
            for q in range(8):
                cpr = op("vector", (lambda e, q=q: e.reduce_max(
                    out=mxb[:, q:q + 1], in_=scbt[:, q * 4:(q + 1) * 4],
                    axis=AX.X)), wait=[("V", cpr["V"])], sem="V")
            for q in range(8):
                cpr = op("vector", (lambda e, q=q: e.tensor_scalar(
                    out=ebx[:, q * 4:(q + 1) * 4], in0=scbt[:, q * 4:(q + 1) * 4],
                    scalar1=mxb[:, q:q + 1], scalar2=None, op0=ALU.is_ge)),
                    wait=[("V", cpr["V"])], sem="V")
            for q in range(8):
                cpr = op("vector", (lambda e, q=q: e.tensor_tensor_scan(
                    out=cub[:, q * 4:(q + 1) * 4], data0=ebx[:, q * 4:(q + 1) * 4],
                    data1=zcol[:, 0:4], initial=0.0, op0=ALU.add, op1=ALU.add)),
                    wait=[("V", cpr["V"])], sem="V")
            cpr = op("vector", (lambda e: e.tensor_scalar(
                out=cub[:, 0:32], in0=cub[:, 0:32], scalar1=1.0, scalar2=None,
                op0=ALU.is_equal)), wait=[("V", cpr["V"])], sem="V")
            cfb = op("vector", (lambda e: e.tensor_mul(out=fbf[:, 0:32], in0=ebx[:, 0:32],
                                                       in1=cub[:, 0:32])),
                     wait=[("V", cpr["V"])], sem="V")
            for q in range(8):
                ctr = op("tensor", (lambda e, q=q: e.transpose(
                    ptb[1][0:4, q * 128:(q + 1) * 128], fbf[:, q * 4:(q + 1) * 4],
                    identbf[:])),
                   wait=([("V", cfb["V"])] if q == 0 else ()), sem="Tn")
            cfs = op("vector", (lambda e: e.tensor_copy(out=fsb[0:4, :],
                                                        in_=ptb[1][0:4, 0:1024])),
                     wait=[("Tn", ctr["Tn"])], sem="V")
            for th in range(2):
                cmk = op("tensor", (lambda e, th=th: e.matmul(
                    pbank[2 + th][0:64, 0:512], ltm[0:4, 0:64],
                    fsb[0:4, th * 512:(th + 1) * 512], start=True, stop=True)),
                   wait=([("V", cfs["V"])] if th == 0 else ()), sem="Tn")
            for th in range(2):
                op("vector", (lambda e, th=th: e.tensor_copy(
                    out=msb[0:64, th * 512:(th + 1) * 512],
                    in_=pbank[2 + th][0:64, 0:512])),
                   wait=[("Tn", cmk["Tn"])], sem="V")
            for th in range(2):
                cum6 = op("vector", (lambda e, th=th: e.tensor_tensor(
                    out=umb[0:64, th * 512:(th + 1) * 512],
                    in0=pbank[4 + th][0:64, 0:512],
                    in1=msb[0:64, th * 512:(th + 1) * 512], op=ALU.mult)), sem="V")
            v_um = cum6["V"]
            if DEBUG:
                dma("gpsimd", d_df1, scbt[:, 0:32], wait=bar("V"))
                dma("gpsimd", d_df2, fbf[:, 0:32], wait=bar("V"))
                dma("gpsimd", d_df3, msb[0:64, :], wait=bar("V"))
                dma("gpsimd", d_df4, umb[0:64, :], wait=bar("V"))
                dma("gpsimd", d_df5, sc4[0:4, :], wait=bar("V"))
            v_wc6 = {}
            t_mm6 = {}
            t_lora6 = {}
            v_ev6 = 0
            v_ev6b = {}
            v_bl6 = {}
            for db in range(8):
                bpar = db % 2
                cbl = dma("sync", blstg2[bpar][:].rearrange("j (d f) -> j d f", d=4),
                          d_wbl[4 * db:4 * db + 4].rearrange("d j f -> j d f"),
                          wait=([("V", v_bl6[db - 2])] if db >= 2 else ()), sem=f"RB{bpar}")
                cblc = op("vector", (lambda e, bpar=bpar: e.tensor_copy(
                    out=blb[bpar][:], in_=blstg2[bpar][:])),
                    wait=[(f"RB{bpar}", cbl[f"RB{bpar}"])] +
                         ([("Tn", t_lora6.get(db - 2, 0))] if db >= 2 else []), sem="V")
                v_bl6[db] = cblc["V"]
                for kg in range(8):
                    idx = db * 8 + kg
                    wpar = kg % 2
                    w = [("V", v_wc6[idx - 1])] if idx >= 1 else ()
                    cw6 = dma("sync", wslff[:].rearrange("p (k d f) -> p k d f", k=4, d=4),
                              d_wffn[db, kg * 4:(kg + 1) * 4].rearrange("k d p f -> p k d f"),
                              wait=w, sem="WS0")
                    cc6 = op("vector", (lambda e, wpar=wpar: e.tensor_copy(
                        out=wbff[wpar][:], in_=wslff[:])),
                       wait=[("WS0", cw6["WS0"])] +
                            ([("Tn", t_mm6.get(idx - 2, 0))] if idx >= 2 else []), sem="V")
                    v_wc6[idx] = cc6["V"]
                    for k4 in range(4):
                        kc = kg * 4 + k4
                        for dcin in range(4):
                            for th in range(2):
                                wmm = []
                                if k4 == 0 and dcin == 0 and th == 0:
                                    wmm = [("V", cc6["V"])]
                                    if kg == 0:
                                        wmm.append(("V", v_um))
                                if k4 == 0 and kg == 0:
                                    wmm.append(("V", v_ev6b.get((dcin, th), 0)))
                                cm6b = op("tensor", (lambda e, wpar=wpar, dcin=dcin, th=th,
                                                     kc=kc, k4=k4: e.matmul(
                                        pbank[dcin * 2 + th][:],
                                        wbff[wpar][:, (k4 * 4 + dcin) * 128:(k4 * 4 + dcin + 1) * 128],
                                        x2b[:, kc * 1024 + th * 512:kc * 1024 + (th + 1) * 512],
                                        start=(kc == 0), stop=False)),
                                   wait=wmm, sem=("Tn" if (dcin == 3 and th == 1) else None))
                    t_mm6[idx] = cm6b["Tn"]
                for dcin in range(4):
                    for th in range(2):
                        cm6c = op("tensor", (lambda e, bpar=bpar, dcin=dcin, th=th: e.matmul(
                            pbank[dcin * 2 + th][:],
                            blb[bpar][0:64, dcin * 128:(dcin + 1) * 128],
                            umb[0:64, th * 512:(th + 1) * 512],
                            start=False, stop=True)),
                           wait=([("V", v_bl6[db])] if (dcin == 0 and th == 0) else ()),
                           sem=("Tn" if (dcin == 3 and th == 1) else None))
                t_lora = cm6c["Tn"]
                t_lora6[db] = t_lora
                for dcin in range(4):
                    dc = 4 * db + dcin
                    for th in range(2):
                        cev = op("vector", (lambda e, dcin=dcin, th=th, dc=dc:
                                e.scalar_tensor_tensor(
                                    out=resid[:, dc * 1024 + th * 512:dc * 1024 + (th + 1) * 512],
                                    in0=pbank[dcin * 2 + th][:],
                                    scalar=bcolc[:, dc:dc + 1],
                                    in1=x2b[:, dc * 1024 + th * 512:dc * 1024 + (th + 1) * 512],
                                    op0=ALU.add, op1=ALU.add)),
                           wait=([("Tn", t_lora)] if (dcin == 0 and th == 0) else ()),
                           sem="V")
                        v_ev6b[(dcin, th)] = cev["V"]
                    v_ev6 = cev["V"]

            phase_barrier()
            a_sq6 = {}
            for dc in range(KC):
                par = dc % 2
                csq = op("scalar", (lambda e, dc=dc, par=par: e.activation(
                    out=ngrh[par][:], in_=resid[:, dc * 1024:(dc + 1) * 1024],
                    func=AF.Square)),
                   wait=([("Tn", a_sq6[dc - 2])] if dc >= 2 else ()), sem="A")
                for th in range(2):
                    op("tensor", (lambda e, dc=dc, th=th: e.matmul(
                        pbank[0 + th][0:1, 0:512], onesb[:, 0:1],
                        resid[:, dc * 1024 + th * 512:dc * 1024 + (th + 1) * 512],
                        start=(dc == 0), stop=(dc == KC - 1))),
                       wait=([("A", csq["A"])] if (dc == 0 and th == 0) else ()), sem="Tn")
                    cm7 = op("tensor", (lambda e, dc=dc, th=th, par=par: e.matmul(
                        pbank[2 + th][0:1, 0:512], onesb[:, 0:1],
                        ngrh[par][:, th * 512:(th + 1) * 512],
                        start=(dc == 0), stop=(dc == KC - 1))),
                       wait=[("A", csq["A"])], sem="Tn")
                a_sq6[dc] = cm7["Tn"]
            t_row6 = cm7["Tn"]
            for th in range(2):
                op("vector", (lambda e, th=th: e.tensor_scalar(
                    out=rowb[0:1, th * 512:(th + 1) * 512], in0=pbank[th][0:1, 0:512],
                    scalar1=1.0 / DM, scalar2=None, op0=ALU.mult)),
                   wait=[("Tn", t_row6)], sem="V")
                op("vector", (lambda e, th=th: e.tensor_scalar(
                    out=rowb2[0:1, th * 512:(th + 1) * 512], in0=pbank[2 + th][0:1, 0:512],
                    scalar1=1.0 / DM, scalar2=None, op0=ALU.mult)), sem="V")
            op("vector", (lambda e: e.tensor_mul(out=rowb3[0:1, :], in0=rowb[0:1, :],
                                                 in1=rowb[0:1, :])), sem="V")
            cvv = op("vector", (lambda e: e.tensor_sub(out=rowb3[0:1, :], in0=rowb2[0:1, :],
                                                       in1=rowb3[0:1, :])), sem="V")
            cl1 = op("scalar", (lambda e: e.activation(out=rowb2[0:1, :], in_=rowb3[0:1, :],
                                                       func=AF.Ln, bias=epsc[0:1, 0:1])),
                     wait=[("V", cvv["V"])], sem="A")
            cl2 = op("scalar", (lambda e: e.activation(out=rowb3[0:1, :], in_=rowb2[0:1, :],
                                                       func=AF.Exp, scale=-0.5)), sem="A")
            cr1 = dma("gpsimd", d_rows[6][:].rearrange("(o f) -> o f", o=1), rowb[0:1, :],
                      wait=[("A", cl2["A"])], sem="SG0")
            cr2 = dma("gpsimd", d_rows[7][:].rearrange("(o f) -> o f", o=1), rowb3[0:1, :],
                      sem="SG0")
            cb1 = dma("gpsimd", mrep[:], d_rows[6][:].rearrange("(o f) -> o f", o=1)
                      .partition_broadcast(128), wait=[("SG0", cr2["SG0"])], sem="SG0")
            cb2 = dma("gpsimd", rrep[:], d_rows[7][:].rearrange("(o f) -> o f", o=1)
                      .partition_broadcast(128), sem="SG0")
            ln2_ready = cb2["SG0"]
            g_out = {0: 0, 1: 0}
            for dc in range(KC):
                par = dc % 2
                ct1 = op("vector", (lambda e, dc=dc: e.tensor_sub(
                    out=tmpf1[:], in0=resid[:, dc * 1024:(dc + 1) * 1024],
                    in1=mrep[:])),
                   wait=([("SG0", ln2_ready)] if dc == 0 else []) +
                        [(f"XS{par}", g_out[par])], sem="V")
                cou = op("vector", (lambda e, dc=dc, par=par: e.tensor_mul(
                    out=outt[par][:], in0=tmpf1[:], in1=rrep[:])), sem="V")
                cso = dma("gpsimd", d_out[dc], outt[par][:], wait=[("V", cou["V"])],
                          sem=f"XS{par}")
                g_out[par] = cso[f"XS{par}"]

        phase_barrier()
        sched.append(("gpsimd", None, [(s, v) for (s, v) in allbar() if v > 0], []))

        block = ent(nc.Block())
        for eng_name in ("sync", "gpsimd", "vector", "scalar", "tensor"):
            sec = getattr(block, eng_name)

            def emit(eng, _n=eng_name):
                last = {}
                for (en, builder, waits, incs) in sched:
                    if en != _n:
                        continue
                    for (s, v) in waits:
                        if v > 0 and last.get(s, -1) < v:
                            eng.wait_ge(SEM[s], v)
                            last[s] = v
                    if builder is None:
                        continue
                    instr = builder(eng)
                    for (s, k) in incs:
                        instr.then_inc(SEM[s], k)
            sec(emit)
    return nc



def _prep_host(inputs):
    f32 = np.float32
    X = np.asarray(inputs["L0_attn_hidden_states"], dtype=f32)
    Win = np.asarray(inputs["L0_attn_in_proj_w"], dtype=f32)
    conv_w = np.asarray(inputs["L0_attn_conv_w"], dtype=f32)
    conv_b = np.asarray(inputs["L0_attn_conv_b"], dtype=f32)
    dt_bias = np.asarray(inputs["L0_attn_dt_bias"], dtype=f32)
    D = np.asarray(inputs["L0_attn_D"], dtype=f32)
    norm_w = np.asarray(inputs["L0_attn_norm_w"], dtype=f32)
    Wout = np.asarray(inputs["L0_attn_out_proj_w"], dtype=f32)
    Wf = np.asarray(inputs["L0_ffn_W"], dtype=f32)
    bf_ = np.asarray(inputs["L0_ffn_b"], dtype=f32)
    Af = np.asarray(inputs["L0_ffn_A"], dtype=f32)
    Bf = np.asarray(inputs["L0_ffn_B"], dtype=f32)
    gw = np.asarray(inputs["L0_ffn_gate_w"], dtype=f32)
    gbv = np.asarray(inputs["L0_ffn_gate_b"], dtype=f32)

    sh = {}
    sh["wxbc"] = np.ascontiguousarray(
        Win[INTER:, :].reshape(NXBC, 128, KC, 128).transpose(0, 3, 2, 1))
    sh["wgate"] = np.ascontiguousarray(
        Win[:INTER, :].reshape(NFI, 128, KC, 128).transpose(0, 3, 2, 1))
    sh["wout"] = np.ascontiguousarray(
        Wout.reshape(8, 4, 128, NFI, 128).transpose(0, 3, 1, 4, 2))
    sh["wffn"] = np.ascontiguousarray(
        Wf.reshape(8, 4, 128, KC, 128).transpose(0, 3, 1, 4, 2))
    ag = np.concatenate([Af, gw], axis=0)
    sh["wag"] = np.ascontiguousarray(ag.T.reshape(KC, 128, 68))
    sh["wbl"] = np.ascontiguousarray(Bf.reshape(KC, 128, 64).transpose(0, 2, 1))
    sh["dr_identbf"] = np.eye(128, dtype=BF)
    sh["dr_identf"] = np.eye(128, dtype=f32)
    ii = np.arange(128)
    sh["dr_maskm"] = np.where(ii[None, :] >= ii[:, None], 0.0, -1e38).astype(f32)
    lt = np.zeros((4, 64), dtype=BF)
    for i, r in enumerate(RANKS):
        lt[i, :r] = 1.0
    sh["dr_ltm"] = lt
    sh["dr_onesb"] = np.ones((128, 1), dtype=BF)
    sh["dr_acol"] = (-np.arange(1, 129, dtype=f32))[:, None]
    sh["dtb"] = np.ascontiguousarray(dt_bias[:, None])
    sh["dr_drep"] = np.ascontiguousarray(np.repeat(D.reshape(NFI, 2), 64, axis=1).reshape(NFI, 128).T)
    sh["nw"] = np.ascontiguousarray(norm_w.reshape(NFI, 128).T)
    sh["bcol"] = np.ascontiguousarray(bf_.reshape(KC, 128).T)
    sh["gb"] = np.ascontiguousarray(gbv[:, None])
    sh["dr_cw"] = np.ascontiguousarray(
        conv_w[:, 0, :].reshape(80, 128, 4).transpose(1, 0, 2).reshape(128, 320))
    sh["dr_cb"] = np.ascontiguousarray(conv_b.reshape(80, 128).T)

    in_maps = []
    for c in range(NCORES):
        b, r = divmod(c, 4)
        xh = np.zeros((TH, DM), dtype=f32)
        t0 = r * T
        if r == 0:
            xh[3:] = X[b, t0:t0 + T]
        else:
            xh[:] = X[b, t0 - 3:t0 + T]
        m = dict(sh)
        m["xh"] = np.ascontiguousarray(xh.T.reshape(KC, 128, TH))
        rm = np.zeros((128, 4), dtype=f32)
        rm[:, :r] = 1.0
        m["dr_ringm"] = rm
        m["dr_ringm1"] = np.ascontiguousarray((1.0 - rm).astype(f32))
        mc = np.zeros((4, 1), dtype=f32)
        mc[:r, 0] = 1.0
        m["dr_ringmc"] = mc
        m["dr_ringmc1"] = np.ascontiguousarray((1.0 - mc).astype(f32))
        m["dr_epsc"] = np.full((128, 1), 1e-5, dtype=f32)
        m["dr_onesrow"] = np.ones((1, 128), dtype=f32)
        in_maps.append(m)
    return in_maps


def kernel(**inputs):
    in_maps = _prep_host(inputs)
    nc = build_program()
    res = run_bass_kernel_spmd(nc, in_maps, core_ids=list(range(NCORES)))
    out = np.zeros((2, 4096, 4096), dtype=np.float32)
    for c in range(NCORES):
        b, r = divmod(c, 4)
        o = np.asarray(res.results[c]["out"], dtype=np.float32).reshape(4096, T)
        out[b, r * T:(r + 1) * T, :] = o.T
    return out


# version 11 (final)
# speedup vs baseline: 1.0029x; 1.0029x over previous
import numpy as np
import ml_dtypes
from contextlib import ExitStack
import concourse.bass as bass
import concourse.mybir as mybir
from concourse.bass_utils import run_bass_kernel_spmd

F32 = mybir.dt.float32
BF16 = mybir.dt.bfloat16
AF = mybir.ActivationFunctionType
ALU = mybir.AluOpType
AX = mybir.AxisListType
BF = ml_dtypes.bfloat16

NCORES = 8
T = 1024
TH = T + 3
NCH = 8
CHL = 128
DM = 4096
KC = 32
INTER = 8192
NFI = 64
NXBC = 81
NH = 128
HD = 64
SS = 128
NG = 8
GROUP = 1024
EPS = 1e-5
LNEPS = 1e-5
RANKS = (8, 16, 32, 64)
TS = (347, 340, 340)
TS_OFF = (0, 347, 687)

PHASES = 99
DEBUG = False


def build_program():
    nc = bass.Bass("TRN2", target_bir_lowering=False)
    scr = dict(kind="ExternalOutput") if DEBUG else {}

    d_xh = nc.dram_tensor("xh", [KC, 128, TH], F32, kind="ExternalInput").ap()
    d_wxbc = nc.dram_tensor("wxbc", [NXBC, 128, KC, 128], F32, kind="ExternalInput").ap()
    d_wgate = nc.dram_tensor("wgate", [NFI, 128, KC, 128], F32, kind="ExternalInput").ap()
    d_wout = nc.dram_tensor("wout", [8, NFI, 4, 128, 128], F32, kind="ExternalInput").ap()
    d_wffn = nc.dram_tensor("wffn", [8, KC, 4, 128, 128], F32, kind="ExternalInput").ap()
    d_wag = nc.dram_tensor("wag", [KC, 128, 68], F32, kind="ExternalInput").ap()
    d_wbl = nc.dram_tensor("wbl", [KC, 64, 128], F32, kind="ExternalInput").ap()
    d_identbf = nc.dram_tensor("dr_identbf", [128, 128], BF16, kind="ExternalInput").ap()
    d_identf = nc.dram_tensor("dr_identf", [128, 128], F32, kind="ExternalInput").ap()
    d_maskm = nc.dram_tensor("dr_maskm", [128, 128], F32, kind="ExternalInput").ap()
    d_ltm = nc.dram_tensor("dr_ltm", [4, 64], BF16, kind="ExternalInput").ap()
    d_onesb = nc.dram_tensor("dr_onesb", [128, 1], BF16, kind="ExternalInput").ap()
    d_acol = nc.dram_tensor("dr_acol", [128, 1], F32, kind="ExternalInput").ap()
    d_dtb = nc.dram_tensor("dtb", [128, 1], F32, kind="ExternalInput").ap()
    d_drep = nc.dram_tensor("dr_drep", [128, NFI], F32, kind="ExternalInput").ap()
    d_nw = nc.dram_tensor("nw", [128, NFI], F32, kind="ExternalInput").ap()
    d_bcol = nc.dram_tensor("bcol", [128, KC], F32, kind="ExternalInput").ap()
    d_gb = nc.dram_tensor("gb", [4, 1], F32, kind="ExternalInput").ap()
    d_cw = nc.dram_tensor("dr_cw", [128, 320], F32, kind="ExternalInput").ap()
    d_cb = nc.dram_tensor("dr_cb", [128, 80], F32, kind="ExternalInput").ap()
    d_ringm = nc.dram_tensor("dr_ringm", [128, 4], F32, kind="ExternalInput").ap()
    d_ringm1 = nc.dram_tensor("dr_ringm1", [128, 4], F32, kind="ExternalInput").ap()
    d_ringmc = nc.dram_tensor("dr_ringmc", [4, 1], F32, kind="ExternalInput").ap()
    d_ringmc1 = nc.dram_tensor("dr_ringmc1", [4, 1], F32, kind="ExternalInput").ap()
    d_epsc = nc.dram_tensor("dr_epsc", [128, 1], F32, kind="ExternalInput").ap()
    d_onesrow = nc.dram_tensor("dr_onesrow", [1, 128], F32, kind="ExternalInput").ap()

    d_out = nc.dram_tensor("out", [KC, 128, T], F32, kind="ExternalOutput").ap()

    d_xtbf = nc.dram_tensor("xtbf_s", [KC, 128, T], BF16, **scr).ap()
    d_xbcs = nc.dram_tensor("xbc_s", [80, 128, T], BF16, **scr).ap()
    d_dts = nc.dram_tensor("dt_s", [128, T], F32, **scr).ap()
    d_csf = nc.dram_tensor("csflat_s", [NCH, 16384], F32, **scr).ap()
    d_ecs = nc.dram_tensor("expcs_s", [NCH, 16384], F32, **scr).ap()
    d_cdr = nc.dram_tensor("cdrow_s", [NCH, 128], F32, **scr).ap()
    d_yds = nc.dram_tensor("yd_s", [NCH, 128, 8192], BF16, **scr).ap()
    d_sts = nc.dram_tensor("st_s", [NCH, 128, 8192], BF16, **scr).ap()
    d_gs = nc.dram_tensor("g_s", [NFI, 128, T], BF16, **scr).ap()
    d_ngs = nc.dram_tensor("ng_s", [NFI, 128, T], BF16, **scr).ap()
    d_rows = nc.dram_tensor("rows_s", [8, 1024], F32, **scr).ap()
    d_gnr = nc.dram_tensor("gnrow_s", [4, NG, 256], F32, **scr).ap()
    d_ccsi = nc.dram_tensor("ccs_in", [128, 8193], F32).ap()
    d_ccso = nc.dram_tensor("ccs_out", [4 * 128, 8193], F32).ap()
    d_dbg_sloc = nc.dram_tensor("dbg_sloc", [128, 8192], F32, **scr).ap()
    d_dbg_state = nc.dram_tensor("dbg_state", [128, 8192], F32, **scr).ap()
    d_ys = nc.dram_tensor("y_s", [NCH, 128, 8192], BF16, **scr).ap()
    d_r1 = nc.dram_tensor("r1_s", [KC, 128, T], BF16, **scr).ap()
    d_x2s = nc.dram_tensor("x2_s", [KC, 128, T], BF16, **scr).ap()
    d_df1 = nc.dram_tensor("dbgf_scbt", [128, 32], F32, **scr).ap()
    d_df2 = nc.dram_tensor("dbgf_fbf", [128, 32], BF16, **scr).ap()
    d_df3 = nc.dram_tensor("dbgf_msb", [64, 1024], BF16, **scr).ap()
    d_df4 = nc.dram_tensor("dbgf_umb", [64, 1024], BF16, **scr).ap()
    d_df5 = nc.dram_tensor("dbgf_sc4", [4, 1024], F32, **scr).ap()

    st = ExitStack()
    with st:
        ent = st.enter_context
        identbf = ent(nc.sbuf_tensor("identbf", [128, 128], BF16))
        identf = ent(nc.sbuf_tensor("identf", [128, 128], F32))
        maskm = ent(nc.sbuf_tensor("maskm", [128, 128], F32))
        ltm = ent(nc.sbuf_tensor("ltm", [4, 64], BF16))
        onesb = ent(nc.sbuf_tensor("onesb", [128, 1], BF16))
        acol = ent(nc.sbuf_tensor("acol", [128, 1], F32))
        dtbc = ent(nc.sbuf_tensor("dtbc", [128, 1], F32))
        drep = ent(nc.sbuf_tensor("drep", [128, NFI], F32))
        nwc = ent(nc.sbuf_tensor("nwc", [128, NFI], F32))
        bcolc = ent(nc.sbuf_tensor("bcolc", [128, KC], F32))
        gbc = ent(nc.sbuf_tensor("gbc", [4, 1], F32))
        cwc = ent(nc.sbuf_tensor("cwc", [128, 320], F32))
        cbc = ent(nc.sbuf_tensor("cbc", [128, 80], F32))
        ringm = ent(nc.sbuf_tensor("ringm", [128, 4], F32))
        ringm1 = ent(nc.sbuf_tensor("ringm1", [128, 4], F32))
        ringmc = ent(nc.sbuf_tensor("ringmc", [4, 1], F32))
        ringmc1 = ent(nc.sbuf_tensor("ringmc1", [4, 1], F32))
        epsc = ent(nc.sbuf_tensor("epsc", [128, 1], F32))
        onesrow = ent(nc.sbuf_tensor("onesrow", [1, 128], F32))
        zcol = ent(nc.sbuf_tensor("zcol", [128, 128], F32))
        dtq = ent(nc.sbuf_tensor("dtq", [128, T], F32))
        datq = ent(nc.sbuf_tensor("datq", [128, T], F32))
        logd = ent(nc.sbuf_tensor("logd", [128, 1], F32))
        csb = ent(nc.sbuf_tensor("csb", [128, 256], F32))
        ecsb = ent(nc.sbuf_tensor("ecsb", [128, 256], F32))
        dcy = ent(nc.sbuf_tensor("dcy", [128, 256], F32))
        csT = ent(nc.sbuf_tensor("csT", [128, 256], F32))
        dcyT = ent(nc.sbuf_tensor("dcyT", [128, 256], F32))
        dtT = ent(nc.sbuf_tensor("dtT", [128, 256], F32))
        cdrp = ent(nc.sbuf_tensor("cdrp", [128, 256], F32))
        btb = ent(nc.sbuf_tensor("btb", [128, 1024], BF16))
        erow = ent(nc.sbuf_tensor("erow", [4, 128], F32))
        expdr = ent(nc.sbuf_tensor("expdr", [4, 128], F32))
        scbt = ent(nc.sbuf_tensor("scbt", [128, 32], F32))
        mxb = ent(nc.sbuf_tensor("mxb", [128, 8], F32))
        ebx = ent(nc.sbuf_tensor("ebx", [128, 32], F32))
        cub = ent(nc.sbuf_tensor("cub", [128, 32], F32))
        fbf = ent(nc.sbuf_tensor("fbf", [128, 32], BF16))
        fsb = ent(nc.sbuf_tensor("fsb", [4, 1024], BF16))
        sc4 = ent(nc.sbuf_tensor("sc4", [4, 1024], F32))
        msb = ent(nc.sbuf_tensor("msb", [64, 1024], BF16))
        umb = ent(nc.sbuf_tensor("umb", [64, 1024], BF16))
        agstg = ent(nc.sbuf_tensor("agstg", [128, 68], F32))
        agb = ent(nc.sbuf_tensor("agb", [128, KC * 68], BF16))
        blb = [ent(nc.sbuf_tensor(f"blb{i}", [64, 512], BF16)) for i in range(2)]
        blstg2 = [ent(nc.sbuf_tensor(f"blstg{i}", [64, 512], F32)) for i in range(2)]

        MSZ = 76288
        M = ent(nc.sbuf_tensor("mega", [128, MSZ], BF16))

        def mb(off_kb, sz_kb, dtype=BF16):
            e0 = int(off_kb * 512)
            n = int(sz_kb * 512)
            assert e0 + n <= MSZ, f"arena overflow {off_kb}+{sz_kb}"
            ap = M[:, e0:e0 + n]
            if dtype == F32:
                ap = ap.bitcast(F32)
            return ap

        xtbf = mb(0, 64.25)
        wsl = [mb(64.25 + 16 * i, 16, F32) for i in range(2)]
        wbfs = [mb(96.25 + 8 * i, 8) for i in range(2)]
        projf = [mb(112.25 + 4.25 * i, 4.25, F32) for i in range(2)]
        convt = [mb(120.75 + 4 * i, 4, F32) for i in range(2)]
        xbcb = [mb(128.75 + 2 * i, 2) for i in range(2)]
        gtile = [mb(112.25 + 2 * i, 2) for i in range(2)]
        ctl = mb(104, 2)
        rbc = [mb(0 + 8 * i, 8, F32) for i in range(2)]
        p1b = [mb(16 + 4 * i, 4) for i in range(2)]
        lmt = [mb(24 + 4 * i, 4) for i in range(2)]
        scp = [mb(32 + 4 * i, 4) for i in range(2)]
        xgw = [mb(46 + 5 * i, 5) for i in range(2)]
        xdt = [mb(56 + 4 * i, 4) for i in range(2)]
        xdtp = [mb(64 + 4 * i, 4) for i in range(2)]
        ydb = [mb(72 + 2 * i, 2) for i in range(2)]
        stb2 = [mb(76 + 1 * i, 1) for i in range(2)]
        scr2 = mb(78, 8, F32)
        sloc = mb(116.5, 32, F32)
        ybuf = mb(0, 32)
        pvb = mb(32, 16)
        stre = mb(48, 16)
        ydre = mb(64, 16)
        ebc = [mb(80 + 8 * i, 8, F32) for i in range(2)]
        cpb = [mb(96 + 4 * i, 4) for i in range(2)]
        ygg = mb(106, 4)
        gtall = mb(64, 32)
        gt2 = [mb(110 + 0.5 * i, 0.5) for i in range(2)]
        sqt = [mb(111 + 0.5 * i, 0.5) for i in range(2)]
        ngt = [mb(112 + 0.5 * i, 0.5) for i in range(2)]
        rsr = mb(113, 1, F32)
        x2b = mb(0, 64)
        resid = mb(64, 64)
        wsm = [mb(128 + 2 * i, 2, F32) for i in range(2)]
        wsmb = [mb(132 + 1 * i, 1) for i in range(2)]
        ngrh = [mb(134 + 2 * i, 2) for i in range(2)]
        xtre = [mb(138 + 2 * i, 2) for i in range(2)]
        outt = [mb(64 + 4 * i, 4, F32) for i in range(2)]

        outt = [mb(0 + 4 * i, 4, F32) for i in range(2)]
        rowg = mb(114, 1, F32)[0:1, 0:256]
        rowg2 = mb(115, 1, F32)[0:1, 0:256]
        wsl5f = mb(0, 16, F32)
        wbf5 = [mb(16 + 8 * i, 8) for i in range(2)]
        rhs8 = [mb(32 + 16 * i, 16) for i in range(2)]
        wslff = mb(134, 8, F32)
        wbff = [mb(142, 4), mb(128, 4)]
        rowb = mb(48, 4, F32)[0:1, 0:1024]
        rowb2 = mb(52, 4, F32)[0:1, 0:1024]
        rowb3 = mb(56, 4, F32)[0:1, 0:1024]
        mrep = mb(138, 4, F32)
        rrep = mb(142, 4, F32)
        tmpf1 = mb(128, 4, F32)

        pbank_t = [ent(nc.psum_tensor(f"pb{i}", [128, 512], F32)) for i in range(8)]
        pbank = [pbank_t[i][:] for i in range(8)]
        ptb = [pbank_t[6 + i][:].bitcast(BF16) for i in range(2)]

        sem_names = ["W", "G", "G2", "V", "A", "Tn", "CC", "WS0", "WS1", "XS0", "XS1",
                     "XG0", "XG1", "RB0", "RB1", "YS0", "YS1", "SS0", "SS1", "SG0", "SG1",
                     "YP0", "YP1", "SP0", "SP1"]
        SEM = {n: ent(nc.semaphore(n)) for n in sem_names}
        CNT = {n: 0 for n in sem_names}
        sched = []

        def op(eng, builder, wait=(), sem=None, k=1):
            for (s, v) in wait:
                assert v <= CNT[s], f"wait {s}>={v} but only {CNT[s]} scheduled"
            incs = []
            if sem is not None:
                CNT[sem] += k
                incs = [(sem, k)]
            sched.append((eng, builder, [(s, v) for (s, v) in wait if v > 0], incs))
            return {s: CNT[s] for s in sem_names}

        def bar(*sems):
            return [(s, CNT[s]) for s in sems]

        def allbar():
            return bar(*sem_names)

        def dma(queue, out, in_, wait=(), sem=None, slow_ok=False):
            if sem is None:
                sem = "W" if queue == "sync" else ("G" if queue == "gpsimd" else "G2")
            return op(queue, (lambda e, o=out, i=in_, s=slow_ok: e.dma_start(
                out=o, in_=i, allow_slow_non_contiguous=s)), wait, sem, 16)

        def phase_barrier():
            b = allbar()
            for e in ("sync", "gpsimd", "vector", "scalar", "tensor"):
                sched.append((e, None, [(s, v) for (s, v) in b if v > 0], []))

        for dst, src in ((identbf[:], d_identbf), (identf[:], d_identf), (maskm[:], d_maskm),
                         (ltm[:], d_ltm), (onesb[:], d_onesb), (acol[:], d_acol),
                         (dtbc[:], d_dtb), (drep[:], d_drep), (nwc[:], d_nw),
                         (bcolc[:], d_bcol), (gbc[:], d_gb), (cwc[:], d_cw), (cbc[:], d_cb),
                         (ringm[:], d_ringm), (ringm1[:], d_ringm1),
                         (ringmc[:], d_ringmc), (ringmc1[:], d_ringmc1),
                         (epsc[:], d_epsc), (onesrow[:], d_onesrow)):
            dma("gpsimd", dst, src)
        op("vector", lambda e: e.memset(zcol[:], 0.0), sem="V")
        op("vector", lambda e: e.memset(logd[:], 0.0), sem="V")

        xcast_done = []
        for kc in range(KC):
            par = f"SG{kc % 2}"
            w = [("V", xcast_done[kc - 2])] if kc >= 2 else (bar("G") if kc == 0 else ())
            c = dma("gpsimd", projf[kc % 2][:, 0:TH], d_xh[kc], wait=w, sem=par)
            c = op("vector",
                   (lambda e, kc=kc: e.tensor_copy(out=xtbf[:, kc * TH:(kc + 1) * TH],
                                                   in_=projf[kc % 2][:, 0:TH])),
                   wait=[(par, c[par])], sem="V")
            xcast_done.append(c["V"])
            dma("sync", d_xtbf[kc], xtbf[:, kc * TH + 3:(kc + 1) * TH], wait=bar("V"))

        if PHASES >= 1:
            v_cast = {}
            t_mm = {}
            a_evac = {}
            v_conv = {}
            g_store = {}

            def p1_fetch(fc):
                b = fc % 2
                ws_ = f"WS{b}"
                w = [("V", v_cast[fc - 2])] if fc >= 2 else ()
                c0 = dma("sync", wsl[b][:], d_wxbc[fc].rearrange("p kc f -> p (kc f)"),
                         wait=w, sem=ws_)
                wv = [(ws_, c0[ws_])]
                if fc >= 2:
                    wv.append(("Tn", t_mm[fc - 2]))
                c = op("vector", (lambda e, b=b: e.tensor_copy(out=wbfs[b][:], in_=wsl[b][:])),
                       wait=wv, sem="V")
                v_cast[fc] = c["V"]

            p1_fetch(0)
            p1_fetch(1)
            for fc in range(NXBC):
                b = fc % 2
                wt = [("V", v_cast[fc])]
                if fc >= 2:
                    wt.append(("A", a_evac[fc - 2]))
                for kc in range(KC):
                    for tsi in range(3):
                        def mmb(e, b=b, kc=kc, tsi=tsi):
                            return e.matmul(
                                pbank[3 * b + tsi][:, 0:TS[tsi]],
                                wbfs[b][:, kc * 128:(kc + 1) * 128],
                                xtbf[:, kc * TH + TS_OFF[tsi]:kc * TH + TS_OFF[tsi] + TS[tsi]],
                                start=(kc == 0), stop=(kc == KC - 1))
                        c = op("tensor", mmb, wait=(wt if (kc == 0 and tsi == 0) else ()),
                               sem=("Tn" if (kc == KC - 1 and tsi == 2) else None))
                t_mm[fc] = c["Tn"]
                if fc + 2 < NXBC:
                    p1_fetch(fc + 2)
                we = [("Tn", t_mm[fc])]
                if fc >= 2:
                    we.append(("V", v_conv.get(fc - 2, 0)))
                else:
                    we.append(("V", v_cast[1]))
                for tsi in range(3):
                    c = op("scalar",
                           (lambda e, b=b, tsi=tsi: e.copy(
                               out=projf[b][:, TS_OFF[tsi]:TS_OFF[tsi] + TS[tsi]],
                               in_=pbank[3 * b + tsi][:, 0:TS[tsi]])),
                           wait=(we if tsi == 0 else ()), sem="A")
                a_evac[fc] = c["A"]
                if fc < 80:
                    wv2 = [("A", a_evac[fc])]
                    for k in range(4):
                        if k == 0:
                            c = op("vector",
                                   (lambda e, b=b, fc=fc: e.tensor_scalar(
                                       out=convt[b][:], in0=projf[b][:, 0:T],
                                       scalar1=cwc[:, fc * 4:fc * 4 + 1], scalar2=None,
                                       op0=ALU.mult)),
                                   wait=wv2, sem="V")
                        else:
                            c = op("vector",
                                   (lambda e, b=b, fc=fc, k=k: e.scalar_tensor_tensor(
                                       out=convt[b][:], in0=projf[b][:, k:k + T],
                                       scalar=cwc[:, fc * 4 + k:fc * 4 + k + 1],
                                       in1=convt[b][:], op0=ALU.mult, op1=ALU.add)),
                                   sem="V")
                    v_conv[fc] = c["V"]
                    ws = [("V", v_conv[fc])]
                    if fc >= 2:
                        ws.append((f"XS{b}", g_store[fc - 2]))
                    c = op("scalar",
                           (lambda e, b=b, fc=fc: e.activation(
                               out=xbcb[b][:], in_=convt[b][:], func=AF.Silu,
                               bias=cbc[:, fc:fc + 1], scale=1.0)),
                           wait=ws, sem="A")
                    c = dma("gpsimd", d_xbcs[fc], xbcb[b][:], wait=bar("A"), sem=f"XS{b}")
                    g_store[fc] = c[f"XS{b}"]
                else:
                    op("scalar", (lambda e, b=b: e.activation(
                        out=dtq[:], in_=projf[b][:, 3:TH], func=AF.Identity,
                        bias=dtbc[:, 0:1])), wait=bar("A", "V"), sem="A")
                    op("scalar", (lambda e: e.activation(out=convt[0][:], in_=dtq[:],
                                                         func=AF.Abs)), sem="A")
                    op("scalar", (lambda e: e.activation(out=convt[1][:], in_=convt[0][:],
                                                         func=AF.Exp, scale=-1.0)), sem="A")
                    op("scalar", (lambda e: e.activation(out=convt[0][:], in_=convt[1][:],
                                                         func=AF.Ln, bias=1.0)), sem="A")
                    op("vector", (lambda e: e.tensor_relu(out=convt[1][:], in_=dtq[:])),
                       wait=bar("A"), sem="V")
                    op("vector", (lambda e: e.tensor_add(out=dtq[:], in0=convt[1][:],
                                                         in1=convt[0][:])), sem="V")
                    op("vector", (lambda e: e.tensor_scalar(out=dtq[:], in0=dtq[:],
                                                            scalar1=100.0, scalar2=0.001,
                                                            op0=ALU.min, op1=ALU.max)), sem="V")
                    op("vector", (lambda e: e.tensor_scalar(out=datq[:], in0=dtq[:],
                                                            scalar1=acol[:, 0:1], scalar2=None,
                                                            op0=ALU.mult)), sem="V")
                    if DEBUG:
                        dma("gpsimd", d_dts, dtq[:], wait=bar("V"))

        if PHASES >= 2:
            phase_barrier()
            op("vector", lambda e: e.memset(sloc[:], 0.0), sem="V")
            grp_done_V = {}
            g_csf = {}
            grp_done_T = {}
            v_states_evac = {0: 0, 1: 0}
            v_yd_evac = {}
            g_yd_store = {}
            g_st_store = {0: 0, 1: 0}
            v_p2 = {}
            a_exp = {}
            v_xdt_done = {}
            v_bt_evac = {0: 0, 1: 0}
            v_sloc_upd = 0
            for wv in range(4):
                wpre = list(bar("V", "A", "Tn", "G", "G2", "XG0", "XG1", "RB0", "RB1",
                                "YS0", "YS1", "SS0", "SS1"))
                v_pre = 0
                first_g = True
                for j in range(2):
                    c = 2 * wv + j
                    op("vector", (lambda e, c=c, j=j: e.tensor_tensor_scan(
                        out=csb[:, j * 128:(j + 1) * 128],
                        data0=datq[:, c * 128:(c + 1) * 128], data1=zcol[:],
                        initial=0.0, op0=ALU.add, op1=ALU.add)),
                       wait=(wpre if j == 0 else ()), sem="V")
                    op("vector", (lambda e, j=j: e.tensor_add(
                        out=logd[:], in0=logd[:], in1=csb[:, j * 128 + 127:j * 128 + 128])),
                       wait=[("V", CNT["V"])], sem="V")
                    op("scalar", (lambda e, j=j: e.activation(
                        out=ecsb[:, j * 128:(j + 1) * 128], in_=csb[:, j * 128:(j + 1) * 128],
                        func=AF.Exp)), wait=bar("V"), sem="A")
                    op("scalar", (lambda e, j=j: e.activation(
                        out=dcy[:, j * 128:(j + 1) * 128], in_=csb[:, j * 128:(j + 1) * 128],
                        func=AF.Exp, scale=-1.0,
                        bias=csb[:, j * 128 + 127:j * 128 + 128])), sem="A")
                    wg0 = list(bar("A")) + ([("V", CNT["V"])] if first_g else [])
                    first_g = False
                    dma("gpsimd", d_ecs[c].rearrange("(h l) -> h l", h=128),
                        ecsb[:, j * 128:(j + 1) * 128], wait=wg0)
                    dma("gpsimd", d_cdr[c].rearrange("(h o) -> h o", o=1),
                        ecsb[:, j * 128 + 127:j * 128 + 128])
                    ccsf = dma("gpsimd", d_csf[c].rearrange("(h l) -> h l", h=128),
                               csb[:, j * 128:(j + 1) * 128])
                    g_csf[c] = ccsf["G"]
                    cg = dma("gpsimd", cdrp[:, j * 128:(j + 1) * 128],
                             d_cdr[c].rearrange("(o h) -> o h", o=1).partition_broadcast(128),
                             wait=bar("G"))
                    g_cdrp_wave = cg["G"]
                    wtp = list(bar("A", "V")) + [("V", v_states_evac[0]),
                                                 ("V", v_states_evac[1])]
                    op("tensor", (lambda e, j=j: e.transpose(
                        pbank[0][:, 0:128], csb[:, j * 128:(j + 1) * 128], identf[:])),
                       wait=wtp, sem="Tn")
                    op("tensor", (lambda e, j=j: e.transpose(
                        pbank[0][:, 128:256], dcy[:, j * 128:(j + 1) * 128], identf[:])),
                       sem="Tn")
                    op("tensor", (lambda e, c=c: e.transpose(
                        pbank[0][:, 256:384], dtq[:, c * 128:(c + 1) * 128], identf[:])),
                       sem="Tn")
                    op("vector", (lambda e, j=j: e.tensor_copy(
                        out=csT[:, j * 128:(j + 1) * 128], in_=pbank[0][:, 0:128])),
                       wait=bar("Tn"), sem="V")
                    op("vector", (lambda e, j=j: e.tensor_copy(
                        out=dcyT[:, j * 128:(j + 1) * 128], in_=pbank[0][:, 128:256])),
                       sem="V")
                    c2 = op("vector", (lambda e, j=j: e.tensor_copy(
                        out=dtT[:, j * 128:(j + 1) * 128], in_=pbank[0][:, 256:384])),
                       sem="V")
                    v_pre = c2["V"]

                for g in range(NG):
                    gidx = wv * NG + g
                    gb_ = g % 2
                    xg_ = f"XG{gb_}"
                    wl = []
                    if gidx >= 2:
                        wl = [("V", grp_done_V[gidx - 2]), ("Tn", grp_done_T[gidx - 2])]
                    for i in range(8):
                        dma("scalar", xgw[gb_][:, i * 256:(i + 1) * 256],
                            d_xbcs[8 * g + i][:, wv * 256:(wv + 1) * 256],
                            wait=(wl if i == 0 else ()), sem=xg_)
                    dma("scalar", xgw[gb_][:, 8 * 256:9 * 256],
                        d_xbcs[64 + g][:, wv * 256:(wv + 1) * 256], sem=xg_)
                    c2 = dma("scalar", xgw[gb_][:, 9 * 256:10 * 256],
                             d_xbcs[72 + g][:, wv * 256:(wv + 1) * 256], sem=xg_)
                    g_xgw = c2[xg_]

                    for j in range(2):
                        c = 2 * wv + j
                        sidx = 2 * gidx + j
                        jj = sidx % 2
                        rb_ = f"RB{jj}"
                        wr = [("V", v_p2.get(jj, 0)), ("G", g_csf[c])]
                        c2 = dma("sync", rbc[jj][:],
                                 d_csf[c][g * 2048:(g + 1) * 2048].partition_broadcast(128),
                                 wait=wr, sem=rb_)
                        grbc = c2[rb_]
                        wp1 = [(rb_, grbc), ("V", v_pre), ("A", a_exp.get(jj, 0))]
                        h0 = 16 * g
                        op("vector", (lambda e, jj=jj, j=j, h0=h0: e.tensor_tensor(
                            out=scr2[:].rearrange("p (k l) -> p k l", k=16),
                            in0=rbc[jj][:].rearrange("p (k l) -> p k l", k=16),
                            in1=csT[:, j * 128 + h0:j * 128 + h0 + 16]
                                .unsqueeze(2).broadcast_to([128, 16, 128]),
                            op=ALU.subtract)), wait=wp1, sem="V")
                        c2 = op("vector", (lambda e, jj=jj: e.tensor_tensor(
                            out=p1b[jj][:].rearrange("p (k l) -> p k l", k=16),
                            in0=scr2[:].rearrange("p (k l) -> p k l", k=16),
                            in1=maskm[:].unsqueeze(1).broadcast_to([128, 16, 128]),
                            op=ALU.add)), sem="V")
                        vp1 = c2["V"]
                        c2 = op("scalar", (lambda e, jj=jj: e.activation(
                            out=lmt[jj][:], in_=p1b[jj][:], func=AF.Exp)),
                           wait=[("V", vp1), ("V", v_p2.get(jj, 0))], sem="A")
                        a_exp[jj] = c2["A"]
                        ws = [(xg_, g_xgw), ("V", v_p2.get(jj, 0))]
                        c2 = op("tensor", (lambda e, gb_=gb_, j=j, jj=jj: e.matmul(
                            pbank[2 + jj][:, 0:128],
                            xgw[gb_][:, 8 * 256 + j * 128:8 * 256 + (j + 1) * 128],
                            xgw[gb_][:, 9 * 256 + j * 128:9 * 256 + (j + 1) * 128],
                            start=True, stop=True)),
                           wait=ws, sem="Tn")
                        t_sc = c2["Tn"]
                        wp2 = [("Tn", t_sc), ("A", a_exp[jj])]
                        if gidx >= 1:
                            wp2.append(("Tn", grp_done_T[gidx - 1]))
                        for q in range(4):
                            c2 = op("vector", (lambda e, jj=jj, q=q: e.tensor_tensor(
                                out=scp[jj][:, q * 512:(q + 1) * 512].rearrange(
                                    "p (r f) -> p r f", r=4),
                                in0=pbank[2 + jj][:, 0:128].unsqueeze(1).broadcast_to(
                                    [128, 4, 128]),
                                in1=lmt[jj][:, q * 512:(q + 1) * 512].rearrange(
                                    "p (r f) -> p r f", r=4),
                                op=ALU.mult)),
                               wait=(wp2 if q == 0 else ()), sem="V")
                        v_p2[jj] = c2["V"]
                        t_tx = {}
                        for i in range(8):
                            tp = i // 4
                            wtx = ([("V", v_xdt_done.get(tp, 0))] if i % 4 == 0 else ())
                            c2 = op("tensor", (lambda e, gb_=gb_, i=i, j=j, tp=tp: e.transpose(
                                ptb[tp][:, (i % 4) * 128:(i % 4 + 1) * 128],
                                xgw[gb_][:, i * 256 + j * 128:i * 256 + (j + 1) * 128],
                                identbf[:])), wait=wtx, sem="Tn")
                            if i % 4 == 3:
                                t_tx[tp] = c2["Tn"]
                        for half in range(2):
                            hh0 = 16 * g + 8 * half
                            op("vector", (lambda e, jj=jj, j=j, half=half, hh0=hh0:
                               e.tensor_tensor(
                                   out=xdt[jj][:, half * 512:(half + 1) * 512]
                                       .rearrange("p (k d) -> p k d", k=8),
                                   in0=ptb[half][:, 0:512].rearrange("p (k d) -> p k d", k=8),
                                   in1=dtT[:, j * 128 + hh0:j * 128 + hh0 + 8]
                                       .unsqueeze(2).broadcast_to([128, 8, 64]),
                                   op=ALU.mult)),
                               wait=[("Tn", t_tx[half])], sem="V")
                            c2 = op("vector", (lambda e, jj=jj, j=j, half=half, hh0=hh0:
                                    e.tensor_tensor(
                                        out=xdtp[jj][:, half * 512:(half + 1) * 512]
                                            .rearrange("p (k d) -> p k d", k=8),
                                        in0=xdt[jj][:, half * 512:(half + 1) * 512]
                                            .rearrange("p (k d) -> p k d", k=8),
                                        in1=dcyT[:, j * 128 + hh0:j * 128 + hh0 + 8]
                                            .unsqueeze(2).broadcast_to([128, 8, 64]),
                                        op=ALU.mult)), sem="V")
                            v_xdt_done[half] = c2["V"]
                        t_ydiag = {}
                        for i in range(8):
                            bk = 4 + (i // 4)
                            off = (i % 4) * 128
                            wyd = []
                            if i % 4 == 0:
                                wyd = [("V", v_xdt_done[i // 4]),
                                       ("V", v_yd_evac.get(bk, 0))]
                            op("tensor", (lambda e, jj=jj, i=i, bk=bk, off=off: e.matmul(
                                pbank[bk][0:64, off:off + 128],
                                xdt[jj][:, (2 * i) * 64:(2 * i + 1) * 64],
                                scp[jj][:, (2 * i) * 128:(2 * i + 1) * 128],
                                start=True, stop=True, tile_position=(0, 0))),
                               wait=wyd, sem="Tn")
                            c2 = op("tensor", (lambda e, jj=jj, i=i, bk=bk, off=off: e.matmul(
                                pbank[bk][64:128, off:off + 128],
                                xdt[jj][:, (2 * i + 1) * 64:(2 * i + 2) * 64],
                                scp[jj][:, (2 * i + 1) * 128:(2 * i + 2) * 128],
                                start=True, stop=True, tile_position=(0, 64))),
                               sem="Tn")
                            t_ydiag[bk] = c2["Tn"]
                        for half in range(2):
                            bk = 4 + half
                            fc0 = 8 * g + 4 * half
                            wye = [("Tn", t_ydiag[bk]), (f"YS{jj}", g_yd_store.get(jj, 0))]
                            xb0 = (4 * half) * 256 + j * 128
                            op("vector", (lambda e, gb_=gb_, xb0=xb0, fc0=fc0:
                               e.tensor_tensor(
                                   out=scr2[:, 0:512].rearrange("p (q l) -> p q l", q=4),
                                   in0=xgw[gb_][:, xb0:xb0 + 1024]
                                       .rearrange("p (i l) -> p i l", i=4)[:, :, 0:128],
                                   in1=drep[:, fc0:fc0 + 4]
                                       .unsqueeze(2).broadcast_to([128, 4, 128]),
                                   op=ALU.mult)), wait=wye, sem="V")
                            c2 = op("vector", (lambda e, jj=jj, bk=bk, half=half:
                                    e.tensor_add(
                                        out=ydb[jj][:, half * 512:(half + 1) * 512],
                                        in0=scr2[:, 0:512],
                                        in1=pbank[bk][:, 0:512])), sem="V")
                            v_yd_evac[bk] = c2["V"]
                        c2 = dma("gpsimd", d_yds[c][:, (8 * g) * 128:(8 * g) * 128 + 1024],
                                 ydb[jj][:], wait=[("V", v_yd_evac[5])], sem=f"YS{jj}")
                        g_yd_store[jj] = c2[f"YS{jj}"]
                        wbt = [("V", v_bt_evac[j])]
                        c2 = op("tensor", (lambda e, gb_=gb_, j=j: e.transpose(
                            ptb[j][:, 512:640],
                            xgw[gb_][:, 8 * 256 + j * 128:8 * 256 + (j + 1) * 128],
                            identbf[:])), wait=wbt, sem="Tn")
                        c2 = op("vector", (lambda e, j=j: e.tensor_copy(
                            out=btb[:, j * 128:(j + 1) * 128], in_=ptb[j][:, 512:640])),
                           wait=[("Tn", c2["Tn"])], sem="V")
                        v_bt_evac[j] = c2["V"]
                        for half in range(2):
                            wsm_ = [("V", v_bt_evac[j]), ("V", v_states_evac[half])]
                            for q in range(8):
                                k = 8 * half + q
                                c2 = op("tensor", (lambda e, jj=jj, j=j, k=k, q=q, half=half:
                                        e.matmul(
                                            pbank[half][:, q * 64:(q + 1) * 64],
                                            btb[:, j * 128:(j + 1) * 128],
                                            xdtp[jj][:, k * 64:(k + 1) * 64],
                                            start=True, stop=True)),
                                       wait=(wsm_ if q == 0 else ()), sem="Tn")
                            t_st = c2["Tn"]
                            wse = [("Tn", t_st), (f"SS{half}", g_st_store[half]),
                                   ("V", v_sloc_upd), ("G", g_cdrp_wave)]
                            c2 = op("vector", (lambda e, half=half: e.tensor_copy(
                                out=stb2[half][:], in_=pbank[half][:])),
                               wait=wse, sem="V")
                            vse = c2["V"]
                            v_states_evac[half] = vse
                            c2 = dma("gpsimd",
                                     d_sts[c][:, (16 * g + 8 * half) * 64:(16 * g + 8 * half) * 64 + 512],
                                     stb2[half][:], wait=[("V", vse)], sem=f"SS{half}")
                            g_st_store[half] = c2[f"SS{half}"]
                            h0u = 16 * g + 8 * half
                            op("vector", (lambda e, j=j, h0u=h0u: e.tensor_tensor(
                                out=sloc[:, h0u * 64:h0u * 64 + 512]
                                    .rearrange("p (k d) -> p k d", k=8),
                                in0=sloc[:, h0u * 64:h0u * 64 + 512]
                                    .rearrange("p (k d) -> p k d", k=8),
                                in1=cdrp[:, j * 128 + h0u:j * 128 + h0u + 8]
                                    .unsqueeze(2).broadcast_to([128, 8, 64]),
                                op=ALU.mult)), sem="V")
                            c2 = op("vector", (lambda e, half=half, h0u=h0u: e.tensor_add(
                                out=sloc[:, h0u * 64:h0u * 64 + 512],
                                in0=sloc[:, h0u * 64:h0u * 64 + 512],
                                in1=stb2[half][:])), sem="V")
                            v_sloc_upd = c2["V"]
                    grp_done_V[gidx] = CNT["V"]
                    grp_done_T[gidx] = CNT["Tn"]

        if PHASES >= 3:
            phase_barrier()
            dma("gpsimd", d_ccsi[:, 0:8192], sloc[:])
            dma("gpsimd", d_ccsi[:, 8192:8193], logd[:], slow_ok=True)
            if DEBUG:
                dma("gpsimd", d_dbg_sloc, sloc[:])
            op("gpsimd", (lambda e: e.collective_compute(
                "AllGather", ALU.bypass,
                replica_groups=[[0, 1, 2, 3], [4, 5, 6, 7]],
                ins=[d_ccsi],
                outs=[d_ccso])), wait=bar("G"), sem="CC")

            for kc in range(KC):
                dma("gpsimd", xtbf[:, kc * T:(kc + 1) * T], d_xtbf[kc], sem="G2")
            xg_ld = bar("G2")
            v_gcast = {}
            t_gmm = {}
            a_gsilu = {}
            g_gstore = {}
            for fc in range(NFI):
                b = fc % 2
                ws_ = f"WS{b}"
                w = [("V", v_gcast[fc - 2])] if fc >= 2 else ()
                c0 = dma("sync", wsl[b][:], d_wgate[fc].rearrange("p kc f -> p (kc f)"),
                         wait=w, sem=ws_)
                wv_ = [(ws_, c0[ws_])] + ([("Tn", t_gmm[fc - 2])] if fc >= 2 else [])
                c2 = op("vector", (lambda e, b=b: e.tensor_copy(out=wbfs[b][:], in_=wsl[b][:])),
                        wait=wv_, sem="V")
                v_gcast[fc] = c2["V"]
                wt = [("V", c2["V"])] + xg_ld + ([("A", a_gsilu[fc - 2])] if fc >= 2 else [])
                for kc in range(KC):
                    for tsi in range(2):
                        c2 = op("tensor", (lambda e, b=b, kc=kc, tsi=tsi: e.matmul(
                            pbank[2 * b + tsi][:],
                            wbfs[b][:, kc * 128:(kc + 1) * 128],
                            xtbf[:, kc * T + tsi * 512:kc * T + (tsi + 1) * 512],
                            start=(kc == 0), stop=(kc == KC - 1))),
                           wait=(wt if (kc == 0 and tsi == 0) else ()),
                           sem=("Tn" if (kc == KC - 1 and tsi == 1) else None))
                t_gmm[fc] = c2["Tn"]
                ws = [("Tn", c2["Tn"])] + ([(f"XS{b}", g_gstore[fc - 2])] if fc >= 2 else [])
                for tsi in range(2):
                    c2 = op("scalar", (lambda e, b=b, tsi=tsi: e.activation(
                        out=gtile[b][:, tsi * 512:(tsi + 1) * 512],
                        in_=pbank[2 * b + tsi][:], func=AF.Silu)),
                       wait=(ws if tsi == 0 else ()), sem="A")
                a_gsilu[fc] = c2["A"]
                c2 = dma("gpsimd", d_gs[fc], gtile[b][:], wait=[("A", c2["A"])], sem=f"XS{b}")
                g_gstore[fc] = c2[f"XS{b}"]

            phase_barrier()
            op("vector", lambda e: e.memset(sloc[:], 0.0),
               wait=list(bar("G")) + [("CC", CNT["CC"])], sem="V")
            for jr in range(4):
                dma("gpsimd", erow[jr:jr + 1, :],
                    d_ccso[jr * 128:(jr + 1) * 128, 8192:8193].rearrange("h o -> o h"),
                    wait=(bar("CC") if jr == 0 else ()), slow_ok=True)
            op("scalar", (lambda e: e.activation(out=expdr[:], in_=erow[:], func=AF.Exp)),
               wait=bar("G"), sem="A")
            op("vector", (lambda e: e.tensor_scalar(
                out=expdr[:], in0=expdr[:], scalar1=ringmc[0:4, 0:1],
                scalar2=ringmc1[0:4, 0:1], op0=ALU.mult, op1=ALU.add)),
               wait=bar("A"), sem="V")
            for jr in range(4):
                dma("gpsimd", d_rows[jr][0:128].rearrange("(o f) -> o f", o=1),
                    expdr[jr:jr + 1, :], wait=bar("V"), sem="SG0")
            g_rows_done = CNT["SG0"]
            Sstg = mb(64.25, 32, F32)
            for jr in range(3):
                dma("gpsimd", Sstg[:], d_ccso[jr * 128:(jr + 1) * 128, 0:8192], wait=bar("V"))
                c2 = op("vector", (lambda e, jr=jr: e.tensor_scalar(
                    out=Sstg[:], in0=Sstg[:], scalar1=ringm[:, jr:jr + 1], scalar2=None,
                    op0=ALU.mult)), wait=bar("G"), sem="V")
                dma("gpsimd", cdrp[:, 0:128],
                    d_rows[jr][0:128].rearrange("(o f) -> o f", o=1).partition_broadcast(128),
                    wait=[("V", c2["V"]), ("SG0", g_rows_done)])
                wcomb = bar("G", "V")
                op("vector", (lambda e: e.tensor_tensor(
                    out=sloc[:].rearrange("p (h d) -> p h d", h=128),
                    in0=sloc[:].rearrange("p (h d) -> p h d", h=128),
                    in1=cdrp[:, 0:128].unsqueeze(2).broadcast_to([128, 128, 64]),
                    op=ALU.mult)), wait=wcomb, sem="V")
                c2 = op("vector", (lambda e: e.tensor_add(
                    out=sloc[:], in0=sloc[:], in1=Sstg[:])), sem="V")
            if DEBUG:
                dma("gpsimd", d_dbg_state, sloc[:], wait=bar("V"))

        if PHASES >= 4:
            phase_barrier()
            v_supd2 = 0
            v_ycomb = {4: 0, 5: 0}
            v_cp = {0: 0, 1: 0}
            t_yoff = {0: 0, 1: 0}
            v_yg = {}
            v_gate_done = {}
            t_mq = {}
            a_ln_grp = {}
            sg1_prev = {}
            g_ng = {0: 0, 1: 0}
            v_apply_grp = {}
            sw_bc = 0
            for wv in range(4):
                wpre2 = list(bar("V", "A", "Tn", "G", "SS0", "SS1", "YS0", "YS1",
                                 "XG0", "XG1", "RB0", "RB1", "XS0", "XS1", "SG1",
                                 "YP0", "YP1", "SP0", "SP1"))
                for j in range(2):
                    c = 2 * wv + j
                    w0 = (wpre2 if j == 0 else [("V", v_supd2)])
                    cst2 = []
                    for q in range(2):
                        sq = f"SP{q}"
                        cq = dma("gpsimd", stre[:, q * 4096:(q + 1) * 4096],
                                 d_sts[c][:, q * 4096:(q + 1) * 4096],
                                 wait=(w0 if q == 0 else ()), sem=sq)
                        cst2.append((sq, cq[sq]))
                    wyd0 = ([("V", v_ycomb[4]), ("V", v_ycomb[5])] if (wv + j) else [])
                    wyd0 += bar("SS0", "SS1")
                    ypseq = ["YS0", "YS1", "YP0", "YP1"]
                    cyd4 = []
                    for q in range(4):
                        sq = ypseq[q]
                        cq = dma("gpsimd", ydre[:, q * 2048:(q + 1) * 2048],
                                 d_yds[c][:, q * 2048:(q + 1) * 2048],
                                 wait=(wyd0 if q == 0 else ()), sem=sq)
                        cyd4.append((sq, cq[sq]))
                    ccd = dma("gpsimd", cdrp[:, 0:128],
                              d_cdr[c].rearrange("(o h) -> o h", o=1).partition_broadcast(128),
                              wait=[("V", v_supd2)], sem="G")
                    wcpv = [("Tn", max(t_yoff[0], t_yoff[1]))]
                    if j == 0:
                        wcpv += bar("SG1", "XS0", "XS1", "SS0", "SS1")
                    cpv = op("vector", (lambda e: e.tensor_copy(out=pvb[:], in_=sloc[:])),
                             wait=wcpv, sem="V")
                    for g in range(NG):
                        par = g % 2
                        cct = dma("scalar", ctl[:, g * 128:(g + 1) * 128],
                                  d_xbcs[72 + g][:, c * 128:(c + 1) * 128],
                                  wait=([("V", v_cp[par])] if (wv + j + g) else ()), sem="XG0")
                        ceb = dma("sync", ebc[par][:],
                                  d_ecs[c][g * 2048:(g + 1) * 2048].partition_broadcast(128),
                                  wait=[("V", v_cp[par]),
                                        ("V", v_gate_done.get(wv - 1, 0))], sem=f"RB{par}")
                        ccp = op("vector", (lambda e, par=par, g=g: e.tensor_tensor(
                            out=cpb[par][:].rearrange("p (r f) -> p r f", r=16),
                            in0=ebc[par][:].rearrange("p (r f) -> p r f", r=16),
                            in1=ctl[:, g * 128:(g + 1) * 128].unsqueeze(1).broadcast_to(
                                [128, 16, 128]),
                            op=ALU.mult)),
                           wait=[("XG0", cct["XG0"]), (f"RB{par}", ceb[f"RB{par}"]),
                                 ("Tn", t_yoff[par]), ("V", cpv["V"])], sem="V")
                        v_cp[par] = ccp["V"]
                        t_ydk = {4: 0, 5: 0}
                        for i in range(8):
                            bk = 4 + (i // 4)
                            off = (i % 4) * 128
                            for hh in range(2):
                                h = 16 * g + 2 * i + hh
                                wyo = []
                                if i == 0 and hh == 0:
                                    wyo = [("V", ccp["V"]), ("V", v_ycomb[4]),
                                           ("V", v_ycomb[5])]
                                cmo = op("tensor", (lambda e, par=par, i=i, hh=hh, h=h,
                                                    bk=bk, off=off: e.matmul(
                                    pbank[bk][hh * 64:(hh + 1) * 64, off:off + 128],
                                    pvb[:, h * 64:(h + 1) * 64],
                                    cpb[par][:, (2 * i + hh) * 128:(2 * i + hh + 1) * 128],
                                    start=True, stop=True,
                                    tile_position=(0, hh * 64))),
                                   wait=wyo, sem="Tn")
                            t_ydk[bk] = cmo["Tn"]
                        t_yoff[par] = cmo["Tn"]
                        for half in range(2):
                            bk = 4 + half
                            fc0 = 8 * g + 4 * half
                            cyc = op("vector", (lambda e, bk=bk, fc0=fc0, j=j: e.tensor_tensor(
                                out=ybuf[:, j * 8192 + fc0 * 128:j * 8192 + fc0 * 128 + 512],
                                in0=pbank[bk][:, 0:512],
                                in1=ydre[:, fc0 * 128:fc0 * 128 + 512],
                                op=ALU.add)),
                               wait=[("Tn", t_ydk[bk]), cyd4[g // 2]], sem="V")
                            v_ycomb[bk] = cyc["V"]
                    op("vector", (lambda e: e.tensor_tensor(
                        out=sloc[:].rearrange("p (h d) -> p h d", h=128),
                        in0=sloc[:].rearrange("p (h d) -> p h d", h=128),
                        in1=cdrp[:, 0:128].unsqueeze(2).broadcast_to([128, 128, 64]),
                        op=ALU.mult)),
                       wait=[("G", ccd["G"]), ("V", cpv["V"])], sem="V")
                    for q in range(2):
                        csu = op("vector", (lambda e, q=q: e.tensor_add(
                            out=sloc[:, q * 4096:(q + 1) * 4096],
                            in0=sloc[:, q * 4096:(q + 1) * 4096],
                            in1=stre[:, q * 4096:(q + 1) * 4096])),
                           wait=[cst2[q]], sem="V")
                    v_supd2 = csu["V"]
                if DEBUG:
                    for j in range(2):
                        dma("gpsimd", d_ys[2 * wv + j],
                            ybuf[:, j * 8192:(j + 1) * 8192], wait=bar("V"))
                c_gt = dma("gpsimd", gtall[:].rearrange("p (k t) -> p k t", k=64),
                           d_gs[0:64][:, :, wv * 256:(wv + 1) * 256]
                           .rearrange("k p t -> p k t"),
                           wait=[("V", v_ycomb[4]), ("V", v_ycomb[5]),
                                 ("V", v_cp[0]), ("V", v_cp[1])], sem="XS0")
                for g in range(NG):
                    for i in range(8):
                        fc = 8 * g + i
                        par = i % 2
                        gflat = (wv * NG + g) * 8 + i
                        wyg = []
                        if g == 0 and i == 0:
                            wyg = [("XS0", c_gt["XS0"]), ("V", CNT["V"])]
                        cyg = op("vector", (lambda e, fc=fc, i=i: e.tensor_tensor(
                            out=ygg[:, i * 256:(i + 1) * 256].rearrange(
                                "p (c l) -> p c l", c=2),
                            in0=gtall[:, fc * 256:(fc + 1) * 256]
                                .rearrange("p (c l) -> p c l", c=2),
                            in1=ybuf[:].rearrange("p (c f) -> p c f", c=2)[
                                :, :, fc * 128:(fc + 1) * 128],
                            op=ALU.mult)),
                           wait=wyg, sem="V")
                        v_yg[gflat] = cyg["V"]
                        csq = op("scalar", (lambda e, i=i, par=par: e.activation(
                            out=sqt[par][:], in_=ygg[:, i * 256:(i + 1) * 256],
                            func=AF.Square)),
                           wait=[("V", cyg["V"]), ("Tn", t_mq.get(par, 0))],
                           sem="A")
                        cmq = op("tensor", (lambda e, i=i, par=par: e.matmul(
                            pbank[0][0:1, 0:256], onesb[:, 0:1], sqt[par][:],
                            start=(i == 0), stop=(i == 7))),
                           wait=[("A", csq["A"])], sem="Tn")
                        t_mq[par] = cmq["Tn"]
                    cln = op("scalar", (lambda e: e.activation(
                        out=rowg, in_=pbank[0][0:1, 0:256],
                        func=AF.Ln, bias=epsc[0:1, 0:1], scale=1.0 / GROUP)),
                       wait=[("Tn", cmq["Tn"])], sem="A")
                    cex = op("scalar", (lambda e: e.activation(
                        out=rowg2, in_=rowg,
                        func=AF.Exp, scale=-0.5)), sem="A")
                    wrb = [("A", cex["A"]),
                           ("V", v_apply_grp.get(wv * NG + g - 1, 0))]
                    crb = op("tensor", (lambda e: e.matmul(
                        pbank[1][:, 0:256], onesrow[0:1, 0:128], rowg2,
                        start=True, stop=True)), wait=wrb, sem="Tn")
                    for i in range(8):
                        fc = 8 * g + i
                        cap = op("vector", (lambda e, fc=fc, i=i: e.scalar_tensor_tensor(
                            out=ybuf[:].rearrange("p (cc f) -> p cc f", cc=2)
                                [:, :, fc * 128:(fc + 1) * 128],
                            in0=ygg[:, i * 256:(i + 1) * 256]
                                .rearrange("p (cc l) -> p cc l", cc=2),
                            scalar=nwc[:, fc:fc + 1],
                            in1=pbank[1][:, 0:256].rearrange("p (cc l) -> p cc l", cc=2),
                            op0=ALU.mult, op1=ALU.mult)),
                           wait=([("Tn", crb["Tn"])] if i == 0 else ()), sem="V")
                    for cc in range(2):
                        cns = dma("gpsimd",
                                  d_ngs[8 * g:8 * g + 8][:, :, wv * 256 + cc * 128:
                                                         wv * 256 + cc * 128 + 128]
                                  .rearrange("k p l -> p k l"),
                                  ybuf[:, cc * 8192 + (8 * g) * 128:
                                       cc * 8192 + (8 * g) * 128 + 1024]
                                  .rearrange("p (k l) -> p k l", k=8),
                                  wait=([("V", cap["V"])] if cc == 0 else ()),
                                  sem=f"SS{cc}")
                        g_ng[cc] = cns[f"SS{cc}"]
                    v_apply_grp[wv * NG + g] = CNT["V"]
                v_gate_done[wv] = CNT["V"]

        if PHASES >= 5:
            phase_barrier()
            v_wcast5 = {}
            t_mm5 = {}
            v_evac5 = 0
            v_ev5b = {}
            for db in range(8):
                for fg in range(8):
                    idx = db * 8 + fg
                    wpar = fg % 2
                    w = [("V", v_wcast5[idx - 1])] if idx >= 1 else ()
                    cw5 = dma("sync", wsl5f[:].rearrange("p (k d f) -> p k d f", k=8, d=4),
                              d_wout[db, fg * 8:(fg + 1) * 8].rearrange("k d p f -> p k d f"),
                              wait=w, sem="WS0")
                    cc5 = op("vector", (lambda e, wpar=wpar: e.tensor_copy(
                        out=wbf5[wpar][:], in_=wsl5f[:])),
                       wait=[("WS0", cw5["WS0"])] +
                            ([("Tn", t_mm5[idx - 2])] if idx >= 2 else []), sem="V")
                    v_wcast5[idx] = cc5["V"]
                    crh = dma("gpsimd", rhs8[wpar][:].rearrange("p (k t) -> p k t", k=8),
                              d_ngs[fg * 8:(fg + 1) * 8].rearrange("k p t -> p k t"),
                              wait=([("Tn", t_mm5[idx - 2])] if idx >= 2 else ()),
                              sem=f"XG{wpar}")
                    for k8 in range(8):
                        for dcin in range(4):
                            for th in range(2):
                                wmm = []
                                if k8 == 0 and dcin == 0 and th == 0:
                                    wmm = [("V", cc5["V"]), (f"XG{wpar}", crh[f"XG{wpar}"])]
                                if k8 == 0 and fg == 0:
                                    wmm.append(("V", v_ev5b.get((dcin, th), 0)))
                                cm5 = op("tensor", (lambda e, wpar=wpar, dcin=dcin, th=th,
                                                    k8=k8, fg=fg: e.matmul(
                                        pbank[dcin * 2 + th][:],
                                        wbf5[wpar][:, (k8 * 4 + dcin) * 128:(k8 * 4 + dcin + 1) * 128],
                                        rhs8[wpar][:, k8 * 1024 + th * 512:k8 * 1024 + (th + 1) * 512],
                                        start=(fg == 0 and k8 == 0),
                                        stop=(fg == 7 and k8 == 7))),
                                   wait=wmm,
                                   sem=("Tn" if (dcin == 3 and th == 1) else None))
                    t_mm5[idx] = cm5["Tn"]
                for dcin in range(4):
                    dc = 4 * db + dcin
                    xpar = dcin % 2
                    cxt = dma("gpsimd", xtre[xpar][:], d_xtbf[dc],
                              wait=([("V", v_evac5)] if (db + dcin) else list(bar("V"))),
                              sem=f"RB{xpar}")
                    for th in range(2):
                        cev = op("vector", (lambda e, dcin=dcin, th=th, dc=dc, xpar=xpar:
                                e.tensor_tensor(
                                    out=resid[:, dc * 1024 + th * 512:dc * 1024 + (th + 1) * 512],
                                    in0=pbank[dcin * 2 + th][:],
                                    in1=xtre[xpar][:, th * 512:(th + 1) * 512],
                                    op=ALU.add)),
                           wait=[("Tn", t_mm5[db * 8 + 7]),
                                 (f"RB{xpar}", cxt[f"RB{xpar}"])], sem="V")
                        v_ev5b[(dcin, th)] = cev["V"]
                    v_evac5 = cev["V"]
            if DEBUG:
                dma("gpsimd", d_r1.rearrange("k p t -> p k t"),
                    resid[:].rearrange("p (k t) -> p k t", k=KC), wait=bar("V"))

            phase_barrier()
            a_sq5 = {}
            t_row5 = 0
            for dc in range(KC):
                par = dc % 2
                csq = op("scalar", (lambda e, dc=dc, par=par: e.activation(
                    out=ngrh[par][:], in_=resid[:, dc * 1024:(dc + 1) * 1024],
                    func=AF.Square)),
                   wait=([("Tn", a_sq5[dc - 2])] if dc >= 2 else ()), sem="A")
                for th in range(2):
                    op("tensor", (lambda e, dc=dc, th=th: e.matmul(
                        pbank[0 + th][0:1, 0:512], onesb[:, 0:1],
                        resid[:, dc * 1024 + th * 512:dc * 1024 + (th + 1) * 512],
                        start=(dc == 0), stop=(dc == KC - 1))),
                       wait=([("A", csq["A"])] if (dc == 0 and th == 0) else ()), sem="Tn")
                    cm6 = op("tensor", (lambda e, dc=dc, th=th, par=par: e.matmul(
                        pbank[2 + th][0:1, 0:512], onesb[:, 0:1],
                        ngrh[par][:, th * 512:(th + 1) * 512],
                        start=(dc == 0), stop=(dc == KC - 1))),
                       wait=[("A", csq["A"])], sem="Tn")
                a_sq5[dc] = cm6["Tn"]
            t_row5 = cm6["Tn"]
            for th in range(2):
                op("vector", (lambda e, th=th: e.tensor_scalar(
                    out=rowb[0:1, th * 512:(th + 1) * 512], in0=pbank[th][0:1, 0:512],
                    scalar1=1.0 / DM, scalar2=None, op0=ALU.mult)),
                   wait=[("Tn", t_row5)], sem="V")
                op("vector", (lambda e, th=th: e.tensor_scalar(
                    out=rowb2[0:1, th * 512:(th + 1) * 512], in0=pbank[2 + th][0:1, 0:512],
                    scalar1=1.0 / DM, scalar2=None, op0=ALU.mult)), sem="V")
            op("vector", (lambda e: e.tensor_mul(out=rowb3[0:1, :], in0=rowb[0:1, :],
                                                 in1=rowb[0:1, :])), sem="V")
            cvv = op("vector", (lambda e: e.tensor_sub(out=rowb3[0:1, :], in0=rowb2[0:1, :],
                                                       in1=rowb3[0:1, :])), sem="V")
            cl1 = op("scalar", (lambda e: e.activation(out=rowb2[0:1, :], in_=rowb3[0:1, :],
                                                       func=AF.Ln, bias=epsc[0:1, 0:1])),
                     wait=[("V", cvv["V"])], sem="A")
            cl2 = op("scalar", (lambda e: e.activation(out=rowb3[0:1, :], in_=rowb2[0:1, :],
                                                       func=AF.Exp, scale=-0.5)), sem="A")
            cr1 = dma("gpsimd", d_rows[4][:].rearrange("(o f) -> o f", o=1), rowb[0:1, :],
                      wait=[("A", cl2["A"])], sem="SG0")
            cr2 = dma("gpsimd", d_rows[5][:].rearrange("(o f) -> o f", o=1), rowb3[0:1, :],
                      sem="SG0")
            cb1 = dma("gpsimd", mrep[:], d_rows[4][:].rearrange("(o f) -> o f", o=1)
                      .partition_broadcast(128), wait=[("SG0", cr2["SG0"])], sem="SG0")
            cb2 = dma("gpsimd", rrep[:], d_rows[5][:].rearrange("(o f) -> o f", o=1)
                      .partition_broadcast(128), sem="SG0")
            ln1_ready = cb2["SG0"]
            for dc in range(KC):
                par = dc % 2
                ct1 = op("vector", (lambda e, dc=dc: e.tensor_sub(
                    out=tmpf1[:], in0=resid[:, dc * 1024:(dc + 1) * 1024],
                    in1=mrep[:])),
                   wait=([("SG0", ln1_ready)] if dc == 0 else ()), sem="V")
                op("vector", (lambda e, dc=dc: e.tensor_mul(
                    out=x2b[:, dc * 1024:(dc + 1) * 1024], in0=tmpf1[:], in1=rrep[:])),
                   sem="V")
            if DEBUG:
                dma("gpsimd", d_x2s.rearrange("k p t -> p k t"),
                    x2b[:, 0:KC * 1024].rearrange("p (k t) -> p k t", k=KC), wait=bar("V"))

        if PHASES >= 6:
            phase_barrier()
            v_agc = {}
            for kc in range(KC):
                par = kc % 2
                w = [("V", v_agc[kc - 1])] if kc >= 1 else ()
                cag = dma("sync", agstg[:, 0:68], d_wag[kc], wait=w, sem=f"WS{par}")
                cac = op("vector", (lambda e, kc=kc: e.tensor_copy(
                    out=agb[:, kc * 68:(kc + 1) * 68], in_=agstg[:, 0:68])),
                   wait=[(f"WS{par}", cag[f"WS{par}"])], sem="V")
                v_agc[kc] = cac["V"]
            for kc in range(KC):
                for th in range(2):
                    cmA = op("tensor", (lambda e, kc=kc, th=th: e.matmul(
                        pbank[4 + th][0:68, 0:512],
                        agb[:, kc * 68:(kc + 1) * 68],
                        x2b[:, kc * 1024 + th * 512:kc * 1024 + (th + 1) * 512],
                        start=(kc == 0), stop=(kc == KC - 1))),
                       wait=(bar("V") if (kc == 0 and th == 0) else ()), sem="Tn")
            t_ag = cmA["Tn"]
            for th in range(2):
                op("vector", (lambda e, th=th: e.tensor_copy(
                    out=sc4[0:4, th * 512:(th + 1) * 512],
                    in_=pbank[4 + th][64:68, 0:512])),
                   wait=([("Tn", t_ag)] if th == 0 else ()), sem="V")
            cgb = op("vector", (lambda e: e.tensor_scalar(
                out=sc4[0:4, :], in0=sc4[0:4, :], scalar1=gbc[0:4, 0:1], scalar2=None,
                op0=ALU.add)), wait=[("V", CNT["V"])], sem="V")
            t_tr6 = {}
            for q in range(8):
                ctr = op("tensor", (lambda e, q=q: e.transpose(
                    pbank[6][:, q * 4:(q + 1) * 4],
                    sc4[0:4, q * 128:(q + 1) * 128], identf[0:4, 0:4])),
                   wait=([("V", cgb["V"])] if q == 0 else ()), sem="Tn")
                t_tr6[q] = ctr["Tn"]
            cst6 = op("vector", (lambda e: e.tensor_copy(out=scbt[:, 0:32],
                                                         in_=pbank[6][:, 0:32])),
                      wait=[("Tn", t_tr6[7])], sem="V")
            cpr = cst6
            for q in range(8):
                cpr = op("vector", (lambda e, q=q: e.reduce_max(
                    out=mxb[:, q:q + 1], in_=scbt[:, q * 4:(q + 1) * 4],
                    axis=AX.X)), wait=[("V", cpr["V"])], sem="V")
            for q in range(8):
                cpr = op("vector", (lambda e, q=q: e.tensor_scalar(
                    out=ebx[:, q * 4:(q + 1) * 4], in0=scbt[:, q * 4:(q + 1) * 4],
                    scalar1=mxb[:, q:q + 1], scalar2=None, op0=ALU.is_ge)),
                    wait=[("V", cpr["V"])], sem="V")
            for q in range(8):
                cpr = op("vector", (lambda e, q=q: e.tensor_tensor_scan(
                    out=cub[:, q * 4:(q + 1) * 4], data0=ebx[:, q * 4:(q + 1) * 4],
                    data1=zcol[:, 0:4], initial=0.0, op0=ALU.add, op1=ALU.add)),
                    wait=[("V", cpr["V"])], sem="V")
            cpr = op("vector", (lambda e: e.tensor_scalar(
                out=cub[:, 0:32], in0=cub[:, 0:32], scalar1=1.0, scalar2=None,
                op0=ALU.is_equal)), wait=[("V", cpr["V"])], sem="V")
            cfb = op("vector", (lambda e: e.tensor_mul(out=fbf[:, 0:32], in0=ebx[:, 0:32],
                                                       in1=cub[:, 0:32])),
                     wait=[("V", cpr["V"])], sem="V")
            for q in range(8):
                ctr = op("tensor", (lambda e, q=q: e.transpose(
                    ptb[1][0:4, q * 128:(q + 1) * 128], fbf[:, q * 4:(q + 1) * 4],
                    identbf[:])),
                   wait=([("V", cfb["V"])] if q == 0 else ()), sem="Tn")
            cfs = op("vector", (lambda e: e.tensor_copy(out=fsb[0:4, :],
                                                        in_=ptb[1][0:4, 0:1024])),
                     wait=[("Tn", ctr["Tn"])], sem="V")
            for th in range(2):
                cmk = op("tensor", (lambda e, th=th: e.matmul(
                    pbank[2 + th][0:64, 0:512], ltm[0:4, 0:64],
                    fsb[0:4, th * 512:(th + 1) * 512], start=True, stop=True)),
                   wait=([("V", cfs["V"])] if th == 0 else ()), sem="Tn")
            for th in range(2):
                op("vector", (lambda e, th=th: e.tensor_copy(
                    out=msb[0:64, th * 512:(th + 1) * 512],
                    in_=pbank[2 + th][0:64, 0:512])),
                   wait=[("Tn", cmk["Tn"])], sem="V")
            for th in range(2):
                cum6 = op("vector", (lambda e, th=th: e.tensor_tensor(
                    out=umb[0:64, th * 512:(th + 1) * 512],
                    in0=pbank[4 + th][0:64, 0:512],
                    in1=msb[0:64, th * 512:(th + 1) * 512], op=ALU.mult)), sem="V")
            v_um = cum6["V"]
            if DEBUG:
                dma("gpsimd", d_df1, scbt[:, 0:32], wait=bar("V"))
                dma("gpsimd", d_df2, fbf[:, 0:32], wait=bar("V"))
                dma("gpsimd", d_df3, msb[0:64, :], wait=bar("V"))
                dma("gpsimd", d_df4, umb[0:64, :], wait=bar("V"))
                dma("gpsimd", d_df5, sc4[0:4, :], wait=bar("V"))
            v_wc6 = {}
            t_mm6 = {}
            t_lora6 = {}
            v_ev6 = 0
            v_ev6b = {}
            v_bl6 = {}
            for db in range(8):
                bpar = db % 2
                cbl = dma("sync", blstg2[bpar][:].rearrange("j (d f) -> j d f", d=4),
                          d_wbl[4 * db:4 * db + 4].rearrange("d j f -> j d f"),
                          wait=([("V", v_bl6[db - 2])] if db >= 2 else ()), sem=f"RB{bpar}")
                cblc = op("vector", (lambda e, bpar=bpar: e.tensor_copy(
                    out=blb[bpar][:], in_=blstg2[bpar][:])),
                    wait=[(f"RB{bpar}", cbl[f"RB{bpar}"])] +
                         ([("Tn", t_lora6.get(db - 2, 0))] if db >= 2 else []), sem="V")
                v_bl6[db] = cblc["V"]
                for kg in range(8):
                    idx = db * 8 + kg
                    wpar = kg % 2
                    w = [("V", v_wc6[idx - 1])] if idx >= 1 else ()
                    cw6 = dma("sync", wslff[:].rearrange("p (k d f) -> p k d f", k=4, d=4),
                              d_wffn[db, kg * 4:(kg + 1) * 4].rearrange("k d p f -> p k d f"),
                              wait=w, sem="WS0")
                    cc6 = op("vector", (lambda e, wpar=wpar: e.tensor_copy(
                        out=wbff[wpar][:], in_=wslff[:])),
                       wait=[("WS0", cw6["WS0"])] +
                            ([("Tn", t_mm6.get(idx - 2, 0))] if idx >= 2 else []), sem="V")
                    v_wc6[idx] = cc6["V"]
                    for k4 in range(4):
                        kc = kg * 4 + k4
                        for dcin in range(4):
                            for th in range(2):
                                wmm = []
                                if k4 == 0 and dcin == 0 and th == 0:
                                    wmm = [("V", cc6["V"])]
                                    if kg == 0:
                                        wmm.append(("V", v_um))
                                if k4 == 0 and kg == 0:
                                    wmm.append(("V", v_ev6b.get((dcin, th), 0)))
                                cm6b = op("tensor", (lambda e, wpar=wpar, dcin=dcin, th=th,
                                                     kc=kc, k4=k4: e.matmul(
                                        pbank[dcin * 2 + th][:],
                                        wbff[wpar][:, (k4 * 4 + dcin) * 128:(k4 * 4 + dcin + 1) * 128],
                                        x2b[:, kc * 1024 + th * 512:kc * 1024 + (th + 1) * 512],
                                        start=(kc == 0), stop=False)),
                                   wait=wmm, sem=("Tn" if (dcin == 3 and th == 1) else None))
                    t_mm6[idx] = cm6b["Tn"]
                for dcin in range(4):
                    for th in range(2):
                        cm6c = op("tensor", (lambda e, bpar=bpar, dcin=dcin, th=th: e.matmul(
                            pbank[dcin * 2 + th][:],
                            blb[bpar][0:64, dcin * 128:(dcin + 1) * 128],
                            umb[0:64, th * 512:(th + 1) * 512],
                            start=False, stop=True)),
                           wait=([("V", v_bl6[db])] if (dcin == 0 and th == 0) else ()),
                           sem=("Tn" if (dcin == 3 and th == 1) else None))
                t_lora = cm6c["Tn"]
                t_lora6[db] = t_lora
                for dcin in range(4):
                    dc = 4 * db + dcin
                    for th in range(2):
                        cev = op("vector", (lambda e, dcin=dcin, th=th, dc=dc:
                                e.scalar_tensor_tensor(
                                    out=resid[:, dc * 1024 + th * 512:dc * 1024 + (th + 1) * 512],
                                    in0=pbank[dcin * 2 + th][:],
                                    scalar=bcolc[:, dc:dc + 1],
                                    in1=x2b[:, dc * 1024 + th * 512:dc * 1024 + (th + 1) * 512],
                                    op0=ALU.add, op1=ALU.add)),
                           wait=([("Tn", t_lora)] if (dcin == 0 and th == 0) else ()),
                           sem="V")
                        v_ev6b[(dcin, th)] = cev["V"]
                    v_ev6 = cev["V"]

            phase_barrier()
            a_sq6 = {}
            for dc in range(KC):
                par = dc % 2
                csq = op("scalar", (lambda e, dc=dc, par=par: e.activation(
                    out=ngrh[par][:], in_=resid[:, dc * 1024:(dc + 1) * 1024],
                    func=AF.Square)),
                   wait=([("Tn", a_sq6[dc - 2])] if dc >= 2 else ()), sem="A")
                for th in range(2):
                    op("tensor", (lambda e, dc=dc, th=th: e.matmul(
                        pbank[0 + th][0:1, 0:512], onesb[:, 0:1],
                        resid[:, dc * 1024 + th * 512:dc * 1024 + (th + 1) * 512],
                        start=(dc == 0), stop=(dc == KC - 1))),
                       wait=([("A", csq["A"])] if (dc == 0 and th == 0) else ()), sem="Tn")
                    cm7 = op("tensor", (lambda e, dc=dc, th=th, par=par: e.matmul(
                        pbank[2 + th][0:1, 0:512], onesb[:, 0:1],
                        ngrh[par][:, th * 512:(th + 1) * 512],
                        start=(dc == 0), stop=(dc == KC - 1))),
                       wait=[("A", csq["A"])], sem="Tn")
                a_sq6[dc] = cm7["Tn"]
            t_row6 = cm7["Tn"]
            for th in range(2):
                op("vector", (lambda e, th=th: e.tensor_scalar(
                    out=rowb[0:1, th * 512:(th + 1) * 512], in0=pbank[th][0:1, 0:512],
                    scalar1=1.0 / DM, scalar2=None, op0=ALU.mult)),
                   wait=[("Tn", t_row6)], sem="V")
                op("vector", (lambda e, th=th: e.tensor_scalar(
                    out=rowb2[0:1, th * 512:(th + 1) * 512], in0=pbank[2 + th][0:1, 0:512],
                    scalar1=1.0 / DM, scalar2=None, op0=ALU.mult)), sem="V")
            op("vector", (lambda e: e.tensor_mul(out=rowb3[0:1, :], in0=rowb[0:1, :],
                                                 in1=rowb[0:1, :])), sem="V")
            cvv = op("vector", (lambda e: e.tensor_sub(out=rowb3[0:1, :], in0=rowb2[0:1, :],
                                                       in1=rowb3[0:1, :])), sem="V")
            cl1 = op("scalar", (lambda e: e.activation(out=rowb2[0:1, :], in_=rowb3[0:1, :],
                                                       func=AF.Ln, bias=epsc[0:1, 0:1])),
                     wait=[("V", cvv["V"])], sem="A")
            cl2 = op("scalar", (lambda e: e.activation(out=rowb3[0:1, :], in_=rowb2[0:1, :],
                                                       func=AF.Exp, scale=-0.5)), sem="A")
            cr1 = dma("gpsimd", d_rows[6][:].rearrange("(o f) -> o f", o=1), rowb[0:1, :],
                      wait=[("A", cl2["A"])], sem="SG0")
            cr2 = dma("gpsimd", d_rows[7][:].rearrange("(o f) -> o f", o=1), rowb3[0:1, :],
                      sem="SG0")
            cb1 = dma("gpsimd", mrep[:], d_rows[6][:].rearrange("(o f) -> o f", o=1)
                      .partition_broadcast(128), wait=[("SG0", cr2["SG0"])], sem="SG0")
            cb2 = dma("gpsimd", rrep[:], d_rows[7][:].rearrange("(o f) -> o f", o=1)
                      .partition_broadcast(128), sem="SG0")
            ln2_ready = cb2["SG0"]
            g_out = {0: 0, 1: 0}
            for dc in range(KC):
                par = dc % 2
                ct1 = op("vector", (lambda e, dc=dc: e.tensor_sub(
                    out=tmpf1[:], in0=resid[:, dc * 1024:(dc + 1) * 1024],
                    in1=mrep[:])),
                   wait=([("SG0", ln2_ready)] if dc == 0 else []) +
                        [(f"XS{par}", g_out[par])], sem="V")
                cou = op("vector", (lambda e, dc=dc, par=par: e.tensor_mul(
                    out=outt[par][:], in0=tmpf1[:], in1=rrep[:])), sem="V")
                cso = dma("gpsimd", d_out[dc], outt[par][:], wait=[("V", cou["V"])],
                          sem=f"XS{par}")
                g_out[par] = cso[f"XS{par}"]

        phase_barrier()
        sched.append(("gpsimd", None, [(s, v) for (s, v) in allbar() if v > 0], []))

        block = ent(nc.Block())
        for eng_name in ("sync", "gpsimd", "vector", "scalar", "tensor"):
            sec = getattr(block, eng_name)

            def emit(eng, _n=eng_name):
                last = {}
                for (en, builder, waits, incs) in sched:
                    if en != _n:
                        continue
                    for (s, v) in waits:
                        if v > 0 and last.get(s, -1) < v:
                            eng.wait_ge(SEM[s], v)
                            last[s] = v
                    if builder is None:
                        continue
                    instr = builder(eng)
                    for (s, k) in incs:
                        instr.then_inc(SEM[s], k)
            sec(emit)
    return nc



def _prep_host(inputs):
    f32 = np.float32
    X = np.asarray(inputs["L0_attn_hidden_states"], dtype=f32)
    Win = np.asarray(inputs["L0_attn_in_proj_w"], dtype=f32)
    conv_w = np.asarray(inputs["L0_attn_conv_w"], dtype=f32)
    conv_b = np.asarray(inputs["L0_attn_conv_b"], dtype=f32)
    dt_bias = np.asarray(inputs["L0_attn_dt_bias"], dtype=f32)
    D = np.asarray(inputs["L0_attn_D"], dtype=f32)
    norm_w = np.asarray(inputs["L0_attn_norm_w"], dtype=f32)
    Wout = np.asarray(inputs["L0_attn_out_proj_w"], dtype=f32)
    Wf = np.asarray(inputs["L0_ffn_W"], dtype=f32)
    bf_ = np.asarray(inputs["L0_ffn_b"], dtype=f32)
    Af = np.asarray(inputs["L0_ffn_A"], dtype=f32)
    Bf = np.asarray(inputs["L0_ffn_B"], dtype=f32)
    gw = np.asarray(inputs["L0_ffn_gate_w"], dtype=f32)
    gbv = np.asarray(inputs["L0_ffn_gate_b"], dtype=f32)

    sh = {}
    sh["wxbc"] = np.ascontiguousarray(
        Win[INTER:, :].reshape(NXBC, 128, KC, 128).transpose(0, 3, 2, 1))
    sh["wgate"] = np.ascontiguousarray(
        Win[:INTER, :].reshape(NFI, 128, KC, 128).transpose(0, 3, 2, 1))
    sh["wout"] = np.ascontiguousarray(
        Wout.reshape(8, 4, 128, NFI, 128).transpose(0, 3, 1, 4, 2))
    sh["wffn"] = np.ascontiguousarray(
        Wf.reshape(8, 4, 128, KC, 128).transpose(0, 3, 1, 4, 2))
    ag = np.concatenate([Af, gw], axis=0)
    sh["wag"] = np.ascontiguousarray(ag.T.reshape(KC, 128, 68))
    sh["wbl"] = np.ascontiguousarray(Bf.reshape(KC, 128, 64).transpose(0, 2, 1))
    sh["dr_identbf"] = np.eye(128, dtype=BF)
    sh["dr_identf"] = np.eye(128, dtype=f32)
    ii = np.arange(128)
    sh["dr_maskm"] = np.where(ii[None, :] >= ii[:, None], 0.0, -1e38).astype(f32)
    lt = np.zeros((4, 64), dtype=BF)
    for i, r in enumerate(RANKS):
        lt[i, :r] = 1.0
    sh["dr_ltm"] = lt
    sh["dr_onesb"] = np.ones((128, 1), dtype=BF)
    sh["dr_acol"] = (-np.arange(1, 129, dtype=f32))[:, None]
    sh["dtb"] = np.ascontiguousarray(dt_bias[:, None])
    sh["dr_drep"] = np.ascontiguousarray(np.repeat(D.reshape(NFI, 2), 64, axis=1).reshape(NFI, 128).T)
    sh["nw"] = np.ascontiguousarray(norm_w.reshape(NFI, 128).T)
    sh["bcol"] = np.ascontiguousarray(bf_.reshape(KC, 128).T)
    sh["gb"] = np.ascontiguousarray(gbv[:, None])
    sh["dr_cw"] = np.ascontiguousarray(
        conv_w[:, 0, :].reshape(80, 128, 4).transpose(1, 0, 2).reshape(128, 320))
    sh["dr_cb"] = np.ascontiguousarray(conv_b.reshape(80, 128).T)

    in_maps = []
    for c in range(NCORES):
        b, r = divmod(c, 4)
        xh = np.zeros((TH, DM), dtype=f32)
        t0 = r * T
        if r == 0:
            xh[3:] = X[b, t0:t0 + T]
        else:
            xh[:] = X[b, t0 - 3:t0 + T]
        m = dict(sh)
        m["xh"] = np.ascontiguousarray(xh.T.reshape(KC, 128, TH))
        rm = np.zeros((128, 4), dtype=f32)
        rm[:, :r] = 1.0
        m["dr_ringm"] = rm
        m["dr_ringm1"] = np.ascontiguousarray((1.0 - rm).astype(f32))
        mc = np.zeros((4, 1), dtype=f32)
        mc[:r, 0] = 1.0
        m["dr_ringmc"] = mc
        m["dr_ringmc1"] = np.ascontiguousarray((1.0 - mc).astype(f32))
        m["dr_epsc"] = np.full((128, 1), 1e-5, dtype=f32)
        m["dr_onesrow"] = np.ones((1, 128), dtype=f32)
        in_maps.append(m)
    return in_maps


def kernel(**inputs):
    in_maps = _prep_host(inputs)
    nc = build_program()
    res = run_bass_kernel_spmd(nc, in_maps, core_ids=list(range(NCORES)))
    out = np.zeros((2, 4096, 4096), dtype=np.float32)
    for c in range(NCORES):
        b, r = divmod(c, 4)
        o = np.asarray(res.results[c]["out"], dtype=np.float32).reshape(4096, T)
        out[b, r * T:(r + 1) * T, :] = o.T
    return out


# version 12
# speedup vs baseline: 1.0265x; 1.0235x over previous
import numpy as np
import ml_dtypes
from contextlib import ExitStack
import concourse.bass as bass
import concourse.mybir as mybir
from concourse.bass_utils import run_bass_kernel_spmd

F32 = mybir.dt.float32
BF16 = mybir.dt.bfloat16
AF = mybir.ActivationFunctionType
ALU = mybir.AluOpType
AX = mybir.AxisListType
BF = ml_dtypes.bfloat16

NCORES = 8
T = 1024
TH = T + 3
NCH = 8
CHL = 128
DM = 4096
KC = 32
INTER = 8192
NFI = 64
NXBC = 81
NH = 128
HD = 64
SS = 128
NG = 8
GROUP = 1024
EPS = 1e-5
LNEPS = 1e-5
RANKS = (8, 16, 32, 64)
TS = (347, 340, 340)
TS_OFF = (0, 347, 687)

PHASES = 99
DEBUG = False


def build_program():
    nc = bass.Bass("TRN2", target_bir_lowering=False)
    scr = dict(kind="ExternalOutput") if DEBUG else {}

    d_xh = nc.dram_tensor("xh", [KC, 128, TH], F32, kind="ExternalInput").ap()
    d_wxbc = nc.dram_tensor("wxbc", [NXBC, 128, KC, 128], F32, kind="ExternalInput").ap()
    d_wgate = nc.dram_tensor("wgate", [NFI, 128, KC, 128], F32, kind="ExternalInput").ap()
    d_wout = nc.dram_tensor("wout", [8, NFI, 4, 128, 128], F32, kind="ExternalInput").ap()
    d_wffn = nc.dram_tensor("wffn", [8, KC, 4, 128, 128], F32, kind="ExternalInput").ap()
    d_wag = nc.dram_tensor("wag", [KC, 128, 68], F32, kind="ExternalInput").ap()
    d_wbl = nc.dram_tensor("wbl", [KC, 64, 128], F32, kind="ExternalInput").ap()
    d_identbf = nc.dram_tensor("dr_identbf", [128, 128], BF16, kind="ExternalInput").ap()
    d_identf = nc.dram_tensor("dr_identf", [128, 128], F32, kind="ExternalInput").ap()
    d_maskm = nc.dram_tensor("dr_maskm", [128, 128], F32, kind="ExternalInput").ap()
    d_ltm = nc.dram_tensor("dr_ltm", [4, 64], BF16, kind="ExternalInput").ap()
    d_onesb = nc.dram_tensor("dr_onesb", [128, 1], BF16, kind="ExternalInput").ap()
    d_acol = nc.dram_tensor("dr_acol", [128, 1], F32, kind="ExternalInput").ap()
    d_dtb = nc.dram_tensor("dtb", [128, 1], F32, kind="ExternalInput").ap()
    d_drep = nc.dram_tensor("dr_drep", [128, NFI], F32, kind="ExternalInput").ap()
    d_nw = nc.dram_tensor("nw", [128, NFI], F32, kind="ExternalInput").ap()
    d_bcol = nc.dram_tensor("bcol", [128, KC], F32, kind="ExternalInput").ap()
    d_gb = nc.dram_tensor("gb", [4, 1], F32, kind="ExternalInput").ap()
    d_cw = nc.dram_tensor("dr_cw", [128, 320], F32, kind="ExternalInput").ap()
    d_cb = nc.dram_tensor("dr_cb", [128, 80], F32, kind="ExternalInput").ap()
    d_ringm = nc.dram_tensor("dr_ringm", [128, 4], F32, kind="ExternalInput").ap()
    d_ringm1 = nc.dram_tensor("dr_ringm1", [128, 4], F32, kind="ExternalInput").ap()
    d_ringmc = nc.dram_tensor("dr_ringmc", [4, 1], F32, kind="ExternalInput").ap()
    d_ringmc1 = nc.dram_tensor("dr_ringmc1", [4, 1], F32, kind="ExternalInput").ap()
    d_epsc = nc.dram_tensor("dr_epsc", [128, 1], F32, kind="ExternalInput").ap()
    d_onesrow = nc.dram_tensor("dr_onesrow", [1, 128], F32, kind="ExternalInput").ap()

    d_out = nc.dram_tensor("out", [KC, 128, T], F32, kind="ExternalOutput").ap()

    d_xtbf = nc.dram_tensor("xtbf_s", [KC, 128, T], BF16, **scr).ap()
    d_xbcs = nc.dram_tensor("xbc_s", [80, 128, T], BF16, **scr).ap()
    d_dts = nc.dram_tensor("dt_s", [128, T], F32, **scr).ap()
    d_csf = nc.dram_tensor("csflat_s", [NCH, 16384], F32, **scr).ap()
    d_ecs = nc.dram_tensor("expcs_s", [NCH, 16384], F32, **scr).ap()
    d_cdr = nc.dram_tensor("cdrow_s", [NCH, 128], F32, **scr).ap()
    d_yds = nc.dram_tensor("yd_s", [NCH, 128, 8192], BF16, **scr).ap()
    d_sts = nc.dram_tensor("st_s", [NCH, 128, 8192], BF16, **scr).ap()
    d_gs = nc.dram_tensor("g_s", [NFI, 128, T], BF16, **scr).ap()
    d_ngs = nc.dram_tensor("ng_s", [NFI, 128, T], BF16, **scr).ap()
    d_rows = nc.dram_tensor("rows_s", [8, 1024], F32, **scr).ap()
    d_gnr = nc.dram_tensor("gnrow_s", [4, NG, 256], F32, **scr).ap()
    d_ccsi = nc.dram_tensor("ccs_in", [128, 8193], F32).ap()
    d_ccso = nc.dram_tensor("ccs_out", [4 * 128, 8193], F32).ap()
    d_dbg_sloc = nc.dram_tensor("dbg_sloc", [128, 8192], F32, **scr).ap()
    d_dbg_state = nc.dram_tensor("dbg_state", [128, 8192], F32, **scr).ap()
    d_ys = nc.dram_tensor("y_s", [NCH, 128, 8192], BF16, **scr).ap()
    d_r1 = nc.dram_tensor("r1_s", [KC, 128, T], BF16, **scr).ap()
    d_x2s = nc.dram_tensor("x2_s", [KC, 128, T], BF16, **scr).ap()
    d_df1 = nc.dram_tensor("dbgf_scbt", [128, 32], F32, **scr).ap()
    d_df2 = nc.dram_tensor("dbgf_fbf", [128, 32], BF16, **scr).ap()
    d_df3 = nc.dram_tensor("dbgf_msb", [64, 1024], BF16, **scr).ap()
    d_df4 = nc.dram_tensor("dbgf_umb", [64, 1024], BF16, **scr).ap()
    d_df5 = nc.dram_tensor("dbgf_sc4", [4, 1024], F32, **scr).ap()

    st = ExitStack()
    with st:
        ent = st.enter_context
        identbf = ent(nc.sbuf_tensor("identbf", [128, 128], BF16))
        identf = ent(nc.sbuf_tensor("identf", [128, 128], F32))
        maskm = ent(nc.sbuf_tensor("maskm", [128, 128], F32))
        ltm = ent(nc.sbuf_tensor("ltm", [4, 64], BF16))
        onesb = ent(nc.sbuf_tensor("onesb", [128, 1], BF16))
        acol = ent(nc.sbuf_tensor("acol", [128, 1], F32))
        dtbc = ent(nc.sbuf_tensor("dtbc", [128, 1], F32))
        drep = ent(nc.sbuf_tensor("drep", [128, NFI], F32))
        nwc = ent(nc.sbuf_tensor("nwc", [128, NFI], F32))
        bcolc = ent(nc.sbuf_tensor("bcolc", [128, KC], F32))
        gbc = ent(nc.sbuf_tensor("gbc", [4, 1], F32))
        cwc = ent(nc.sbuf_tensor("cwc", [128, 320], F32))
        cbc = ent(nc.sbuf_tensor("cbc", [128, 80], F32))
        ringm = ent(nc.sbuf_tensor("ringm", [128, 4], F32))
        ringm1 = ent(nc.sbuf_tensor("ringm1", [128, 4], F32))
        ringmc = ent(nc.sbuf_tensor("ringmc", [4, 1], F32))
        ringmc1 = ent(nc.sbuf_tensor("ringmc1", [4, 1], F32))
        epsc = ent(nc.sbuf_tensor("epsc", [128, 1], F32))
        onesrow = ent(nc.sbuf_tensor("onesrow", [1, 128], F32))
        zcol = ent(nc.sbuf_tensor("zcol", [128, 128], F32))
        dtq = ent(nc.sbuf_tensor("dtq", [128, T], F32))
        datq = ent(nc.sbuf_tensor("datq", [128, T], F32))
        logd = ent(nc.sbuf_tensor("logd", [128, 1], F32))
        csb = ent(nc.sbuf_tensor("csb", [128, 256], F32))
        ecsb = ent(nc.sbuf_tensor("ecsb", [128, 256], F32))
        dcy = ent(nc.sbuf_tensor("dcy", [128, 256], F32))
        csT = ent(nc.sbuf_tensor("csT", [128, 256], F32))
        dcyT = ent(nc.sbuf_tensor("dcyT", [128, 256], F32))
        dtT = ent(nc.sbuf_tensor("dtT", [128, 256], F32))
        cdrp = ent(nc.sbuf_tensor("cdrp", [128, 256], F32))
        btb = ent(nc.sbuf_tensor("btb", [128, 1024], BF16))
        erow = ent(nc.sbuf_tensor("erow", [4, 128], F32))
        expdr = ent(nc.sbuf_tensor("expdr", [4, 128], F32))
        scbt = ent(nc.sbuf_tensor("scbt", [128, 32], F32))
        mxb = ent(nc.sbuf_tensor("mxb", [128, 8], F32))
        ebx = ent(nc.sbuf_tensor("ebx", [128, 32], F32))
        cub = ent(nc.sbuf_tensor("cub", [128, 32], F32))
        fbf = ent(nc.sbuf_tensor("fbf", [128, 32], BF16))
        fsb = ent(nc.sbuf_tensor("fsb", [4, 1024], BF16))
        sc4 = ent(nc.sbuf_tensor("sc4", [4, 1024], F32))
        msb = ent(nc.sbuf_tensor("msb", [64, 1024], BF16))
        umb = ent(nc.sbuf_tensor("umb", [64, 1024], BF16))
        agstg = ent(nc.sbuf_tensor("agstg", [128, 68], F32))
        agb = ent(nc.sbuf_tensor("agb", [128, KC * 68], BF16))
        blb = [ent(nc.sbuf_tensor(f"blb{i}", [64, 512], BF16)) for i in range(2)]
        blstg2 = [ent(nc.sbuf_tensor(f"blstg{i}", [64, 512], F32)) for i in range(2)]

        MSZ = 76288
        M = ent(nc.sbuf_tensor("mega", [128, MSZ], BF16))

        def mb(off_kb, sz_kb, dtype=BF16):
            e0 = int(off_kb * 512)
            n = int(sz_kb * 512)
            assert e0 + n <= MSZ, f"arena overflow {off_kb}+{sz_kb}"
            ap = M[:, e0:e0 + n]
            if dtype == F32:
                ap = ap.bitcast(F32)
            return ap

        xtbf = mb(0, 64.25)
        wsl = [mb(64.25 + 16 * i, 16, F32) for i in range(2)]
        wbfs = [mb(96.25 + 8 * i, 8) for i in range(2)]
        projf = [mb(112.25 + 4.25 * i, 4.25, F32) for i in range(2)]
        convt = [mb(120.75 + 4 * i, 4, F32) for i in range(2)]
        xbcb = [mb(128.75 + 2 * i, 2) for i in range(2)]
        gtile = [mb(112.25 + 2 * i, 2) for i in range(2)]
        ctl = mb(104, 2)
        rbc = [mb(0 + 8 * i, 8, F32) for i in range(2)]
        p1b = [mb(16 + 4 * i, 4) for i in range(2)]
        lmt = [mb(24 + 4 * i, 4) for i in range(2)]
        scp = [mb(32 + 4 * i, 4) for i in range(2)]
        xgw = [mb(46 + 5 * i, 5) for i in range(2)]
        xdt = [mb(56 + 4 * i, 4) for i in range(2)]
        xdtp = [mb(64 + 4 * i, 4) for i in range(2)]
        ydb = [mb(72 + 2 * i, 2) for i in range(2)]
        stb2 = [mb(76 + 1 * i, 1) for i in range(2)]
        scr2 = mb(78, 8, F32)
        sloc = mb(116.5, 32, F32)
        ybuf = mb(0, 32)
        pvb = mb(32, 16)
        stre = mb(48, 16)
        ydre = mb(64, 16)
        ebc = [mb(80 + 8 * i, 8, F32) for i in range(2)]
        cpb = [mb(96 + 4 * i, 4) for i in range(2)]
        ygg = mb(106, 4)
        gtall = mb(64, 32)
        gt2 = [mb(110 + 0.5 * i, 0.5) for i in range(2)]
        sqt = [mb(111 + 0.5 * i, 0.5) for i in range(2)]
        ngt = [mb(112 + 0.5 * i, 0.5) for i in range(2)]
        rsr = mb(113, 1, F32)
        x2b = mb(0, 64)
        resid = mb(64, 64)
        wsm = [mb(128 + 2 * i, 2, F32) for i in range(2)]
        wsmb = [mb(132 + 1 * i, 1) for i in range(2)]
        ngrh = [mb(134 + 2 * i, 2) for i in range(2)]
        xtre = [mb(138 + 2 * i, 2) for i in range(2)]
        outt = [mb(64 + 4 * i, 4, F32) for i in range(2)]

        outt = [mb(0 + 4 * i, 4, F32) for i in range(2)]
        rowg = mb(114, 1, F32)[0:1, 0:256]
        rowg2 = mb(115, 1, F32)[0:1, 0:256]
        wsl5f = mb(0, 16, F32)
        wbf5 = [mb(16 + 8 * i, 8) for i in range(2)]
        rhs8 = [mb(32 + 16 * i, 16) for i in range(2)]
        wslff = mb(134, 8, F32)
        wbff = [mb(142, 4), mb(128, 4)]
        rowb = mb(48, 4, F32)[0:1, 0:1024]
        rowb2 = mb(52, 4, F32)[0:1, 0:1024]
        rowb3 = mb(56, 4, F32)[0:1, 0:1024]
        mrep = mb(138, 4, F32)
        rrep = mb(142, 4, F32)
        tmpf1 = mb(128, 4, F32)

        pbank_t = [ent(nc.psum_tensor(f"pb{i}", [128, 512], F32)) for i in range(8)]
        pbank = [pbank_t[i][:] for i in range(8)]
        ptb = [pbank_t[6 + i][:].bitcast(BF16) for i in range(2)]

        sem_names = ["W", "G", "G2", "V", "A", "Tn", "CC", "WS0", "WS1", "XS0", "XS1",
                     "XG0", "XG1", "RB0", "RB1", "YS0", "YS1", "SS0", "SS1", "SG0", "SG1",
                     "YP0", "YP1", "SP0", "SP1"]
        SEM = {n: ent(nc.semaphore(n)) for n in sem_names}
        CNT = {n: 0 for n in sem_names}
        sched = []

        def op(eng, builder, wait=(), sem=None, k=1):
            for (s, v) in wait:
                assert v <= CNT[s], f"wait {s}>={v} but only {CNT[s]} scheduled"
            incs = []
            if sem is not None:
                CNT[sem] += k
                incs = [(sem, k)]
            sched.append((eng, builder, [(s, v) for (s, v) in wait if v > 0], incs))
            return {s: CNT[s] for s in sem_names}

        def bar(*sems):
            return [(s, CNT[s]) for s in sems]

        def allbar():
            return bar(*sem_names)

        def dma(queue, out, in_, wait=(), sem=None, slow_ok=False):
            if sem is None:
                sem = "W" if queue == "sync" else ("G" if queue == "gpsimd" else "G2")
            return op(queue, (lambda e, o=out, i=in_, s=slow_ok: e.dma_start(
                out=o, in_=i, allow_slow_non_contiguous=s)), wait, sem, 16)

        def phase_barrier():
            b = allbar()
            for e in ("sync", "gpsimd", "vector", "scalar", "tensor"):
                sched.append((e, None, [(s, v) for (s, v) in b if v > 0], []))

        for dst, src in ((identbf[:], d_identbf), (identf[:], d_identf), (maskm[:], d_maskm),
                         (ltm[:], d_ltm), (onesb[:], d_onesb), (acol[:], d_acol),
                         (dtbc[:], d_dtb), (drep[:], d_drep), (nwc[:], d_nw),
                         (bcolc[:], d_bcol), (gbc[:], d_gb), (cwc[:], d_cw), (cbc[:], d_cb),
                         (ringm[:], d_ringm), (ringm1[:], d_ringm1),
                         (ringmc[:], d_ringmc), (ringmc1[:], d_ringmc1),
                         (epsc[:], d_epsc), (onesrow[:], d_onesrow)):
            dma("gpsimd", dst, src)
        op("vector", lambda e: e.memset(zcol[:], 0.0), sem="V")
        op("vector", lambda e: e.memset(logd[:], 0.0), sem="V")

        xcast_done = []
        for kc in range(KC):
            par = f"SG{kc % 2}"
            w = [("V", xcast_done[kc - 2])] if kc >= 2 else (bar("G") if kc == 0 else ())
            c = dma("gpsimd", projf[kc % 2][:, 0:TH], d_xh[kc], wait=w, sem=par)
            c = op("vector",
                   (lambda e, kc=kc: e.tensor_copy(out=xtbf[:, kc * TH:(kc + 1) * TH],
                                                   in_=projf[kc % 2][:, 0:TH])),
                   wait=[(par, c[par])], sem="V")
            xcast_done.append(c["V"])
            dma("sync", d_xtbf[kc], xtbf[:, kc * TH + 3:(kc + 1) * TH], wait=bar("V"))

        if PHASES >= 1:
            v_cast = {}
            t_mm = {}
            a_evac = {}
            v_conv = {}
            g_store = {}

            def p1_fetch(fc):
                b = fc % 2
                ws_ = f"WS{b}"
                w = [("V", v_cast[fc - 2])] if fc >= 2 else ()
                c0 = dma("sync", wsl[b][:], d_wxbc[fc].rearrange("p kc f -> p (kc f)"),
                         wait=w, sem=ws_)
                wv = [(ws_, c0[ws_])]
                if fc >= 2:
                    wv.append(("Tn", t_mm[fc - 2]))
                c = op("vector", (lambda e, b=b: e.tensor_copy(out=wbfs[b][:], in_=wsl[b][:])),
                       wait=wv, sem="V")
                v_cast[fc] = c["V"]

            p1_fetch(0)
            p1_fetch(1)
            for fc in range(NXBC):
                b = fc % 2
                wt = [("V", v_cast[fc])]
                if fc >= 2:
                    wt.append(("A", a_evac[fc - 2]))
                for kc in range(KC):
                    for tsi in range(3):
                        def mmb(e, b=b, kc=kc, tsi=tsi):
                            return e.matmul(
                                pbank[3 * b + tsi][:, 0:TS[tsi]],
                                wbfs[b][:, kc * 128:(kc + 1) * 128],
                                xtbf[:, kc * TH + TS_OFF[tsi]:kc * TH + TS_OFF[tsi] + TS[tsi]],
                                start=(kc == 0), stop=(kc == KC - 1))
                        c = op("tensor", mmb, wait=(wt if (kc == 0 and tsi == 0) else ()),
                               sem=("Tn" if (kc == KC - 1 and tsi == 2) else None))
                t_mm[fc] = c["Tn"]
                if fc + 2 < NXBC:
                    p1_fetch(fc + 2)
                we = [("Tn", t_mm[fc])]
                if fc >= 2:
                    we.append(("V", v_conv.get(fc - 2, 0)))
                else:
                    we.append(("V", v_cast[1]))
                for tsi in range(3):
                    c = op("scalar",
                           (lambda e, b=b, tsi=tsi: e.copy(
                               out=projf[b][:, TS_OFF[tsi]:TS_OFF[tsi] + TS[tsi]],
                               in_=pbank[3 * b + tsi][:, 0:TS[tsi]])),
                           wait=(we if tsi == 0 else ()), sem="A")
                a_evac[fc] = c["A"]
                if fc < 80:
                    wv2 = [("A", a_evac[fc])]
                    for k in range(4):
                        if k == 0:
                            c = op("vector",
                                   (lambda e, b=b, fc=fc: e.tensor_scalar(
                                       out=convt[b][:], in0=projf[b][:, 0:T],
                                       scalar1=cwc[:, fc * 4:fc * 4 + 1], scalar2=None,
                                       op0=ALU.mult)),
                                   wait=wv2, sem="V")
                        else:
                            c = op("vector",
                                   (lambda e, b=b, fc=fc, k=k: e.scalar_tensor_tensor(
                                       out=convt[b][:], in0=projf[b][:, k:k + T],
                                       scalar=cwc[:, fc * 4 + k:fc * 4 + k + 1],
                                       in1=convt[b][:], op0=ALU.mult, op1=ALU.add)),
                                   sem="V")
                    v_conv[fc] = c["V"]
                    ws = [("V", v_conv[fc])]
                    if fc >= 2:
                        ws.append((f"XS{b}", g_store[fc - 2]))
                    c = op("scalar",
                           (lambda e, b=b, fc=fc: e.activation(
                               out=xbcb[b][:], in_=convt[b][:], func=AF.Silu,
                               bias=cbc[:, fc:fc + 1], scale=1.0)),
                           wait=ws, sem="A")
                    c = dma("gpsimd", d_xbcs[fc], xbcb[b][:], wait=bar("A"), sem=f"XS{b}")
                    g_store[fc] = c[f"XS{b}"]
                else:
                    op("scalar", (lambda e, b=b: e.activation(
                        out=dtq[:], in_=projf[b][:, 3:TH], func=AF.Identity,
                        bias=dtbc[:, 0:1])), wait=bar("A", "V"), sem="A")
                    op("scalar", (lambda e: e.activation(out=convt[0][:], in_=dtq[:],
                                                         func=AF.Abs)), sem="A")
                    op("scalar", (lambda e: e.activation(out=convt[1][:], in_=convt[0][:],
                                                         func=AF.Exp, scale=-1.0)), sem="A")
                    op("scalar", (lambda e: e.activation(out=convt[0][:], in_=convt[1][:],
                                                         func=AF.Ln, bias=1.0)), sem="A")
                    op("vector", (lambda e: e.tensor_relu(out=convt[1][:], in_=dtq[:])),
                       wait=bar("A"), sem="V")
                    op("vector", (lambda e: e.tensor_add(out=dtq[:], in0=convt[1][:],
                                                         in1=convt[0][:])), sem="V")
                    op("vector", (lambda e: e.tensor_scalar(out=dtq[:], in0=dtq[:],
                                                            scalar1=100.0, scalar2=0.001,
                                                            op0=ALU.min, op1=ALU.max)), sem="V")
                    op("vector", (lambda e: e.tensor_scalar(out=datq[:], in0=dtq[:],
                                                            scalar1=acol[:, 0:1], scalar2=None,
                                                            op0=ALU.mult)), sem="V")
                    if DEBUG:
                        dma("gpsimd", d_dts, dtq[:], wait=bar("V"))

        if PHASES >= 2:
            phase_barrier()
            op("vector", lambda e: e.memset(sloc[:], 0.0), sem="V")
            grp_done_V = {}
            g_csf = {}
            grp_done_T = {}
            v_states_evac = {0: 0, 1: 0}
            v_yd_evac = {}
            g_yd_store = {}
            g_st_store = {0: 0, 1: 0}
            v_p2 = {}
            a_exp = {}
            v_xdt_done = {}
            v_bt_evac = {0: 0, 1: 0}
            v_sloc_upd = 0
            for wv in range(4):
                wpre = list(bar("A", "Tn", "G"))
                v_pre = 0
                first_g = True
                for j in range(2):
                    c = 2 * wv + j
                    op("vector", (lambda e, c=c, j=j: e.tensor_tensor_scan(
                        out=csb[:, j * 128:(j + 1) * 128],
                        data0=datq[:, c * 128:(c + 1) * 128], data1=zcol[:],
                        initial=0.0, op0=ALU.add, op1=ALU.add)),
                       wait=(wpre if j == 0 else ()), sem="V")
                    op("vector", (lambda e, j=j: e.tensor_add(
                        out=logd[:], in0=logd[:], in1=csb[:, j * 128 + 127:j * 128 + 128])),
                       wait=[("V", CNT["V"])], sem="V")
                    op("scalar", (lambda e, j=j: e.activation(
                        out=ecsb[:, j * 128:(j + 1) * 128], in_=csb[:, j * 128:(j + 1) * 128],
                        func=AF.Exp)), wait=bar("V"), sem="A")
                    op("scalar", (lambda e, j=j: e.activation(
                        out=dcy[:, j * 128:(j + 1) * 128], in_=csb[:, j * 128:(j + 1) * 128],
                        func=AF.Exp, scale=-1.0,
                        bias=csb[:, j * 128 + 127:j * 128 + 128])), sem="A")
                    wg0 = list(bar("A")) + ([("V", CNT["V"])] if first_g else [])
                    first_g = False
                    dma("gpsimd", d_ecs[c].rearrange("(h l) -> h l", h=128),
                        ecsb[:, j * 128:(j + 1) * 128], wait=wg0)
                    dma("gpsimd", d_cdr[c].rearrange("(h o) -> h o", o=1),
                        ecsb[:, j * 128 + 127:j * 128 + 128])
                    ccsf = dma("gpsimd", d_csf[c].rearrange("(h l) -> h l", h=128),
                               csb[:, j * 128:(j + 1) * 128])
                    g_csf[c] = ccsf["G"]
                    cg = dma("gpsimd", cdrp[:, j * 128:(j + 1) * 128],
                             d_cdr[c].rearrange("(o h) -> o h", o=1).partition_broadcast(128),
                             wait=bar("G"))
                    g_cdrp_wave = cg["G"]
                    wtp = list(bar("A", "V")) + [("V", v_states_evac[0]),
                                                 ("V", v_states_evac[1])]
                    op("tensor", (lambda e, j=j: e.transpose(
                        pbank[0][:, 0:128], csb[:, j * 128:(j + 1) * 128], identf[:])),
                       wait=wtp, sem="Tn")
                    op("tensor", (lambda e, j=j: e.transpose(
                        pbank[0][:, 128:256], dcy[:, j * 128:(j + 1) * 128], identf[:])),
                       sem="Tn")
                    op("tensor", (lambda e, c=c: e.transpose(
                        pbank[0][:, 256:384], dtq[:, c * 128:(c + 1) * 128], identf[:])),
                       sem="Tn")
                    op("vector", (lambda e, j=j: e.tensor_copy(
                        out=csT[:, j * 128:(j + 1) * 128], in_=pbank[0][:, 0:128])),
                       wait=bar("Tn"), sem="V")
                    op("vector", (lambda e, j=j: e.tensor_copy(
                        out=dcyT[:, j * 128:(j + 1) * 128], in_=pbank[0][:, 128:256])),
                       sem="V")
                    c2 = op("vector", (lambda e, j=j: e.tensor_copy(
                        out=dtT[:, j * 128:(j + 1) * 128], in_=pbank[0][:, 256:384])),
                       sem="V")
                    v_pre = c2["V"]

                for g in range(NG):
                    gidx = wv * NG + g
                    gb_ = g % 2
                    xg_ = f"XG{gb_}"
                    wl = []
                    if gidx >= 2:
                        wl = [("V", grp_done_V[gidx - 2]), ("Tn", grp_done_T[gidx - 2])]
                    for i in range(8):
                        dma("scalar", xgw[gb_][:, i * 256:(i + 1) * 256],
                            d_xbcs[8 * g + i][:, wv * 256:(wv + 1) * 256],
                            wait=(wl if i == 0 else ()), sem=xg_)
                    dma("scalar", xgw[gb_][:, 8 * 256:9 * 256],
                        d_xbcs[64 + g][:, wv * 256:(wv + 1) * 256], sem=xg_)
                    c2 = dma("scalar", xgw[gb_][:, 9 * 256:10 * 256],
                             d_xbcs[72 + g][:, wv * 256:(wv + 1) * 256], sem=xg_)
                    g_xgw = c2[xg_]

                    for j in range(2):
                        c = 2 * wv + j
                        sidx = 2 * gidx + j
                        jj = sidx % 2
                        rb_ = f"RB{jj}"
                        wr = [("V", v_p2.get(jj, 0)), ("G", g_csf[c])]
                        c2 = dma("sync", rbc[jj][:],
                                 d_csf[c][g * 2048:(g + 1) * 2048].partition_broadcast(128),
                                 wait=wr, sem=rb_)
                        grbc = c2[rb_]
                        wp1 = [(rb_, grbc), ("V", v_pre), ("A", a_exp.get(jj, 0))]
                        h0 = 16 * g
                        op("vector", (lambda e, jj=jj, j=j, h0=h0: e.tensor_tensor(
                            out=scr2[:].rearrange("p (k l) -> p k l", k=16),
                            in0=rbc[jj][:].rearrange("p (k l) -> p k l", k=16),
                            in1=csT[:, j * 128 + h0:j * 128 + h0 + 16]
                                .unsqueeze(2).broadcast_to([128, 16, 128]),
                            op=ALU.subtract)), wait=wp1, sem="V")
                        c2 = op("vector", (lambda e, jj=jj: e.tensor_tensor(
                            out=p1b[jj][:].rearrange("p (k l) -> p k l", k=16),
                            in0=scr2[:].rearrange("p (k l) -> p k l", k=16),
                            in1=maskm[:].unsqueeze(1).broadcast_to([128, 16, 128]),
                            op=ALU.add)), sem="V")
                        vp1 = c2["V"]
                        c2 = op("scalar", (lambda e, jj=jj: e.activation(
                            out=lmt[jj][:], in_=p1b[jj][:], func=AF.Exp)),
                           wait=[("V", vp1), ("V", v_p2.get(jj, 0))], sem="A")
                        a_exp[jj] = c2["A"]
                        ws = [(xg_, g_xgw), ("V", v_p2.get(jj, 0))]
                        c2 = op("tensor", (lambda e, gb_=gb_, j=j, jj=jj: e.matmul(
                            pbank[2 + jj][:, 0:128],
                            xgw[gb_][:, 8 * 256 + j * 128:8 * 256 + (j + 1) * 128],
                            xgw[gb_][:, 9 * 256 + j * 128:9 * 256 + (j + 1) * 128],
                            start=True, stop=True)),
                           wait=ws, sem="Tn")
                        t_sc = c2["Tn"]
                        wp2 = [("Tn", t_sc), ("A", a_exp[jj])]
                        if gidx >= 1:
                            wp2.append(("Tn", grp_done_T[gidx - 1]))
                        for q in range(4):
                            c2 = op("vector", (lambda e, jj=jj, q=q: e.tensor_tensor(
                                out=scp[jj][:, q * 512:(q + 1) * 512].rearrange(
                                    "p (r f) -> p r f", r=4),
                                in0=pbank[2 + jj][:, 0:128].unsqueeze(1).broadcast_to(
                                    [128, 4, 128]),
                                in1=lmt[jj][:, q * 512:(q + 1) * 512].rearrange(
                                    "p (r f) -> p r f", r=4),
                                op=ALU.mult)),
                               wait=(wp2 if q == 0 else ()), sem="V")
                        v_p2[jj] = c2["V"]
                        t_tx = {}
                        for i in range(8):
                            tp = i // 4
                            wtx = ([("V", v_xdt_done.get(tp, 0))] if i % 4 == 0 else ())
                            c2 = op("tensor", (lambda e, gb_=gb_, i=i, j=j, tp=tp: e.transpose(
                                ptb[tp][:, (i % 4) * 128:(i % 4 + 1) * 128],
                                xgw[gb_][:, i * 256 + j * 128:i * 256 + (j + 1) * 128],
                                identbf[:])), wait=wtx, sem="Tn")
                            if i % 4 == 3:
                                t_tx[tp] = c2["Tn"]
                        for half in range(2):
                            hh0 = 16 * g + 8 * half
                            op("vector", (lambda e, jj=jj, j=j, half=half, hh0=hh0:
                               e.tensor_tensor(
                                   out=xdt[jj][:, half * 512:(half + 1) * 512]
                                       .rearrange("p (k d) -> p k d", k=8),
                                   in0=ptb[half][:, 0:512].rearrange("p (k d) -> p k d", k=8),
                                   in1=dtT[:, j * 128 + hh0:j * 128 + hh0 + 8]
                                       .unsqueeze(2).broadcast_to([128, 8, 64]),
                                   op=ALU.mult)),
                               wait=[("Tn", t_tx[half])], sem="V")
                            c2 = op("vector", (lambda e, jj=jj, j=j, half=half, hh0=hh0:
                                    e.tensor_tensor(
                                        out=xdtp[jj][:, half * 512:(half + 1) * 512]
                                            .rearrange("p (k d) -> p k d", k=8),
                                        in0=xdt[jj][:, half * 512:(half + 1) * 512]
                                            .rearrange("p (k d) -> p k d", k=8),
                                        in1=dcyT[:, j * 128 + hh0:j * 128 + hh0 + 8]
                                            .unsqueeze(2).broadcast_to([128, 8, 64]),
                                        op=ALU.mult)), sem="V")
                            v_xdt_done[half] = c2["V"]
                        t_ydiag = {}
                        for i in range(8):
                            bk = 4 + (i // 4)
                            off = (i % 4) * 128
                            wyd = []
                            if i % 4 == 0:
                                wyd = [("V", v_xdt_done[i // 4]),
                                       ("V", v_yd_evac.get(bk, 0))]
                            op("tensor", (lambda e, jj=jj, i=i, bk=bk, off=off: e.matmul(
                                pbank[bk][0:64, off:off + 128],
                                xdt[jj][:, (2 * i) * 64:(2 * i + 1) * 64],
                                scp[jj][:, (2 * i) * 128:(2 * i + 1) * 128],
                                start=True, stop=True, tile_position=(0, 0))),
                               wait=wyd, sem="Tn")
                            c2 = op("tensor", (lambda e, jj=jj, i=i, bk=bk, off=off: e.matmul(
                                pbank[bk][64:128, off:off + 128],
                                xdt[jj][:, (2 * i + 1) * 64:(2 * i + 2) * 64],
                                scp[jj][:, (2 * i + 1) * 128:(2 * i + 2) * 128],
                                start=True, stop=True, tile_position=(0, 64))),
                               sem="Tn")
                            t_ydiag[bk] = c2["Tn"]
                        for half in range(2):
                            bk = 4 + half
                            fc0 = 8 * g + 4 * half
                            wye = [("Tn", t_ydiag[bk]), (f"YS{jj}", g_yd_store.get(jj, 0))]
                            xb0 = (4 * half) * 256 + j * 128
                            op("vector", (lambda e, gb_=gb_, xb0=xb0, fc0=fc0:
                               e.tensor_tensor(
                                   out=scr2[:, 0:512].rearrange("p (q l) -> p q l", q=4),
                                   in0=xgw[gb_][:, xb0:xb0 + 1024]
                                       .rearrange("p (i l) -> p i l", i=4)[:, :, 0:128],
                                   in1=drep[:, fc0:fc0 + 4]
                                       .unsqueeze(2).broadcast_to([128, 4, 128]),
                                   op=ALU.mult)), wait=wye, sem="V")
                            c2 = op("vector", (lambda e, jj=jj, bk=bk, half=half:
                                    e.tensor_add(
                                        out=ydb[jj][:, half * 512:(half + 1) * 512],
                                        in0=scr2[:, 0:512],
                                        in1=pbank[bk][:, 0:512])), sem="V")
                            v_yd_evac[bk] = c2["V"]
                        c2 = dma("gpsimd", d_yds[c][:, (8 * g) * 128:(8 * g) * 128 + 1024],
                                 ydb[jj][:], wait=[("V", v_yd_evac[5])], sem=f"YS{jj}")
                        g_yd_store[jj] = c2[f"YS{jj}"]
                        wbt = [("V", v_bt_evac[j])]
                        c2 = op("tensor", (lambda e, gb_=gb_, j=j: e.transpose(
                            ptb[j][:, 512:640],
                            xgw[gb_][:, 8 * 256 + j * 128:8 * 256 + (j + 1) * 128],
                            identbf[:])), wait=wbt, sem="Tn")
                        c2 = op("vector", (lambda e, j=j: e.tensor_copy(
                            out=btb[:, j * 128:(j + 1) * 128], in_=ptb[j][:, 512:640])),
                           wait=[("Tn", c2["Tn"])], sem="V")
                        v_bt_evac[j] = c2["V"]
                        for half in range(2):
                            wsm_ = [("V", v_bt_evac[j]), ("V", v_states_evac[half])]
                            for q in range(8):
                                k = 8 * half + q
                                c2 = op("tensor", (lambda e, jj=jj, j=j, k=k, q=q, half=half:
                                        e.matmul(
                                            pbank[half][:, q * 64:(q + 1) * 64],
                                            btb[:, j * 128:(j + 1) * 128],
                                            xdtp[jj][:, k * 64:(k + 1) * 64],
                                            start=True, stop=True)),
                                       wait=(wsm_ if q == 0 else ()), sem="Tn")
                            t_st = c2["Tn"]
                            wse = [("Tn", t_st), (f"SS{half}", g_st_store[half]),
                                   ("V", v_sloc_upd), ("G", g_cdrp_wave)]
                            c2 = op("vector", (lambda e, half=half: e.tensor_copy(
                                out=stb2[half][:], in_=pbank[half][:])),
                               wait=wse, sem="V")
                            vse = c2["V"]
                            v_states_evac[half] = vse
                            c2 = dma("gpsimd",
                                     d_sts[c][:, (16 * g + 8 * half) * 64:(16 * g + 8 * half) * 64 + 512],
                                     stb2[half][:], wait=[("V", vse)], sem=f"SS{half}")
                            g_st_store[half] = c2[f"SS{half}"]
                            h0u = 16 * g + 8 * half
                            op("vector", (lambda e, j=j, h0u=h0u: e.tensor_tensor(
                                out=sloc[:, h0u * 64:h0u * 64 + 512]
                                    .rearrange("p (k d) -> p k d", k=8),
                                in0=sloc[:, h0u * 64:h0u * 64 + 512]
                                    .rearrange("p (k d) -> p k d", k=8),
                                in1=cdrp[:, j * 128 + h0u:j * 128 + h0u + 8]
                                    .unsqueeze(2).broadcast_to([128, 8, 64]),
                                op=ALU.mult)), sem="V")
                            c2 = op("vector", (lambda e, half=half, h0u=h0u: e.tensor_add(
                                out=sloc[:, h0u * 64:h0u * 64 + 512],
                                in0=sloc[:, h0u * 64:h0u * 64 + 512],
                                in1=stb2[half][:])), sem="V")
                            v_sloc_upd = c2["V"]
                    grp_done_V[gidx] = CNT["V"]
                    grp_done_T[gidx] = CNT["Tn"]

        if PHASES >= 3:
            phase_barrier()
            dma("gpsimd", d_ccsi[:, 0:8192], sloc[:])
            dma("gpsimd", d_ccsi[:, 8192:8193], logd[:], slow_ok=True)
            if DEBUG:
                dma("gpsimd", d_dbg_sloc, sloc[:])
            op("gpsimd", (lambda e: e.collective_compute(
                "AllGather", ALU.bypass,
                replica_groups=[[0, 1, 2, 3], [4, 5, 6, 7]],
                ins=[d_ccsi],
                outs=[d_ccso])), wait=bar("G"), sem="CC")

            for kc in range(KC):
                dma("gpsimd", xtbf[:, kc * T:(kc + 1) * T], d_xtbf[kc], sem="G2")
            xg_ld = bar("G2")
            v_gcast = {}
            t_gmm = {}
            a_gsilu = {}
            g_gstore = {}
            for fc in range(NFI):
                b = fc % 2
                ws_ = f"WS{b}"
                w = [("V", v_gcast[fc - 2])] if fc >= 2 else ()
                c0 = dma("sync", wsl[b][:], d_wgate[fc].rearrange("p kc f -> p (kc f)"),
                         wait=w, sem=ws_)
                wv_ = [(ws_, c0[ws_])] + ([("Tn", t_gmm[fc - 2])] if fc >= 2 else [])
                c2 = op("vector", (lambda e, b=b: e.tensor_copy(out=wbfs[b][:], in_=wsl[b][:])),
                        wait=wv_, sem="V")
                v_gcast[fc] = c2["V"]
                wt = [("V", c2["V"])] + xg_ld + ([("A", a_gsilu[fc - 2])] if fc >= 2 else [])
                for kc in range(KC):
                    for tsi in range(2):
                        c2 = op("tensor", (lambda e, b=b, kc=kc, tsi=tsi: e.matmul(
                            pbank[2 * b + tsi][:],
                            wbfs[b][:, kc * 128:(kc + 1) * 128],
                            xtbf[:, kc * T + tsi * 512:kc * T + (tsi + 1) * 512],
                            start=(kc == 0), stop=(kc == KC - 1))),
                           wait=(wt if (kc == 0 and tsi == 0) else ()),
                           sem=("Tn" if (kc == KC - 1 and tsi == 1) else None))
                t_gmm[fc] = c2["Tn"]
                ws = [("Tn", c2["Tn"])] + ([(f"XS{b}", g_gstore[fc - 2])] if fc >= 2 else [])
                for tsi in range(2):
                    c2 = op("scalar", (lambda e, b=b, tsi=tsi: e.activation(
                        out=gtile[b][:, tsi * 512:(tsi + 1) * 512],
                        in_=pbank[2 * b + tsi][:], func=AF.Silu)),
                       wait=(ws if tsi == 0 else ()), sem="A")
                a_gsilu[fc] = c2["A"]
                c2 = dma("gpsimd", d_gs[fc], gtile[b][:], wait=[("A", c2["A"])], sem=f"XS{b}")
                g_gstore[fc] = c2[f"XS{b}"]

            phase_barrier()
            op("vector", lambda e: e.memset(sloc[:], 0.0),
               wait=list(bar("G")) + [("CC", CNT["CC"])], sem="V")
            for jr in range(4):
                dma("gpsimd", erow[jr:jr + 1, :],
                    d_ccso[jr * 128:(jr + 1) * 128, 8192:8193].rearrange("h o -> o h"),
                    wait=(bar("CC") if jr == 0 else ()), slow_ok=True)
            op("scalar", (lambda e: e.activation(out=expdr[:], in_=erow[:], func=AF.Exp)),
               wait=bar("G"), sem="A")
            op("vector", (lambda e: e.tensor_scalar(
                out=expdr[:], in0=expdr[:], scalar1=ringmc[0:4, 0:1],
                scalar2=ringmc1[0:4, 0:1], op0=ALU.mult, op1=ALU.add)),
               wait=bar("A"), sem="V")
            for jr in range(4):
                dma("gpsimd", d_rows[jr][0:128].rearrange("(o f) -> o f", o=1),
                    expdr[jr:jr + 1, :], wait=bar("V"), sem="SG0")
            g_rows_done = CNT["SG0"]
            Sstg = mb(64.25, 32, F32)
            for jr in range(3):
                dma("gpsimd", Sstg[:], d_ccso[jr * 128:(jr + 1) * 128, 0:8192], wait=bar("V"))
                c2 = op("vector", (lambda e, jr=jr: e.tensor_scalar(
                    out=Sstg[:], in0=Sstg[:], scalar1=ringm[:, jr:jr + 1], scalar2=None,
                    op0=ALU.mult)), wait=bar("G"), sem="V")
                dma("gpsimd", cdrp[:, 0:128],
                    d_rows[jr][0:128].rearrange("(o f) -> o f", o=1).partition_broadcast(128),
                    wait=[("V", c2["V"]), ("SG0", g_rows_done)])
                wcomb = bar("G", "V")
                op("vector", (lambda e: e.tensor_tensor(
                    out=sloc[:].rearrange("p (h d) -> p h d", h=128),
                    in0=sloc[:].rearrange("p (h d) -> p h d", h=128),
                    in1=cdrp[:, 0:128].unsqueeze(2).broadcast_to([128, 128, 64]),
                    op=ALU.mult)), wait=wcomb, sem="V")
                c2 = op("vector", (lambda e: e.tensor_add(
                    out=sloc[:], in0=sloc[:], in1=Sstg[:])), sem="V")
            if DEBUG:
                dma("gpsimd", d_dbg_state, sloc[:], wait=bar("V"))

        if PHASES >= 4:
            phase_barrier()
            v_supd2 = 0
            v_ycomb = {4: 0, 5: 0}
            v_cp = {0: 0, 1: 0}
            t_yoff = {0: 0, 1: 0}
            v_yg = {}
            v_gate_done = {}
            t_mq = {}
            a_ln_grp = {}
            sg1_prev = {}
            g_ng = {0: 0, 1: 0}
            v_apply_grp = {}
            sw_bc = 0
            for wv in range(4):
                wpre2 = list(bar("V", "A", "Tn", "G", "SS0", "SS1", "YS0", "YS1",
                                 "XG0", "XG1", "RB0", "RB1", "XS0", "XS1", "SG1",
                                 "YP0", "YP1", "SP0", "SP1"))
                for j in range(2):
                    c = 2 * wv + j
                    w0 = (wpre2 if j == 0 else [("V", v_supd2)])
                    cst2 = []
                    for q in range(2):
                        sq = f"SP{q}"
                        cq = dma("gpsimd", stre[:, q * 4096:(q + 1) * 4096],
                                 d_sts[c][:, q * 4096:(q + 1) * 4096],
                                 wait=(w0 if q == 0 else ()), sem=sq)
                        cst2.append((sq, cq[sq]))
                    wyd0 = ([("V", v_ycomb[4]), ("V", v_ycomb[5])] if (wv + j) else [])
                    wyd0 += bar("SS0", "SS1")
                    ypseq = ["YS0", "YS1", "YP0", "YP1"]
                    cyd4 = []
                    for q in range(4):
                        sq = ypseq[q]
                        cq = dma("gpsimd", ydre[:, q * 2048:(q + 1) * 2048],
                                 d_yds[c][:, q * 2048:(q + 1) * 2048],
                                 wait=(wyd0 if q == 0 else ()), sem=sq)
                        cyd4.append((sq, cq[sq]))
                    ccd = dma("gpsimd", cdrp[:, 0:128],
                              d_cdr[c].rearrange("(o h) -> o h", o=1).partition_broadcast(128),
                              wait=[("V", v_supd2)], sem="G")
                    wcpv = [("Tn", max(t_yoff[0], t_yoff[1]))]
                    if j == 0:
                        wcpv += bar("SG1", "XS0", "XS1", "SS0", "SS1")
                    cpv = op("vector", (lambda e: e.tensor_copy(out=pvb[:], in_=sloc[:])),
                             wait=wcpv, sem="V")
                    for g in range(NG):
                        par = g % 2
                        cct = dma("scalar", ctl[:, g * 128:(g + 1) * 128],
                                  d_xbcs[72 + g][:, c * 128:(c + 1) * 128],
                                  wait=([("V", v_cp[par])] if (wv + j + g) else ()), sem="XG0")
                        ceb = dma("sync", ebc[par][:],
                                  d_ecs[c][g * 2048:(g + 1) * 2048].partition_broadcast(128),
                                  wait=[("V", v_cp[par]),
                                        ("V", v_gate_done.get(wv - 1, 0))], sem=f"RB{par}")
                        ccp = op("vector", (lambda e, par=par, g=g: e.tensor_tensor(
                            out=cpb[par][:].rearrange("p (r f) -> p r f", r=16),
                            in0=ebc[par][:].rearrange("p (r f) -> p r f", r=16),
                            in1=ctl[:, g * 128:(g + 1) * 128].unsqueeze(1).broadcast_to(
                                [128, 16, 128]),
                            op=ALU.mult)),
                           wait=[("XG0", cct["XG0"]), (f"RB{par}", ceb[f"RB{par}"]),
                                 ("Tn", t_yoff[par]), ("V", cpv["V"])], sem="V")
                        v_cp[par] = ccp["V"]
                        t_ydk = {4: 0, 5: 0}
                        for i in range(8):
                            bk = 4 + (i // 4)
                            off = (i % 4) * 128
                            for hh in range(2):
                                h = 16 * g + 2 * i + hh
                                wyo = []
                                if i == 0 and hh == 0:
                                    wyo = [("V", ccp["V"]), ("V", v_ycomb[4]),
                                           ("V", v_ycomb[5])]
                                cmo = op("tensor", (lambda e, par=par, i=i, hh=hh, h=h,
                                                    bk=bk, off=off: e.matmul(
                                    pbank[bk][hh * 64:(hh + 1) * 64, off:off + 128],
                                    pvb[:, h * 64:(h + 1) * 64],
                                    cpb[par][:, (2 * i + hh) * 128:(2 * i + hh + 1) * 128],
                                    start=True, stop=True,
                                    tile_position=(0, hh * 64))),
                                   wait=wyo, sem="Tn")
                            t_ydk[bk] = cmo["Tn"]
                        t_yoff[par] = cmo["Tn"]
                        for half in range(2):
                            bk = 4 + half
                            fc0 = 8 * g + 4 * half
                            cyc = op("vector", (lambda e, bk=bk, fc0=fc0, j=j: e.tensor_tensor(
                                out=ybuf[:, j * 8192 + fc0 * 128:j * 8192 + fc0 * 128 + 512],
                                in0=pbank[bk][:, 0:512],
                                in1=ydre[:, fc0 * 128:fc0 * 128 + 512],
                                op=ALU.add)),
                               wait=[("Tn", t_ydk[bk]), cyd4[g // 2]], sem="V")
                            v_ycomb[bk] = cyc["V"]
                    op("vector", (lambda e: e.tensor_tensor(
                        out=sloc[:].rearrange("p (h d) -> p h d", h=128),
                        in0=sloc[:].rearrange("p (h d) -> p h d", h=128),
                        in1=cdrp[:, 0:128].unsqueeze(2).broadcast_to([128, 128, 64]),
                        op=ALU.mult)),
                       wait=[("G", ccd["G"]), ("V", cpv["V"])], sem="V")
                    for q in range(2):
                        csu = op("vector", (lambda e, q=q: e.tensor_add(
                            out=sloc[:, q * 4096:(q + 1) * 4096],
                            in0=sloc[:, q * 4096:(q + 1) * 4096],
                            in1=stre[:, q * 4096:(q + 1) * 4096])),
                           wait=[cst2[q]], sem="V")
                    v_supd2 = csu["V"]
                if DEBUG:
                    for j in range(2):
                        dma("gpsimd", d_ys[2 * wv + j],
                            ybuf[:, j * 8192:(j + 1) * 8192], wait=bar("V"))
                c_gth = []
                for q in range(2):
                    cq = dma("gpsimd",
                             gtall[:, q * 8192:(q + 1) * 8192]
                             .rearrange("p (k t) -> p k t", k=32),
                             d_gs[32 * q:32 * (q + 1)][:, :, wv * 256:(wv + 1) * 256]
                             .rearrange("k p t -> p k t"),
                             wait=([("V", v_ycomb[4]), ("V", v_ycomb[5]),
                                    ("V", v_cp[0]), ("V", v_cp[1])] if q == 0 else ()),
                             sem=f"XS{q}")
                    c_gth.append((f"XS{q}", cq[f"XS{q}"]))
                for g in range(NG):
                    for i in range(8):
                        fc = 8 * g + i
                        par = i % 2
                        gflat = (wv * NG + g) * 8 + i
                        wyg = []
                        if g == 0 and i == 0:
                            wyg = [c_gth[0], ("V", CNT["V"])]
                        elif g == 4 and i == 0:
                            wyg = [c_gth[1]]
                        cyg = op("vector", (lambda e, fc=fc, i=i: e.tensor_tensor(
                            out=ygg[:, i * 256:(i + 1) * 256].rearrange(
                                "p (c l) -> p c l", c=2),
                            in0=gtall[:, fc * 256:(fc + 1) * 256]
                                .rearrange("p (c l) -> p c l", c=2),
                            in1=ybuf[:].rearrange("p (c f) -> p c f", c=2)[
                                :, :, fc * 128:(fc + 1) * 128],
                            op=ALU.mult)),
                           wait=wyg, sem="V")
                        v_yg[gflat] = cyg["V"]
                        csq = op("scalar", (lambda e, i=i, par=par: e.activation(
                            out=sqt[par][:], in_=ygg[:, i * 256:(i + 1) * 256],
                            func=AF.Square)),
                           wait=[("V", cyg["V"]), ("Tn", t_mq.get(par, 0))],
                           sem="A")
                        cmq = op("tensor", (lambda e, i=i, par=par: e.matmul(
                            pbank[0][0:1, 0:256], onesb[:, 0:1], sqt[par][:],
                            start=(i == 0), stop=(i == 7))),
                           wait=[("A", csq["A"])], sem="Tn")
                        t_mq[par] = cmq["Tn"]
                    cln = op("scalar", (lambda e: e.activation(
                        out=rowg, in_=pbank[0][0:1, 0:256],
                        func=AF.Ln, bias=epsc[0:1, 0:1], scale=1.0 / GROUP)),
                       wait=[("Tn", cmq["Tn"])], sem="A")
                    cex = op("scalar", (lambda e: e.activation(
                        out=rowg2, in_=rowg,
                        func=AF.Exp, scale=-0.5)), sem="A")
                    wrb = [("A", cex["A"]),
                           ("V", v_apply_grp.get(wv * NG + g - 1, 0))]
                    crb = op("tensor", (lambda e: e.matmul(
                        pbank[1][:, 0:256], onesrow[0:1, 0:128], rowg2,
                        start=True, stop=True)), wait=wrb, sem="Tn")
                    for i in range(8):
                        fc = 8 * g + i
                        cap = op("vector", (lambda e, fc=fc, i=i: e.scalar_tensor_tensor(
                            out=ybuf[:].rearrange("p (cc f) -> p cc f", cc=2)
                                [:, :, fc * 128:(fc + 1) * 128],
                            in0=ygg[:, i * 256:(i + 1) * 256]
                                .rearrange("p (cc l) -> p cc l", cc=2),
                            scalar=nwc[:, fc:fc + 1],
                            in1=pbank[1][:, 0:256].rearrange("p (cc l) -> p cc l", cc=2),
                            op0=ALU.mult, op1=ALU.mult)),
                           wait=([("Tn", crb["Tn"])] if i == 0 else ()), sem="V")
                    for cc in range(2):
                        cns = dma("gpsimd",
                                  d_ngs[8 * g:8 * g + 8][:, :, wv * 256 + cc * 128:
                                                         wv * 256 + cc * 128 + 128]
                                  .rearrange("k p l -> p k l"),
                                  ybuf[:, cc * 8192 + (8 * g) * 128:
                                       cc * 8192 + (8 * g) * 128 + 1024]
                                  .rearrange("p (k l) -> p k l", k=8),
                                  wait=([("V", cap["V"])] if cc == 0 else ()),
                                  sem=f"SS{cc}")
                        g_ng[cc] = cns[f"SS{cc}"]
                    v_apply_grp[wv * NG + g] = CNT["V"]
                v_gate_done[wv] = CNT["V"]

        if PHASES >= 5:
            phase_barrier()
            v_wcast5 = {}
            t_mm5 = {}
            v_evac5 = 0
            v_ev5b = {}
            for db in range(8):
                for fg in range(8):
                    idx = db * 8 + fg
                    wpar = fg % 2
                    w = [("V", v_wcast5[idx - 1])] if idx >= 1 else ()
                    cw5 = dma("sync", wsl5f[:].rearrange("p (k d f) -> p k d f", k=8, d=4),
                              d_wout[db, fg * 8:(fg + 1) * 8].rearrange("k d p f -> p k d f"),
                              wait=w, sem="WS0")
                    cc5 = op("vector", (lambda e, wpar=wpar: e.tensor_copy(
                        out=wbf5[wpar][:], in_=wsl5f[:])),
                       wait=[("WS0", cw5["WS0"])] +
                            ([("Tn", t_mm5[idx - 2])] if idx >= 2 else []), sem="V")
                    v_wcast5[idx] = cc5["V"]
                    crh = dma("gpsimd", rhs8[wpar][:].rearrange("p (k t) -> p k t", k=8),
                              d_ngs[fg * 8:(fg + 1) * 8].rearrange("k p t -> p k t"),
                              wait=([("Tn", t_mm5[idx - 2])] if idx >= 2 else ()),
                              sem=f"XG{wpar}")
                    for k8 in range(8):
                        for dcin in range(4):
                            for th in range(2):
                                wmm = []
                                if k8 == 0 and dcin == 0 and th == 0:
                                    wmm = [("V", cc5["V"]), (f"XG{wpar}", crh[f"XG{wpar}"])]
                                if k8 == 0 and fg == 0:
                                    wmm.append(("V", v_ev5b.get((dcin, th), 0)))
                                cm5 = op("tensor", (lambda e, wpar=wpar, dcin=dcin, th=th,
                                                    k8=k8, fg=fg: e.matmul(
                                        pbank[dcin * 2 + th][:],
                                        wbf5[wpar][:, (k8 * 4 + dcin) * 128:(k8 * 4 + dcin + 1) * 128],
                                        rhs8[wpar][:, k8 * 1024 + th * 512:k8 * 1024 + (th + 1) * 512],
                                        start=(fg == 0 and k8 == 0),
                                        stop=(fg == 7 and k8 == 7))),
                                   wait=wmm,
                                   sem=("Tn" if (dcin == 3 and th == 1) else None))
                    t_mm5[idx] = cm5["Tn"]
                for dcin in range(4):
                    dc = 4 * db + dcin
                    xpar = dcin % 2
                    cxt = dma("gpsimd", xtre[xpar][:], d_xtbf[dc],
                              wait=([("V", v_evac5)] if (db + dcin) else list(bar("V"))),
                              sem=f"RB{xpar}")
                    for th in range(2):
                        cev = op("vector", (lambda e, dcin=dcin, th=th, dc=dc, xpar=xpar:
                                e.tensor_tensor(
                                    out=resid[:, dc * 1024 + th * 512:dc * 1024 + (th + 1) * 512],
                                    in0=pbank[dcin * 2 + th][:],
                                    in1=xtre[xpar][:, th * 512:(th + 1) * 512],
                                    op=ALU.add)),
                           wait=[("Tn", t_mm5[db * 8 + 7]),
                                 (f"RB{xpar}", cxt[f"RB{xpar}"])], sem="V")
                        v_ev5b[(dcin, th)] = cev["V"]
                    v_evac5 = cev["V"]
            if DEBUG:
                dma("gpsimd", d_r1.rearrange("k p t -> p k t"),
                    resid[:].rearrange("p (k t) -> p k t", k=KC), wait=bar("V"))

            phase_barrier()
            a_sq5 = {}
            t_row5 = 0
            for dc in range(KC):
                par = dc % 2
                csq = op("scalar", (lambda e, dc=dc, par=par: e.activation(
                    out=ngrh[par][:], in_=resid[:, dc * 1024:(dc + 1) * 1024],
                    func=AF.Square)),
                   wait=([("Tn", a_sq5[dc - 2])] if dc >= 2 else ()), sem="A")
                for th in range(2):
                    op("tensor", (lambda e, dc=dc, th=th: e.matmul(
                        pbank[0 + th][0:1, 0:512], onesb[:, 0:1],
                        resid[:, dc * 1024 + th * 512:dc * 1024 + (th + 1) * 512],
                        start=(dc == 0), stop=(dc == KC - 1))),
                       wait=([("A", csq["A"])] if (dc == 0 and th == 0) else ()), sem="Tn")
                    cm6 = op("tensor", (lambda e, dc=dc, th=th, par=par: e.matmul(
                        pbank[2 + th][0:1, 0:512], onesb[:, 0:1],
                        ngrh[par][:, th * 512:(th + 1) * 512],
                        start=(dc == 0), stop=(dc == KC - 1))),
                       wait=[("A", csq["A"])], sem="Tn")
                a_sq5[dc] = cm6["Tn"]
            t_row5 = cm6["Tn"]
            for th in range(2):
                op("vector", (lambda e, th=th: e.tensor_scalar(
                    out=rowb[0:1, th * 512:(th + 1) * 512], in0=pbank[th][0:1, 0:512],
                    scalar1=1.0 / DM, scalar2=None, op0=ALU.mult)),
                   wait=[("Tn", t_row5)], sem="V")
                op("vector", (lambda e, th=th: e.tensor_scalar(
                    out=rowb2[0:1, th * 512:(th + 1) * 512], in0=pbank[2 + th][0:1, 0:512],
                    scalar1=1.0 / DM, scalar2=None, op0=ALU.mult)), sem="V")
            op("vector", (lambda e: e.tensor_mul(out=rowb3[0:1, :], in0=rowb[0:1, :],
                                                 in1=rowb[0:1, :])), sem="V")
            cvv = op("vector", (lambda e: e.tensor_sub(out=rowb3[0:1, :], in0=rowb2[0:1, :],
                                                       in1=rowb3[0:1, :])), sem="V")
            cl1 = op("scalar", (lambda e: e.activation(out=rowb2[0:1, :], in_=rowb3[0:1, :],
                                                       func=AF.Ln, bias=epsc[0:1, 0:1])),
                     wait=[("V", cvv["V"])], sem="A")
            cl2 = op("scalar", (lambda e: e.activation(out=rowb3[0:1, :], in_=rowb2[0:1, :],
                                                       func=AF.Exp, scale=-0.5)), sem="A")
            cr1 = dma("gpsimd", d_rows[4][:].rearrange("(o f) -> o f", o=1), rowb[0:1, :],
                      wait=[("A", cl2["A"])], sem="SG0")
            cr2 = dma("gpsimd", d_rows[5][:].rearrange("(o f) -> o f", o=1), rowb3[0:1, :],
                      sem="SG0")
            cb1 = dma("gpsimd", mrep[:], d_rows[4][:].rearrange("(o f) -> o f", o=1)
                      .partition_broadcast(128), wait=[("SG0", cr2["SG0"])], sem="SG0")
            cb2 = dma("gpsimd", rrep[:], d_rows[5][:].rearrange("(o f) -> o f", o=1)
                      .partition_broadcast(128), sem="SG0")
            ln1_ready = cb2["SG0"]
            for dc in range(KC):
                par = dc % 2
                ct1 = op("vector", (lambda e, dc=dc: e.tensor_sub(
                    out=tmpf1[:], in0=resid[:, dc * 1024:(dc + 1) * 1024],
                    in1=mrep[:])),
                   wait=([("SG0", ln1_ready)] if dc == 0 else ()), sem="V")
                op("vector", (lambda e, dc=dc: e.tensor_mul(
                    out=x2b[:, dc * 1024:(dc + 1) * 1024], in0=tmpf1[:], in1=rrep[:])),
                   sem="V")
            if DEBUG:
                dma("gpsimd", d_x2s.rearrange("k p t -> p k t"),
                    x2b[:, 0:KC * 1024].rearrange("p (k t) -> p k t", k=KC), wait=bar("V"))

        if PHASES >= 6:
            phase_barrier()
            v_agc = {}
            for kc in range(KC):
                par = kc % 2
                w = [("V", v_agc[kc - 1])] if kc >= 1 else ()
                cag = dma("sync", agstg[:, 0:68], d_wag[kc], wait=w, sem=f"WS{par}")
                cac = op("vector", (lambda e, kc=kc: e.tensor_copy(
                    out=agb[:, kc * 68:(kc + 1) * 68], in_=agstg[:, 0:68])),
                   wait=[(f"WS{par}", cag[f"WS{par}"])], sem="V")
                v_agc[kc] = cac["V"]
            for kc in range(KC):
                for th in range(2):
                    cmA = op("tensor", (lambda e, kc=kc, th=th: e.matmul(
                        pbank[4 + th][0:68, 0:512],
                        agb[:, kc * 68:(kc + 1) * 68],
                        x2b[:, kc * 1024 + th * 512:kc * 1024 + (th + 1) * 512],
                        start=(kc == 0), stop=(kc == KC - 1))),
                       wait=(bar("V") if (kc == 0 and th == 0) else ()), sem="Tn")
            t_ag = cmA["Tn"]
            for th in range(2):
                op("vector", (lambda e, th=th: e.tensor_copy(
                    out=sc4[0:4, th * 512:(th + 1) * 512],
                    in_=pbank[4 + th][64:68, 0:512])),
                   wait=([("Tn", t_ag)] if th == 0 else ()), sem="V")
            cgb = op("vector", (lambda e: e.tensor_scalar(
                out=sc4[0:4, :], in0=sc4[0:4, :], scalar1=gbc[0:4, 0:1], scalar2=None,
                op0=ALU.add)), wait=[("V", CNT["V"])], sem="V")
            t_tr6 = {}
            for q in range(8):
                ctr = op("tensor", (lambda e, q=q: e.transpose(
                    pbank[6][:, q * 4:(q + 1) * 4],
                    sc4[0:4, q * 128:(q + 1) * 128], identf[0:4, 0:4])),
                   wait=([("V", cgb["V"])] if q == 0 else ()), sem="Tn")
                t_tr6[q] = ctr["Tn"]
            cst6 = op("vector", (lambda e: e.tensor_copy(out=scbt[:, 0:32],
                                                         in_=pbank[6][:, 0:32])),
                      wait=[("Tn", t_tr6[7])], sem="V")
            cpr = cst6
            for q in range(8):
                cpr = op("vector", (lambda e, q=q: e.reduce_max(
                    out=mxb[:, q:q + 1], in_=scbt[:, q * 4:(q + 1) * 4],
                    axis=AX.X)), wait=[("V", cpr["V"])], sem="V")
            for q in range(8):
                cpr = op("vector", (lambda e, q=q: e.tensor_scalar(
                    out=ebx[:, q * 4:(q + 1) * 4], in0=scbt[:, q * 4:(q + 1) * 4],
                    scalar1=mxb[:, q:q + 1], scalar2=None, op0=ALU.is_ge)),
                    wait=[("V", cpr["V"])], sem="V")
            for q in range(8):
                cpr = op("vector", (lambda e, q=q: e.tensor_tensor_scan(
                    out=cub[:, q * 4:(q + 1) * 4], data0=ebx[:, q * 4:(q + 1) * 4],
                    data1=zcol[:, 0:4], initial=0.0, op0=ALU.add, op1=ALU.add)),
                    wait=[("V", cpr["V"])], sem="V")
            cpr = op("vector", (lambda e: e.tensor_scalar(
                out=cub[:, 0:32], in0=cub[:, 0:32], scalar1=1.0, scalar2=None,
                op0=ALU.is_equal)), wait=[("V", cpr["V"])], sem="V")
            cfb = op("vector", (lambda e: e.tensor_mul(out=fbf[:, 0:32], in0=ebx[:, 0:32],
                                                       in1=cub[:, 0:32])),
                     wait=[("V", cpr["V"])], sem="V")
            for q in range(8):
                ctr = op("tensor", (lambda e, q=q: e.transpose(
                    ptb[1][0:4, q * 128:(q + 1) * 128], fbf[:, q * 4:(q + 1) * 4],
                    identbf[:])),
                   wait=([("V", cfb["V"])] if q == 0 else ()), sem="Tn")
            cfs = op("vector", (lambda e: e.tensor_copy(out=fsb[0:4, :],
                                                        in_=ptb[1][0:4, 0:1024])),
                     wait=[("Tn", ctr["Tn"])], sem="V")
            for th in range(2):
                cmk = op("tensor", (lambda e, th=th: e.matmul(
                    pbank[2 + th][0:64, 0:512], ltm[0:4, 0:64],
                    fsb[0:4, th * 512:(th + 1) * 512], start=True, stop=True)),
                   wait=([("V", cfs["V"])] if th == 0 else ()), sem="Tn")
            for th in range(2):
                op("vector", (lambda e, th=th: e.tensor_copy(
                    out=msb[0:64, th * 512:(th + 1) * 512],
                    in_=pbank[2 + th][0:64, 0:512])),
                   wait=[("Tn", cmk["Tn"])], sem="V")
            for th in range(2):
                cum6 = op("vector", (lambda e, th=th: e.tensor_tensor(
                    out=umb[0:64, th * 512:(th + 1) * 512],
                    in0=pbank[4 + th][0:64, 0:512],
                    in1=msb[0:64, th * 512:(th + 1) * 512], op=ALU.mult)), sem="V")
            v_um = cum6["V"]
            if DEBUG:
                dma("gpsimd", d_df1, scbt[:, 0:32], wait=bar("V"))
                dma("gpsimd", d_df2, fbf[:, 0:32], wait=bar("V"))
                dma("gpsimd", d_df3, msb[0:64, :], wait=bar("V"))
                dma("gpsimd", d_df4, umb[0:64, :], wait=bar("V"))
                dma("gpsimd", d_df5, sc4[0:4, :], wait=bar("V"))
            v_wc6 = {}
            t_mm6 = {}
            t_lora6 = {}
            v_ev6 = 0
            v_ev6b = {}
            v_bl6 = {}
            for db in range(8):
                bpar = db % 2
                cbl = dma("sync", blstg2[bpar][:].rearrange("j (d f) -> j d f", d=4),
                          d_wbl[4 * db:4 * db + 4].rearrange("d j f -> j d f"),
                          wait=([("V", v_bl6[db - 2])] if db >= 2 else ()), sem=f"RB{bpar}")
                cblc = op("vector", (lambda e, bpar=bpar: e.tensor_copy(
                    out=blb[bpar][:], in_=blstg2[bpar][:])),
                    wait=[(f"RB{bpar}", cbl[f"RB{bpar}"])] +
                         ([("Tn", t_lora6.get(db - 2, 0))] if db >= 2 else []), sem="V")
                v_bl6[db] = cblc["V"]
                for kg in range(8):
                    idx = db * 8 + kg
                    wpar = kg % 2
                    w = [("V", v_wc6[idx - 1])] if idx >= 1 else ()
                    cw6 = dma("sync", wslff[:].rearrange("p (k d f) -> p k d f", k=4, d=4),
                              d_wffn[db, kg * 4:(kg + 1) * 4].rearrange("k d p f -> p k d f"),
                              wait=w, sem="WS0")
                    cc6 = op("vector", (lambda e, wpar=wpar: e.tensor_copy(
                        out=wbff[wpar][:], in_=wslff[:])),
                       wait=[("WS0", cw6["WS0"])] +
                            ([("Tn", t_mm6.get(idx - 2, 0))] if idx >= 2 else []), sem="V")
                    v_wc6[idx] = cc6["V"]
                    for k4 in range(4):
                        kc = kg * 4 + k4
                        for dcin in range(4):
                            for th in range(2):
                                wmm = []
                                if k4 == 0 and dcin == 0 and th == 0:
                                    wmm = [("V", cc6["V"])]
                                    if kg == 0:
                                        wmm.append(("V", v_um))
                                if k4 == 0 and kg == 0:
                                    wmm.append(("V", v_ev6b.get((dcin, th), 0)))
                                cm6b = op("tensor", (lambda e, wpar=wpar, dcin=dcin, th=th,
                                                     kc=kc, k4=k4: e.matmul(
                                        pbank[dcin * 2 + th][:],
                                        wbff[wpar][:, (k4 * 4 + dcin) * 128:(k4 * 4 + dcin + 1) * 128],
                                        x2b[:, kc * 1024 + th * 512:kc * 1024 + (th + 1) * 512],
                                        start=(kc == 0), stop=False)),
                                   wait=wmm, sem=("Tn" if (dcin == 3 and th == 1) else None))
                    t_mm6[idx] = cm6b["Tn"]
                for dcin in range(4):
                    for th in range(2):
                        cm6c = op("tensor", (lambda e, bpar=bpar, dcin=dcin, th=th: e.matmul(
                            pbank[dcin * 2 + th][:],
                            blb[bpar][0:64, dcin * 128:(dcin + 1) * 128],
                            umb[0:64, th * 512:(th + 1) * 512],
                            start=False, stop=True)),
                           wait=([("V", v_bl6[db])] if (dcin == 0 and th == 0) else ()),
                           sem=("Tn" if (dcin == 3 and th == 1) else None))
                t_lora = cm6c["Tn"]
                t_lora6[db] = t_lora
                for dcin in range(4):
                    dc = 4 * db + dcin
                    for th in range(2):
                        cev = op("vector", (lambda e, dcin=dcin, th=th, dc=dc:
                                e.scalar_tensor_tensor(
                                    out=resid[:, dc * 1024 + th * 512:dc * 1024 + (th + 1) * 512],
                                    in0=pbank[dcin * 2 + th][:],
                                    scalar=bcolc[:, dc:dc + 1],
                                    in1=x2b[:, dc * 1024 + th * 512:dc * 1024 + (th + 1) * 512],
                                    op0=ALU.add, op1=ALU.add)),
                           wait=([("Tn", t_lora)] if (dcin == 0 and th == 0) else ()),
                           sem="V")
                        v_ev6b[(dcin, th)] = cev["V"]
                    v_ev6 = cev["V"]

            phase_barrier()
            a_sq6 = {}
            for dc in range(KC):
                par = dc % 2
                csq = op("scalar", (lambda e, dc=dc, par=par: e.activation(
                    out=ngrh[par][:], in_=resid[:, dc * 1024:(dc + 1) * 1024],
                    func=AF.Square)),
                   wait=([("Tn", a_sq6[dc - 2])] if dc >= 2 else ()), sem="A")
                for th in range(2):
                    op("tensor", (lambda e, dc=dc, th=th: e.matmul(
                        pbank[0 + th][0:1, 0:512], onesb[:, 0:1],
                        resid[:, dc * 1024 + th * 512:dc * 1024 + (th + 1) * 512],
                        start=(dc == 0), stop=(dc == KC - 1))),
                       wait=([("A", csq["A"])] if (dc == 0 and th == 0) else ()), sem="Tn")
                    cm7 = op("tensor", (lambda e, dc=dc, th=th, par=par: e.matmul(
                        pbank[2 + th][0:1, 0:512], onesb[:, 0:1],
                        ngrh[par][:, th * 512:(th + 1) * 512],
                        start=(dc == 0), stop=(dc == KC - 1))),
                       wait=[("A", csq["A"])], sem="Tn")
                a_sq6[dc] = cm7["Tn"]
            t_row6 = cm7["Tn"]
            for th in range(2):
                op("vector", (lambda e, th=th: e.tensor_scalar(
                    out=rowb[0:1, th * 512:(th + 1) * 512], in0=pbank[th][0:1, 0:512],
                    scalar1=1.0 / DM, scalar2=None, op0=ALU.mult)),
                   wait=[("Tn", t_row6)], sem="V")
                op("vector", (lambda e, th=th: e.tensor_scalar(
                    out=rowb2[0:1, th * 512:(th + 1) * 512], in0=pbank[2 + th][0:1, 0:512],
                    scalar1=1.0 / DM, scalar2=None, op0=ALU.mult)), sem="V")
            op("vector", (lambda e: e.tensor_mul(out=rowb3[0:1, :], in0=rowb[0:1, :],
                                                 in1=rowb[0:1, :])), sem="V")
            cvv = op("vector", (lambda e: e.tensor_sub(out=rowb3[0:1, :], in0=rowb2[0:1, :],
                                                       in1=rowb3[0:1, :])), sem="V")
            cl1 = op("scalar", (lambda e: e.activation(out=rowb2[0:1, :], in_=rowb3[0:1, :],
                                                       func=AF.Ln, bias=epsc[0:1, 0:1])),
                     wait=[("V", cvv["V"])], sem="A")
            cl2 = op("scalar", (lambda e: e.activation(out=rowb3[0:1, :], in_=rowb2[0:1, :],
                                                       func=AF.Exp, scale=-0.5)), sem="A")
            cr1 = dma("gpsimd", d_rows[6][:].rearrange("(o f) -> o f", o=1), rowb[0:1, :],
                      wait=[("A", cl2["A"])], sem="SG0")
            cr2 = dma("gpsimd", d_rows[7][:].rearrange("(o f) -> o f", o=1), rowb3[0:1, :],
                      sem="SG0")
            cb1 = dma("gpsimd", mrep[:], d_rows[6][:].rearrange("(o f) -> o f", o=1)
                      .partition_broadcast(128), wait=[("SG0", cr2["SG0"])], sem="SG0")
            cb2 = dma("gpsimd", rrep[:], d_rows[7][:].rearrange("(o f) -> o f", o=1)
                      .partition_broadcast(128), sem="SG0")
            ln2_ready = cb2["SG0"]
            g_out = {0: 0, 1: 0}
            for dc in range(KC):
                par = dc % 2
                ct1 = op("vector", (lambda e, dc=dc: e.tensor_sub(
                    out=tmpf1[:], in0=resid[:, dc * 1024:(dc + 1) * 1024],
                    in1=mrep[:])),
                   wait=([("SG0", ln2_ready)] if dc == 0 else []) +
                        [(f"XS{par}", g_out[par])], sem="V")
                cou = op("vector", (lambda e, dc=dc, par=par: e.tensor_mul(
                    out=outt[par][:], in0=tmpf1[:], in1=rrep[:])), sem="V")
                cso = dma("gpsimd", d_out[dc], outt[par][:], wait=[("V", cou["V"])],
                          sem=f"XS{par}")
                g_out[par] = cso[f"XS{par}"]

        phase_barrier()
        sched.append(("gpsimd", None, [(s, v) for (s, v) in allbar() if v > 0], []))

        block = ent(nc.Block())
        for eng_name in ("sync", "gpsimd", "vector", "scalar", "tensor"):
            sec = getattr(block, eng_name)

            def emit(eng, _n=eng_name):
                last = {}
                for (en, builder, waits, incs) in sched:
                    if en != _n:
                        continue
                    for (s, v) in waits:
                        if v > 0 and last.get(s, -1) < v:
                            eng.wait_ge(SEM[s], v)
                            last[s] = v
                    if builder is None:
                        continue
                    instr = builder(eng)
                    for (s, k) in incs:
                        instr.then_inc(SEM[s], k)
            sec(emit)
    return nc



def _prep_host(inputs):
    f32 = np.float32
    X = np.asarray(inputs["L0_attn_hidden_states"], dtype=f32)
    Win = np.asarray(inputs["L0_attn_in_proj_w"], dtype=f32)
    conv_w = np.asarray(inputs["L0_attn_conv_w"], dtype=f32)
    conv_b = np.asarray(inputs["L0_attn_conv_b"], dtype=f32)
    dt_bias = np.asarray(inputs["L0_attn_dt_bias"], dtype=f32)
    D = np.asarray(inputs["L0_attn_D"], dtype=f32)
    norm_w = np.asarray(inputs["L0_attn_norm_w"], dtype=f32)
    Wout = np.asarray(inputs["L0_attn_out_proj_w"], dtype=f32)
    Wf = np.asarray(inputs["L0_ffn_W"], dtype=f32)
    bf_ = np.asarray(inputs["L0_ffn_b"], dtype=f32)
    Af = np.asarray(inputs["L0_ffn_A"], dtype=f32)
    Bf = np.asarray(inputs["L0_ffn_B"], dtype=f32)
    gw = np.asarray(inputs["L0_ffn_gate_w"], dtype=f32)
    gbv = np.asarray(inputs["L0_ffn_gate_b"], dtype=f32)

    sh = {}
    sh["wxbc"] = np.ascontiguousarray(
        Win[INTER:, :].reshape(NXBC, 128, KC, 128).transpose(0, 3, 2, 1))
    sh["wgate"] = np.ascontiguousarray(
        Win[:INTER, :].reshape(NFI, 128, KC, 128).transpose(0, 3, 2, 1))
    sh["wout"] = np.ascontiguousarray(
        Wout.reshape(8, 4, 128, NFI, 128).transpose(0, 3, 1, 4, 2))
    sh["wffn"] = np.ascontiguousarray(
        Wf.reshape(8, 4, 128, KC, 128).transpose(0, 3, 1, 4, 2))
    ag = np.concatenate([Af, gw], axis=0)
    sh["wag"] = np.ascontiguousarray(ag.T.reshape(KC, 128, 68))
    sh["wbl"] = np.ascontiguousarray(Bf.reshape(KC, 128, 64).transpose(0, 2, 1))
    sh["dr_identbf"] = np.eye(128, dtype=BF)
    sh["dr_identf"] = np.eye(128, dtype=f32)
    ii = np.arange(128)
    sh["dr_maskm"] = np.where(ii[None, :] >= ii[:, None], 0.0, -1e38).astype(f32)
    lt = np.zeros((4, 64), dtype=BF)
    for i, r in enumerate(RANKS):
        lt[i, :r] = 1.0
    sh["dr_ltm"] = lt
    sh["dr_onesb"] = np.ones((128, 1), dtype=BF)
    sh["dr_acol"] = (-np.arange(1, 129, dtype=f32))[:, None]
    sh["dtb"] = np.ascontiguousarray(dt_bias[:, None])
    sh["dr_drep"] = np.ascontiguousarray(np.repeat(D.reshape(NFI, 2), 64, axis=1).reshape(NFI, 128).T)
    sh["nw"] = np.ascontiguousarray(norm_w.reshape(NFI, 128).T)
    sh["bcol"] = np.ascontiguousarray(bf_.reshape(KC, 128).T)
    sh["gb"] = np.ascontiguousarray(gbv[:, None])
    sh["dr_cw"] = np.ascontiguousarray(
        conv_w[:, 0, :].reshape(80, 128, 4).transpose(1, 0, 2).reshape(128, 320))
    sh["dr_cb"] = np.ascontiguousarray(conv_b.reshape(80, 128).T)

    in_maps = []
    for c in range(NCORES):
        b, r = divmod(c, 4)
        xh = np.zeros((TH, DM), dtype=f32)
        t0 = r * T
        if r == 0:
            xh[3:] = X[b, t0:t0 + T]
        else:
            xh[:] = X[b, t0 - 3:t0 + T]
        m = dict(sh)
        m["xh"] = np.ascontiguousarray(xh.T.reshape(KC, 128, TH))
        rm = np.zeros((128, 4), dtype=f32)
        rm[:, :r] = 1.0
        m["dr_ringm"] = rm
        m["dr_ringm1"] = np.ascontiguousarray((1.0 - rm).astype(f32))
        mc = np.zeros((4, 1), dtype=f32)
        mc[:r, 0] = 1.0
        m["dr_ringmc"] = mc
        m["dr_ringmc1"] = np.ascontiguousarray((1.0 - mc).astype(f32))
        m["dr_epsc"] = np.full((128, 1), 1e-5, dtype=f32)
        m["dr_onesrow"] = np.ones((1, 128), dtype=f32)
        in_maps.append(m)
    return in_maps


def kernel(**inputs):
    in_maps = _prep_host(inputs)
    nc = build_program()
    res = run_bass_kernel_spmd(nc, in_maps, core_ids=list(range(NCORES)))
    out = np.zeros((2, 4096, 4096), dtype=np.float32)
    for c in range(NCORES):
        b, r = divmod(c, 4)
        o = np.asarray(res.results[c]["out"], dtype=np.float32).reshape(4096, T)
        out[b, r * T:(r + 1) * T, :] = o.T
    return out
